# version 06
# baseline (speedup 1.0000x reference)
import numpy as np
import ml_dtypes
from contextlib import ExitStack

import concourse.bass as bass
import concourse.mybir as mybir
from concourse.bass import IndirectOffsetOnAxis
from concourse.bass_utils import run_bass_kernel_spmd

F32 = mybir.dt.float32
BF16 = mybir.dt.bfloat16
F16 = mybir.dt.float16
U32 = mybir.dt.uint32
AF = mybir.ActivationFunctionType
OP = mybir.AluOpType
AX = mybir.AxisListType
BF = ml_dtypes.bfloat16

NCORES = 8
B, T, D, H, E = 2, 8192, 2048, 16, 128
HPC = 4
DC = D // 128
QT = 512
NQT = T // QT
INV_SCALE = float(1.0 / np.sqrt(np.float32(128.0)))
LN_ATTN_SCALE = float(np.log(np.sqrt(np.float64(128.0))))

SELK = 1024
S_SEL = 2 * SELK
I_FF = 8192
EPS_LN = 1e-5
EPS_RMS = 1e-6


class Op:
    __slots__ = ("eng", "fn", "deps", "dma", "sem", "milestone_val", "n_global", "dma_class")

    def __init__(self, eng, fn, deps, dma):
        self.eng = eng
        self.fn = fn
        self.deps = deps
        self.dma = dma
        self.sem = False
        self.milestone_val = None
        self.n_global = -1


class Sched:
    ENGINES = ("sync", "scalar", "vector", "tensor", "gpsimd")
    NDMA_CLASSES = 6

    def __init__(self, paranoid=False, serialize_dma=False):
        self.ops = {e: [] for e in self.ENGINES}
        self.n = 0
        self.paranoid = paranoid
        self.serialize_dma = serialize_dma
        self._dma_count = {}
        self._class_prev = {}

    def op(self, eng, fn, deps=(), dma=False):
        deps = [d for d in deps if d is not None]
        h = Op(eng, fn, deps, dma)
        if dma:
            k = self._dma_count.get(eng, 0)
            self._dma_count[eng] = k + 1
            cls = 0 if self.serialize_dma else k % self.NDMA_CLASSES
            h.dma_class = (eng, cls)
            prev = self._class_prev.get(h.dma_class)
            if prev is not None:
                deps.append(prev)
            self._class_prev[h.dma_class] = h
        else:
            h.dma_class = None
        h.deps = deps
        h.n_global = self.n
        self.n += 1
        self.ops[eng].append(h)
        return h

    def _needs_wait(self, dep, eng):
        return dep.eng != eng or dep.dma or self.paranoid

    def build(self, nc, ctx, block):
        for e in self.ENGINES:
            for h in self.ops[e]:
                if h.dma:
                    h.sem = True
                for d in h.deps:
                    if self._needs_wait(d, e):
                        d.sem = True
        sems = {}
        dma_sems = {}
        for e in self.ENGINES:
            if any(o.sem and not o.dma for o in self.ops[e]):
                sems[e] = ctx.enter_context(nc.semaphore(f"prog_{e}"))
            ncls = 1 if self.serialize_dma else self.NDMA_CLASSES
            if any(o.dma for o in self.ops[e]):
                for cl in range(ncls):
                    dma_sems[(e, cl)] = ctx.enter_context(
                        nc.semaphore(f"dma_{e}_{cl}"))
        for e in self.ENGINES:
            c = 0
            dcnt = {}
            for o in self.ops[e]:
                if not o.sem:
                    continue
                if o.dma:
                    k = dcnt.get(o.dma_class, 0) + 16
                    dcnt[o.dma_class] = k
                    o.milestone_val = k
                else:
                    c += 1
                    o.milestone_val = c

        def emit_stream(q, ename):
            waited = {}
            for o in self.ops[ename]:
                for d in o.deps:
                    if not self._needs_wait(d, ename):
                        continue
                    v = d.milestone_val
                    assert v is not None
                    key = d.dma_class if d.dma else d.eng
                    sem = dma_sems[key] if d.dma else sems[key]
                    if v > waited.get(key, 0):
                        q.wait_ge(sem, v)
                        waited[key] = v
                ins = o.fn(q)
                if o.sem:
                    assert ins is not None, f"milestone op on {ename} returned None"
                    if o.dma:
                        ins.then_inc(dma_sems[o.dma_class], 16)
                    else:
                        ins.then_inc(sems[ename], 1)

        streams = {}
        if self.ops["sync"]:
            @block.sync
            def _(q):
                emit_stream(q, "sync")
        if self.ops["scalar"]:
            @block.scalar
            def _(q):
                emit_stream(q, "scalar")
        if self.ops["vector"]:
            @block.vector
            def _(q):
                emit_stream(q, "vector")
        if self.ops["tensor"]:
            @block.tensor
            def _(q):
                emit_stream(q, "tensor")
        if self.ops["gpsimd"]:
            @block.gpsimd
            def _(q):
                emit_stream(q, "gpsimd")
        return streams


def _chunk_schedule():
    out = []
    for qi in range(NQT):
        nk = 4 * qi
        for kc in range(nk + 4):
            out.append((qi, kc, kc - nk if kc >= nk else None))
    return out


CHUNKS = _chunk_schedule()
NCH = len(CHUNKS)


def build_l1(paranoid=True, serialize_dma=False):
    nc = bass.Bass("TRN2", target_bir_lowering=False, num_devices=NCORES)
    x_t = nc.dram_tensor("x_t", [D, T], F32, kind="ExternalInput").ap()
    wq_in = nc.dram_tensor("wq", [HPC, D, E], F32, kind="ExternalInput").ap()
    wk_in = nc.dram_tensor("wk", [HPC, D, E], F32, kind="ExternalInput").ap()
    wv_in = nc.dram_tensor("wv_mov", [D, HPC * E], F32, kind="ExternalInput").ap()
    bq_in = nc.dram_tensor("bq_t", [E, HPC], F32, kind="ExternalInput").ap()
    bk_in = nc.dram_tensor("bk_t", [E, HPC], F32, kind="ExternalInput").ap()
    bv_in = nc.dram_tensor("bv_mov", [1, HPC * E], F32, kind="ExternalInput").ap()
    ones_in = nc.dram_tensor("ones128", [1, 128], F32, kind="ExternalInput").ap()
    mask_in = nc.dram_tensor("masks", [4, 128, QT], F16, kind="ExternalInput").ap()
    z_out = nc.dram_tensor("z", [HPC * E, T], F16, kind="ExternalOutput").ap()
    xbf = nc.dram_tensor("xbf", [D, T], F16, kind="Internal")

    sch = Sched(paranoid=paranoid, serialize_dma=serialize_dma)
    with ExitStack() as ctx:
        ee = ctx.enter_context
        wq_s = ee(nc.sbuf_tensor("wq_s", [128, HPC * DC * E], F16))
        wk_s = ee(nc.sbuf_tensor("wk_s", [128, HPC * DC * E], F16))
        wv_s = ee(nc.sbuf_tensor("wv_s", [128, DC * HPC * E], F16))
        bq_s = ee(nc.sbuf_tensor("bq_s", [128, HPC], F32))
        bk_s = ee(nc.sbuf_tensor("bk_s", [128, HPC], F32))
        bv_row = ee(nc.sbuf_tensor("bv_row", [1, HPC * E], F32))
        bvB = ee(nc.sbuf_tensor("bvB", [128, HPC * E], F32))
        ones_row = ee(nc.sbuf_tensor("ones_row", [1, 128], F32))
        onesb_col = ee(nc.sbuf_tensor("onesb_col", [128, 1], F16))
        mask_s = ee(nc.sbuf_tensor("mask_s", [128, 4 * QT], F16))
        q_s = [ee(nc.sbuf_tensor(f"q_s{i}", [128, T], F16)) for i in range(2)]
        k_s = [ee(nc.sbuf_tensor(f"k_s{i}", [128, T], F16)) for i in range(2)]
        v_s = [ee(nc.sbuf_tensor(f"v_s{i}", [128, T], F16)) for i in range(2)]
        xt_t = [ee(nc.sbuf_tensor(f"xt_t{i}", [128, 4 * QT], F16)) for i in range(2)]
        sq_t = [ee(nc.sbuf_tensor(f"sq_t{i}", [128, QT], F16)) for i in range(2)]
        pT = [ee(nc.sbuf_tensor(f"pT{i}", [128, 2 * QT], F16)) for i in range(2)]
        zn_s = [ee(nc.sbuf_tensor(f"zn_s{i}", [128, QT], F16)) for i in range(2)]
        lrec = ee(nc.sbuf_tensor("lrec", [1, QT], F32))
        lrecB = ee(nc.sbuf_tensor("lrecB", [128, QT], F32))
        cm = ee(nc.sbuf_tensor("cm", [1, 4], F32))
        ctile = ee(nc.sbuf_tensor("ctile", [1, 8], F32))
        lnbias = ee(nc.sbuf_tensor("lnbias", [1, 1], F32))
        cbias = ee(nc.sbuf_tensor("cbias", [128, 2], F32))
        psA = ee(nc.psum_tensor("psA", [128, 2 * QT], F32))
        psB = ee(nc.psum_tensor("psB", [128, 2 * QT], F32))
        psZ = [ee(nc.psum_tensor(f"psZ{i}", [128, QT], F32)) for i in range(2)]
        psL = ee(nc.psum_tensor("psL", [128, QT], F32))
        psR = ee(nc.psum_tensor("psR", [128, QT], F32))
        block = ee(nc.Block())

        def bank(b, c0=0, c1=QT, r0=0, r1=128):
            if b < 4:
                t = psA if b < 2 else psB
                off = QT * (b % 2)
                return t[r0:r1, off + c0:off + c1]
            return [psZ[0], psZ[1], psL, psR][b - 4][r0:r1, c0:c1]

        def wq_tile(h, dc):
            return wq_s[:, (h * DC + dc) * E:(h * DC + dc + 1) * E]

        def wk_tile(h, dc):
            return wk_s[:, (h * DC + dc) * E:(h * DC + dc + 1) * E]

        stage = [sch.op("gpsimd",
                        lambda g, i=i: g.dma_start(
                            out=xbf.ap()[:, i * QT:(i + 1) * QT],
                            in_=x_t[:, i * QT:(i + 1) * QT]), dma=True)
                 for i in range(NQT)]
        wloads = []
        for h in range(HPC):
            wloads.append(sch.op("gpsimd", lambda g, h=h: g.dma_start(
                out=wq_s[:, :].rearrange("p (h dc e) -> p h dc e", h=HPC, dc=DC)[:, h],
                in_=wq_in[h].rearrange("(dc p) e -> p dc e", p=128)), dma=True))
            wloads.append(sch.op("gpsimd", lambda g, h=h: g.dma_start(
                out=wk_s[:, :].rearrange("p (h dc e) -> p h dc e", h=HPC, dc=DC)[:, h],
                in_=wk_in[h].rearrange("(dc p) e -> p dc e", p=128)), dma=True))
        wloads.append(sch.op("gpsimd", lambda g: g.dma_start(
            out=wv_s[:, :].rearrange("p (dc e) -> p dc e", dc=DC),
            in_=wv_in.rearrange("(dc p) e -> p dc e", p=128)), dma=True))
        ld_bq = sch.op("gpsimd", lambda g: g.dma_start(out=bq_s[:, :], in_=bq_in), dma=True)
        ld_bk = sch.op("gpsimd", lambda g: g.dma_start(out=bk_s[:, :], in_=bk_in), dma=True)
        ld_bv = sch.op("gpsimd", lambda g: g.dma_start(out=bv_row[:, :], in_=bv_in), dma=True)
        ld_on = sch.op("gpsimd", lambda g: g.dma_start(out=ones_row[:, :], in_=ones_in), dma=True)
        ld_mk = sch.op("gpsimd", lambda g: g.dma_start(
            out=mask_s[:, :].rearrange("p (c q) -> p c q", c=4),
            in_=mask_in.rearrange("c p q -> p c q")), dma=True)
        ms_onesb = sch.op("vector", lambda v: v.memset(onesb_col[:, :], 1.0))
        ms_lnb = sch.op("vector", lambda v: v.memset(lnbias[:, :], -LN_ATTN_SCALE))
        mm_bvB = sch.op("tensor", lambda t: t.matmul(
            psR[:, 0:HPC * E], ones_row[:, :], bv_row[:, :], start=True, stop=True),
            deps=[ld_bv, ld_on])
        ev_bvB = sch.op("vector", lambda v: v.tensor_copy(bvB[:, :], psR[:, 0:HPC * E]),
                        deps=[mm_bvB])

        tile_user = {}
        g_tile = 0
        zstores = []
        exp_g = {}
        pv_g = {}
        zn_g = {}
        rc_g = {}
        gci = 0
        gqi = 0
        lb_prev = None
        lc_prev = None
        cchain_last = ev_bvB
        attn_prev_tensor = None
        attn_prev_vector = None
        attn_prev_scalar = None

        for grp in range(2):
            qk_evac = {}
            for tt in range(NQT):
                for dq in range(4):
                    i = g_tile
                    deps = [stage[tt]]
                    if i >= 2:
                        deps.append(tile_user[i - 2])
                    ld = sch.op("sync", lambda q, dq=dq, tt=tt, i=i: q.dma_start(
                        out=xt_t[i % 2][:, :].rearrange("p (j t) -> p j t", j=4),
                        in_=xbf.ap().rearrange("(dc p) t -> p dc t", p=128)[
                            :, dq * 4:(dq + 1) * 4, tt * QT:(tt + 1) * QT]),
                        deps=deps, dma=True)
                    last = None
                    for j in range(4):
                        dc = dq * 4 + j
                        for pi in range(2):
                            h = grp * 2 + pi
                            b0 = (tt % 2) * 4 + 2 * pi
                            d_ = [ld]
                            if i == 0 and grp == 0:
                                d_ += wloads
                            if dc == 0:
                                if grp == 0 and tt == 1:
                                    d_.append(ev_bvB)
                                if tt >= 2:
                                    d_.append(qk_evac[(tt - 2, pi, 1)])
                                elif grp == 1:
                                    d_ += [attn_prev_scalar, attn_prev_vector]
                            sch.op("tensor", lambda t, h=h, dc=dc, j=j, b0=b0, i=i:
                                   t.matmul(bank(b0), wq_tile(h, dc),
                                            xt_t[i % 2][:, j * QT:(j + 1) * QT],
                                            start=(dc == 0), stop=(dc == DC - 1)),
                                   deps=d_)
                            last = sch.op("tensor",
                                          lambda t, h=h, dc=dc, j=j, b0=b0, i=i:
                                          t.matmul(bank(b0 + 1), wk_tile(h, dc),
                                                   xt_t[i % 2][:, j * QT:(j + 1) * QT],
                                                   start=(dc == 0),
                                                   stop=(dc == DC - 1)))
                    tile_user[i] = last
                    g_tile += 1
                    if dq == 3:
                        for pi in range(2):
                            h = grp * 2 + pi
                            b0 = (tt % 2) * 4 + 2 * pi
                            d_ = [last, ld_bq, ld_bk]
                            if grp == 1 and tt < 2:
                                d_.append(attn_prev_tensor)
                            e1 = sch.op("vector", lambda v, pi=pi, h=h, b0=b0, tt=tt:
                                        v.tensor_scalar_add(
                                            q_s[pi][:, tt * QT:(tt + 1) * QT],
                                            bank(b0), bq_s[:, h:h + 1]),
                                        deps=d_)
                            e2 = sch.op("vector", lambda v, pi=pi, h=h, b0=b0, tt=tt:
                                        v.tensor_scalar_add(
                                            k_s[pi][:, tt * QT:(tt + 1) * QT],
                                            bank(b0 + 1), bk_s[:, h:h + 1]))
                            qk_evac[(tt, pi, 0)] = e1
                            qk_evac[(tt, pi, 1)] = e2
            red_ops = {}
            sq_ops = {}
            j_list = [(pi, isk) for pi in range(2) for isk in range(2)]
            for jj, (pi, isk) in enumerate(j_list):
                src = (k_s if isk else q_s)[pi]
                for tt in range(NQT):
                    u = jj * NQT + tt
                    d_ = [qk_evac[(tt, pi, isk)]]
                    if u >= 2:
                        d_.append(red_ops[u - 2])
                    sqo = sch.op("scalar", lambda s, src=src, tt=tt, u=u:
                                 s.activation(out=sq_t[u % 2][:, :],
                                              in_=src[:, tt * QT:(tt + 1) * QT],
                                              func=AF.Square), deps=d_)
                    sq_ops[u] = sqo
                    d_ = [sqo, ms_onesb]
                    if u == 0:
                        d_.append(cchain_last)
                    if u < 4:
                        d_.append(qk_evac[(NQT - 2, 1, 1)])
                        d_.append(qk_evac[(NQT - 1, 1, 1)])
                    if u >= 4:
                        d_.append(red_ops[u - 4])
                    mmo = sch.op("tensor", lambda t, u=u: t.matmul(
                        bank(u % 4, r0=0, r1=1), onesb_col[:, :], sq_t[u % 2][:, :],
                        start=True, stop=True), deps=d_)
                    rd = sch.op("vector", lambda v, u=u: v.reduce_max(
                        ctile[0:1, u % 4:u % 4 + 1], bank(u % 4, r0=0, r1=1),
                        axis=AX.X), deps=[mmo])
                    slot = jj
                    if tt == 0:
                        acc = sch.op("vector", lambda v, slot=slot, u=u: v.tensor_copy(
                            cm[0:1, slot:slot + 1], ctile[0:1, u % 4:u % 4 + 1]), deps=[rd])
                    else:
                        acc = sch.op("vector", lambda v, slot=slot, u=u: v.tensor_max(
                            cm[0:1, slot:slot + 1], cm[0:1, slot:slot + 1],
                            ctile[0:1, u % 4:u % 4 + 1]), deps=[rd])
                    red_ops[u] = acc
            cb_ready = []
            for pi in range(2):
                pr = sch.op("vector", lambda v, pi=pi: v.tensor_mul(
                    ctile[0:1, 4 + pi:5 + pi], cm[0:1, 2 * pi:2 * pi + 1],
                    cm[0:1, 2 * pi + 1:2 * pi + 2]),
                    deps=[red_ops[(2 * pi) * NQT + NQT - 1],
                          red_ops[(2 * pi + 1) * NQT + NQT - 1]])
                ln_ = sch.op("scalar", lambda s, pi=pi: s.activation(
                    out=ctile[0:1, 4 + pi:5 + pi], in_=ctile[0:1, 4 + pi:5 + pi],
                    func=AF.Ln), deps=[pr])
                ex_ = sch.op("scalar", lambda s, pi=pi: s.activation(
                    out=ctile[0:1, 4 + pi:5 + pi], in_=ctile[0:1, 4 + pi:5 + pi],
                    func=AF.Exp, bias=lnbias[0:1, 0:1], scale=0.5), deps=[ln_, ms_lnb])
                ng = sch.op("vector", lambda v, pi=pi: v.tensor_scalar(
                    ctile[0:1, 4 + pi:5 + pi], ctile[0:1, 4 + pi:5 + pi],
                    -1.0, 8.0, OP.mult, OP.add), deps=[ex_])
                bcd = [ng]
                if pi == 0:
                    bcd.append(ev_bvB if grp == 0 else lc_prev)
                else:
                    bcd.append(cb_ready[0])
                bc = sch.op("tensor", lambda t, pi=pi: t.matmul(
                    psR[:, 0:1], ones_row[:, :], ctile[0:1, 4 + pi:5 + pi],
                    start=True, stop=True), deps=bcd)
                cb = sch.op("vector", lambda v, pi=pi: v.tensor_copy(
                    cbias[:, pi:pi + 1], psR[:, 0:1]), deps=[bc])
                cb_ready.append(cb)
            v_evac = {}
            for tt in range(NQT):
                for dq in range(4):
                    i = g_tile
                    deps = [stage[tt]]
                    if i >= 2:
                        deps.append(tile_user[i - 2])
                    ld = sch.op("sync", lambda q, dq=dq, tt=tt, i=i: q.dma_start(
                        out=xt_t[i % 2][:, :].rearrange("p (j t) -> p j t", j=4),
                        in_=xbf.ap().rearrange("(dc p) t -> p dc t", p=128)[
                            :, dq * 4:(dq + 1) * 4, tt * QT:(tt + 1) * QT]),
                        deps=deps, dma=True)
                    last = None
                    for j in range(4):
                        dc = dq * 4 + j
                        for ts in range(4):
                            b = (tt % 2) * 4 + ts
                            d_ = [ld]
                            if dc == 0:
                                if tt >= 2:
                                    d_.append(v_evac[(tt - 2, ts, 1)])
                                else:
                                    d_.append(qk_evac[(NQT - 2 + tt, 1, 1)])
                                    d_.append(red_ops[4 * NQT - 1])
                                    d_.append(cb_ready[-1])
                            last = sch.op("tensor",
                                          lambda t, i=i, ts=ts, j=j, b=b, dc=dc,
                                          grp=grp:
                                          t.matmul(bank(b, 0, 2 * E),
                                                   xt_t[i % 2][:, j * QT + ts * 128:
                                                        j * QT + (ts + 1) * 128],
                                                   wv_s[:, dc * HPC * E + grp * 2 * E:
                                                        dc * HPC * E
                                                        + (grp * 2 + 2) * E],
                                                   start=(dc == 0),
                                                   stop=(dc == DC - 1)),
                                          deps=d_)
                    tile_user[i] = last
                    g_tile += 1
                    if dq == 3:
                        for ts in range(4):
                            b = (tt % 2) * 4 + ts
                            tc = tt * 4 + ts
                            for pi in range(2):
                                h = grp * 2 + pi
                                ev = sch.op("vector", lambda v, pi=pi, h=h, b=b, tc=tc:
                                            v.tensor_add(
                                                v_s[pi][:, tc * 128:(tc + 1) * 128],
                                                bank(b, pi * 128, (pi + 1) * 128),
                                                bvB[:, h * 128:(h + 1) * 128]),
                                            deps=[last, ev_bvB])
                                v_evac[(tt, ts, pi)] = ev
            last_v_evac = v_evac[(NQT - 1, 3, 1)]
            lc_prev = cb_ready[1]
            for pi in range(2):
                pair_h = grp * 2 + pi

                def emit_S(n, qi, pl, gp):
                    pp = psA if gp % 2 == 0 else psB
                    d_ = []
                    if gp >= 2 and gp - 2 in exp_g:
                        d_.append(exp_g[gp - 2])
                    else:
                        d_.append(last_v_evac)
                    if n < 2:
                        d_.append(last_v_evac)
                        d_ += cb_ready
                    sch.op("tensor", lambda t, pp=pp, pi=pi, qi=qi, pl=pl:
                           t.matmul(pp[:, 0:QT],
                                    k_s[pi][:, (2 * pl) * 128:(2 * pl + 1) * 128],
                                    q_s[pi][:, qi * QT:(qi + 1) * QT],
                                    start=True, stop=True), deps=d_)
                    sc2 = sch.op("tensor", lambda t, pp=pp, pi=pi, qi=qi, pl=pl:
                                 t.matmul(pp[:, QT:2 * QT],
                                          k_s[pi][:, (2 * pl + 1) * 128:
                                                  (2 * pl + 2) * 128],
                                          q_s[pi][:, qi * QT:(qi + 1) * QT],
                                          start=True, stop=True))
                    return sc2

                def emit_EXP(n, qi, pl, gp, sc2):
                    pp = psA if gp % 2 == 0 else psB
                    ed = [sc2, cb_ready[pi]]
                    if gp >= 2 and gp - 2 in pv_g:
                        ed.append(pv_g[gp - 2])
                    ex = sch.op("scalar", lambda s, pp=pp, gp=gp, pi=pi:
                                s.activation(out=pT[gp % 2][:, :], in_=pp[:, :],
                                             func=AF.Exp, bias=cbias[:, pi:pi + 1],
                                             scale=INV_SCALE), deps=ed)
                    exp_g[gp] = ex
                    return ex

                def emit_PVL(n, qi, pl, npair, gp):
                    nonlocal gqi, lb_prev, lc_prev
                    first = pl == 0
                    last_c = pl == npair - 1
                    zb = psZ[gqi % 2]
                    pvd = [exp_g[gp]]
                    if pl >= npair - 2:
                        mp = pl - (npair - 2)
                        mko = sch.op("vector", lambda v, gp=gp, mp=mp:
                                     v.tensor_mul(pT[gp % 2][:, :], pT[gp % 2][:, :],
                                                  mask_s[:, mp * 2 * QT:
                                                         (mp + 1) * 2 * QT]),
                                     deps=[exp_g[gp], ld_mk])
                        pvd = [mko]
                    if first:
                        if gqi >= 2:
                            pvd.append(zn_g[gqi - 2])
                        else:
                            pvd.append(last_v_evac)
                    for half in range(2):
                        kc = 2 * pl + half
                        sch.op("tensor", lambda t, zb=zb, pi=pi, kc=kc, gp=gp,
                               half=half, st=(first and half == 0),
                               sp=(last_c and half == 1):
                               t.matmul(zb[:, :],
                                        v_s[pi][:, kc * 128:(kc + 1) * 128],
                                        pT[gp % 2][:, half * QT:(half + 1) * QT],
                                        start=st, stop=sp), deps=pvd if half == 0
                               else [])
                    for half in range(2):
                        d_ = []
                        if half == 0 and first:
                            d_ = [rc_g[gqi - 1]] if gqi >= 1 else [last_v_evac]
                        pv = sch.op("tensor", lambda t, gp=gp, half=half,
                                    st=(first and half == 0),
                                    sp=(last_c and half == 1):
                                    t.matmul(psL[0:1, 0:QT], onesb_col[:, :],
                                             pT[gp % 2][:, half * QT:(half + 1) * QT],
                                             start=st, stop=sp), deps=d_)
                    pv_g[gp] = pv
                    if last_c:
                        rcd = [pv]
                        if lb_prev is not None:
                            rcd.append(lb_prev)
                        rc = sch.op("vector", lambda v: v.reciprocal(
                            lrec[:, :], psL[0:1, 0:QT]), deps=rcd)
                        rc_g[gqi] = rc
                        lbd = [rc, lc_prev]
                        lb = sch.op("tensor", lambda t: t.matmul(
                            psR[:, :], ones_row[:, :], lrec[:, :],
                            start=True, stop=True), deps=lbd)
                        lb_prev = lb
                        lc = sch.op("vector", lambda v: v.tensor_copy(
                            lrecB[:, :], psR[:, :]), deps=[lb])
                        lc_prev = lc
                        znd = [lc]
                        if len(zstores) >= 2:
                            znd.append(zstores[-2])
                        zn = sch.op("vector", lambda v, gqi=gqi, zb=zb: v.tensor_mul(
                            zn_s[gqi % 2][:, :], zb[:, :], lrecB[:, :]), deps=znd)
                        zn_g[gqi] = zn
                        st = sch.op("sync", lambda q, pair_h=pair_h, qi=qi, gqi=gqi:
                                    q.dma_start(
                                        out=z_out[pair_h * E:(pair_h + 1) * E,
                                                  qi * QT:(qi + 1) * QT],
                                        in_=zn_s[gqi % 2][:, :]), deps=[zn], dma=True)
                        zstores.append(st)
                        gqi += 1

                plist = []
                for qi in range(NQT):
                    npair = 2 * qi + 2
                    for pl in range(npair):
                        plist.append((qi, pl, npair))
                prev = None
                for n, (qi, pl, npair) in enumerate(plist):
                    gp = gci
                    gci += 1
                    sc2 = emit_S(n, qi, pl, gp)
                    emit_EXP(n, qi, pl, gp, sc2)
                    if prev is not None:
                        emit_PVL(*prev)
                    prev = (n, qi, pl, npair, gp)
                emit_PVL(*prev)
            attn_prev_tensor = pv_g[gci - 1]
            attn_prev_vector = zn_g[gqi - 1]
            attn_prev_scalar = exp_g[gci - 1]
            cchain_last = zn_g[gqi - 1]
        sch.op("sync", lambda q: q.nop(), deps=zstores)
        sch.build(nc, ctx, block)
    return nc


def build_l2(paranoid=True, serialize_dma=False):
    nc = bass.Bass("TRN2", target_bir_lowering=False, num_devices=NCORES)
    z_in = nc.dram_tensor("z16", [H, E, 2048], F16, kind="ExternalInput").ap()
    wo_in = nc.dram_tensor("wo", [H, E, D], F32, kind="ExternalInput").ap()
    bo_in = nc.dram_tensor("bo", [1, D], F32, kind="ExternalInput").ap()
    xo_in = nc.dram_tensor("x_own", [2048, D], F32, kind="ExternalInput").ap()
    wrt_in = nc.dram_tensor("wrt", [1, D], F32, kind="ExternalInput").ap()
    ones_in = nc.dram_tensor("ones128", [1, 128], F32, kind="ExternalInput").ap()
    x1_out = nc.dram_tensor("x1", [2048, D], F32, kind="ExternalOutput").ap()
    x1h_out = nc.dram_tensor("x1h", [2048, D], F16, kind="ExternalOutput").ap()
    sc_out = nc.dram_tensor("scores", [128, 16], F32, kind="ExternalOutput").ap()

    NTC = 16
    sch = Sched(paranoid=paranoid, serialize_dma=serialize_dma)
    with ExitStack() as ctx:
        ee = ctx.enter_context
        z_s = ee(nc.sbuf_tensor("z_s", [128, H * 2048], F16))
        wo_s = ee(nc.sbuf_tensor("wo_s", [128, H * D], F16))
        bo_row = ee(nc.sbuf_tensor("bo_row", [1, D], F32))
        wrt_row = ee(nc.sbuf_tensor("wrt_row", [1, D], F32))
        ones_row = ee(nc.sbuf_tensor("ones_row", [1, 128], F32))
        boB = ee(nc.sbuf_tensor("boB", [128, D], F32))
        wrtB = ee(nc.sbuf_tensor("wrtB", [128, D], F32))
        xo_t = [ee(nc.sbuf_tensor(f"xo_t{i}", [128, D], F32)) for i in range(2)]
        r_t = ee(nc.sbuf_tensor("r_t", [128, D], F32))
        x1_t = [ee(nc.sbuf_tensor(f"x1_t{i}", [128, D], F32)) for i in range(2)]
        st1 = ee(nc.sbuf_tensor("st1", [128, 8], F32))
        sc_s = ee(nc.sbuf_tensor("sc_s", [128, 16], F32))
        eps_c = ee(nc.sbuf_tensor("eps_c", [128, 1], F32))
        ps = [ee(nc.psum_tensor(f"ps{i}", [128, QT], F32)) for i in range(8)]
        block = ee(nc.Block())

        ld_z = [sch.op("gpsimd", lambda g, h=h: g.dma_start(
            out=z_s[:, h * 2048:(h + 1) * 2048], in_=z_in[h]), dma=True)
            for h in range(H)]
        ld_wo = [sch.op("gpsimd", lambda g, h=h: g.dma_start(
            out=wo_s[:, h * D:(h + 1) * D], in_=wo_in[h]), dma=True)
            for h in range(H)]
        ld_bo = sch.op("gpsimd", lambda g: g.dma_start(out=bo_row[:, :], in_=bo_in), dma=True)
        ld_wr = sch.op("gpsimd", lambda g: g.dma_start(out=wrt_row[:, :], in_=wrt_in), dma=True)
        ld_on = sch.op("gpsimd", lambda g: g.dma_start(out=ones_row[:, :], in_=ones_in), dma=True)
        ms_eps = sch.op("vector", lambda v: v.memset(eps_c[:, :], EPS_LN))
        bco = []
        for j in range(4):
            m1 = sch.op("tensor", lambda t, j=j: t.matmul(
                ps[0][:, :], ones_row[:, :], bo_row[0:1, j * QT:(j + 1) * QT],
                start=True, stop=True), deps=[ld_bo, ld_on] + ([bco[-1]] if bco else []))
            c1 = sch.op("vector", lambda v, j=j: v.tensor_copy(
                boB[:, j * QT:(j + 1) * QT], ps[0][:, :]), deps=[m1])
            m2 = sch.op("tensor", lambda t, j=j: t.matmul(
                ps[1][:, :], ones_row[:, :], wrt_row[0:1, j * QT:(j + 1) * QT],
                start=True, stop=True), deps=[ld_wr, c1])
            c2 = sch.op("vector", lambda v, j=j: v.tensor_copy(
                wrtB[:, j * QT:(j + 1) * QT], ps[1][:, :]), deps=[m2])
            bco.append(c2)
        bcast_done = bco[-1]

        x1_stores = []
        x1h_stores = []
        prev_epi = {}
        xb_ops = {}
        rt_free = {}
        x1o_prev = None
        lnv_prev = None
        last_score = None
        for tc in range(NTC):
            par = tc % 2
            ld_x = sch.op("sync", lambda q, tc=tc: q.dma_start(
                out=xo_t[tc % 2][:, :], in_=xo_in[tc * 128:(tc + 1) * 128, :]),
                deps=[xb_ops.get(tc - 2)], dma=True)
            mm_last = {}
            for dblk in range(4):
                bank = par * 4 + dblk
                last = None
                for h in range(H):
                    d_ = []
                    if h == 0:
                        d_ = ld_z + ld_wo + [bcast_done]
                        if tc >= 2:
                            d_.append(prev_epi[par])
                    last = sch.op("tensor", lambda t, h=h, dblk=dblk, bank=bank, tc=tc:
                                  t.matmul(ps[bank][:, :],
                                           z_s[:, h * 2048 + tc * 128:
                                               h * 2048 + (tc + 1) * 128],
                                           wo_s[:, h * D + dblk * QT:
                                                h * D + (dblk + 1) * QT],
                                           start=(h == 0), stop=(h == H - 1)), deps=d_)
                mm_last[dblk] = last
            xb = sch.op("vector", lambda v, tc=tc: v.tensor_add(
                xo_t[tc % 2][:, :], xo_t[tc % 2][:, :], boB[:, :]),
                deps=[ld_x, bcast_done])
            xb_ops[tc] = xb
            ep = None
            for dblk in range(4):
                d_ = [mm_last[dblk], xb]
                if dblk == 0 and tc >= 1:
                    d_.append(rt_free[tc - 1])
                ep = sch.op("vector", lambda v, tc=tc, dblk=dblk, bank=par * 4 + dblk:
                            v.tensor_add(r_t[:, dblk * QT:(dblk + 1) * QT],
                                         ps[bank][:, :],
                                         xo_t[tc % 2][:, dblk * QT:(dblk + 1) * QT]),
                            deps=d_)
            prev_epi[par] = ep
            sm = sch.op("vector", lambda v: v.reduce_sum(
                st1[:, 0:1], r_t[:, :], axis=AX.X), deps=[ep])
            s2 = sch.op("scalar", lambda s, tc=tc: s.activation(
                out=x1_t[tc % 2][:, :], in_=r_t[:, :], func=AF.Square,
                accum_out=st1[:, 1:2]),
                deps=[ep, x1o_prev,
                      x1_stores[-2] if len(x1_stores) >= 2 else None,
                      x1h_stores[-2] if len(x1h_stores) >= 2 else None])
            mean = sch.op("vector", lambda v: v.tensor_scalar_mul(
                st1[:, 2:3], st1[:, 0:1], 1.0 / D), deps=[sm])
            msq = sch.op("vector", lambda v: v.tensor_scalar_mul(
                st1[:, 3:4], st1[:, 1:2], 1.0 / D), deps=[s2])
            m2_ = sch.op("vector", lambda v: v.tensor_mul(
                st1[:, 4:5], st1[:, 2:3], st1[:, 2:3]), deps=[mean])
            var = sch.op("vector", lambda v: v.tensor_sub(
                st1[:, 5:6], st1[:, 3:4], st1[:, 4:5]), deps=[msq, m2_, lnv_prev])
            lnv = sch.op("scalar", lambda s: s.activation(
                out=st1[:, 6:7], in_=st1[:, 5:6], func=AF.Ln,
                bias=eps_c[:, 0:1]), deps=[var, ms_eps])
            lnv_prev = lnv
            rstd = sch.op("scalar", lambda s: s.activation(
                out=st1[:, 7:8], in_=st1[:, 6:7], func=AF.Exp,
                scale=-0.5), deps=[lnv])
            x1o = sch.op("vector", lambda v, tc=tc: v.tensor_scalar(
                x1_t[tc % 2][:, :], r_t[:, :], st1[:, 2:3], st1[:, 7:8],
                OP.subtract, OP.mult),
                deps=[rstd, x1_stores[-2] if len(x1_stores) >= 2 else None,
                      x1h_stores[-2] if len(x1h_stores) >= 2 else None])
            x1o_prev = x1o
            st = sch.op("sync", lambda q, tc=tc: q.dma_start(
                out=x1_out[tc * 128:(tc + 1) * 128, :], in_=x1_t[tc % 2][:, :]),
                deps=[x1o], dma=True)
            sth = sch.op("gpsimd", lambda g, tc=tc: g.dma_start(
                out=x1h_out[tc * 128:(tc + 1) * 128, :], in_=x1_t[tc % 2][:, :]),
                deps=[x1o], dma=True)
            x1_stores.append(st)
            x1h_stores.append(sth)
            scp = sch.op("vector", lambda v, tc=tc: v.tensor_mul(
                r_t[:, :], x1_t[tc % 2][:, :], wrtB[:, :]), deps=[x1o])
            red = sch.op("vector", lambda v, tc=tc: v.reduce_sum(
                sc_s[:, tc:tc + 1], r_t[:, :], axis=AX.X), deps=[scp])
            rt_free[tc] = red
            last_score = red
        st_sc = sch.op("sync", lambda q: q.dma_start(out=sc_out, in_=sc_s[:, :]),
                       deps=[last_score], dma=True)
        sch.op("sync", lambda q: q.nop(),
               deps=x1_stores + x1h_stores + [st_sc])
        sch.build(nc, ctx, block)
    return nc


BISECT_ITERS = 44


def build_l3(paranoid=True, serialize_dma=False):
    nc = bass.Bass("TRN2", target_bir_lowering=False, num_devices=NCORES)
    x1f = nc.dram_tensor("x1f", [2 * T, D], F16, kind="ExternalInput").ap()
    sct_in = nc.dram_tensor("sctile", [128, 128], F32, kind="ExternalInput").ap()
    pos_in = nc.dram_tensor("postile", [128, 128], F32, kind="ExternalInput").ap()
    onesm_in = nc.dram_tensor("onesmat", [128, 128], F32, kind="ExternalInput").ap()
    ones_in = nc.dram_tensor("ones128", [1, 128], F32, kind="ExternalInput").ap()
    tril_in = nc.dram_tensor("trilT", [128, 128], F16, kind="ExternalInput").ap()
    siota_in = nc.dram_tensor("siota", [128, 1024], F16, kind="ExternalInput").ap()
    rot_in = nc.dram_tensor("rotT", [128, 128], F16, kind="ExternalInput").ap()
    invf_in = nc.dram_tensor("invf", [128, 1], F32, kind="ExternalInput").ap()
    ident_in = nc.dram_tensor("ident", [128, 128], F32, kind="ExternalInput").ap()
    pick_in = nc.dram_tensor("pickmat", [4, 4], F32, kind="ExternalInput").ap()
    mask_in = nc.dram_tensor("masks", [4, 128, QT], F16, kind="ExternalInput").ap()
    wq_in = nc.dram_tensor("wq_sel", [D, 2 * E], F32, kind="ExternalInput").ap()
    wk_in = nc.dram_tensor("wk_sel", [D, E], F32, kind="ExternalInput").ap()
    wv_in = nc.dram_tensor("wv_sel", [D, E], F32, kind="ExternalInput").ap()
    bq_in = nc.dram_tensor("bq_sel", [128, 2], F32, kind="ExternalInput").ap()
    bk_in = nc.dram_tensor("bk_sel", [128, 1], F32, kind="ExternalInput").ap()
    bv_in = nc.dram_tensor("bv_sel", [1, 128], F32, kind="ExternalInput").ap()
    ln1_in = nc.dram_tensor("ln1c", [128, 16], F32, kind="ExternalInput").ap()

    pl_out = nc.dram_tensor("poslist", [2, S_SEL], F32, kind="ExternalOutput").ap()
    selx_out = nc.dram_tensor("selx", [S_SEL, D], F16, kind="ExternalOutput").ap()
    zsel_out = nc.dram_tensor("z_sel", [2 * E, S_SEL], F16, kind="ExternalOutput").ap()
    selbf = nc.dram_tensor("selbf", [S_SEL, D], F16, kind="Internal")
    rstdd = nc.dram_tensor("rstdd", [1, S_SEL], F32, kind="Internal")

    sch = Sched(paranoid=paranoid, serialize_dma=serialize_dma)
    with ExitStack() as ctx:
        ee = ctx.enter_context
        sct = ee(nc.sbuf_tensor("sct", [128, 128], F32))
        post = ee(nc.sbuf_tensor("post", [128, 128], F32))
        onesm = ee(nc.sbuf_tensor("onesm_s", [128, 128], F32))
        ones_row = ee(nc.sbuf_tensor("ones_row", [1, 128], F32))
        ones16c = ee(nc.sbuf_tensor("ones16c", [128, 1], F16))
        tril = ee(nc.sbuf_tensor("tril", [128, 128], F16))
        siota = ee(nc.sbuf_tensor("siota_s", [128, 1024], F16))
        rott = ee(nc.sbuf_tensor("rott", [128, 128], F16))
        invf = ee(nc.sbuf_tensor("invf_s", [128, 1], F32))
        ident = ee(nc.sbuf_tensor("ident_s", [128, 128], F32))
        pick = ee(nc.sbuf_tensor("pick", [4, 4], F32))
        mask_s = ee(nc.sbuf_tensor("mask_s", [128, 4 * QT], F16))
        lohi = ee(nc.sbuf_tensor("lohi", [128, 4], F32))
        mid = ee(nc.sbuf_tensor("mid", [128, 2], F32))
        cmp_t = ee(nc.sbuf_tensor("cmp_t", [128, 128], F32))
        cnt = ee(nc.sbuf_tensor("cnt", [128, 2], F32))
        cntR = ee(nc.sbuf_tensor("cntR", [128, 2], F32))
        ge_t = ee(nc.sbuf_tensor("ge_t", [128, 2], mybir.dt.uint8))
        gn_t = ee(nc.sbuf_tensor("gn_t", [128, 2], mybir.dt.uint8))
        m4 = ee(nc.sbuf_tensor("m4", [128, 128], F32))
        r4 = ee(nc.sbuf_tensor("r4", [4, 1], F32))
        maskf = ee(nc.sbuf_tensor("maskf", [128, 128], F32))
        mask16 = ee(nc.sbuf_tensor("mask16", [128, 128], F16))
        srank = ee(nc.sbuf_tensor("srank", [128, 128], F32))
        scan_a = ee(nc.sbuf_tensor("scan_a", [1, 64], F32))
        scan_b = ee(nc.sbuf_tensor("scan_b", [1, 64], F32))
        ms1 = ee(nc.sbuf_tensor("ms1", [128, 128], F32))
        ms2 = ee(nc.sbuf_tensor("ms2", [128, 128], F32))
        oh = [ee(nc.sbuf_tensor(f"oh{i}", [128, 1024], F16)) for i in range(2)]
        pos_s = ee(nc.sbuf_tensor("pos_s", [1, S_SEL], F32))
        sc_r = ee(nc.sbuf_tensor("sc_r", [1, S_SEL], F32))
        posB = ee(nc.sbuf_tensor("posB", [128, S_SEL], F32))
        posT = ee(nc.sbuf_tensor("posT", [128, 16], F32))
        idx_u = ee(nc.sbuf_tensor("idx_u", [128, 16], U32))
        gx = [ee(nc.sbuf_tensor(f"gx{i}", [128, D], F16)) for i in range(2)]
        rst = ee(nc.sbuf_tensor("rst", [128, 80], F32))
        rstrow = ee(nc.sbuf_tensor("rstrow", [1, S_SEL], F32))
        rstdB = ee(nc.sbuf_tensor("rstdB", [128, S_SEL], F32))
        tmpT = [ee(nc.sbuf_tensor(f"tmpT{i}", [128, D], F16)) for i in range(2)]
        hT = ee(nc.sbuf_tensor("hT", [128, DC * S_SEL], F16))
        wq16 = ee(nc.sbuf_tensor("wq16", [128, DC * 2 * E], F16))
        wk16 = ee(nc.sbuf_tensor("wk16", [128, DC * E], F16))
        wv16 = ee(nc.sbuf_tensor("wv16", [128, DC * E], F16))
        bq_s = ee(nc.sbuf_tensor("bq_s", [128, 2], F32))
        bk_s = ee(nc.sbuf_tensor("bk_s", [128, 1], F32))
        bv_row = ee(nc.sbuf_tensor("bv_row", [1, 128], F32))
        bvB = ee(nc.sbuf_tensor("bvB", [128, 128], F32))
        qT = [ee(nc.sbuf_tensor(f"qT{i}", [128, S_SEL], F16)) for i in range(2)]
        kT = ee(nc.sbuf_tensor("kT", [128, S_SEL], F16))
        v16 = ee(nc.sbuf_tensor("v16", [128, S_SEL], F16))
        angB = ee(nc.sbuf_tensor("angB", [128, QT], F32))
        redu = ee(nc.sbuf_tensor("redu", [128, QT], F32))
        cosB = ee(nc.sbuf_tensor("cosB", [128, QT], F32))
        sinB = ee(nc.sbuf_tensor("sinB", [128, QT], F32))
        sq_t = [ee(nc.sbuf_tensor(f"sq_t{i}", [128, QT], F16)) for i in range(2)]
        ctile = ee(nc.sbuf_tensor("ctile", [1, 8], F32))
        cm = ee(nc.sbuf_tensor("cm", [1, 4], F32))
        lnbias = ee(nc.sbuf_tensor("lnbias", [1, 1], F32))
        cbias = ee(nc.sbuf_tensor("cbias", [128, 2], F32))
        pT = [ee(nc.sbuf_tensor(f"pT{i}", [128, QT], F16)) for i in range(2)]
        zn_s = [ee(nc.sbuf_tensor(f"zn_s{i}", [128, QT], F16)) for i in range(2)]
        lrec = ee(nc.sbuf_tensor("lrec", [1, QT], F32))
        lrecB = ee(nc.sbuf_tensor("lrecB", [128, QT], F32))
        epsR = ee(nc.sbuf_tensor("epsR", [128, 1], F32))
        ps = [ee(nc.psum_tensor(f"ps{i}", [128, QT], F32)) for i in range(8)]
        block = ee(nc.Block())

        lds = []
        for (dst, src) in ((sct, sct_in), (post, pos_in), (onesm, onesm_in),
                           (ones_row, ones_in), (tril, tril_in), (siota, siota_in),
                           (rott, rot_in), (invf, invf_in), (ident, ident_in),
                           (pick, pick_in), (bq_s, bq_in), (bk_s, bk_in),
                           (bv_row, bv_in)):
            lds.append(sch.op("gpsimd", lambda g, dst=dst, src=src:
                              g.dma_start(out=dst[:, :], in_=src), dma=True))
        ld_mask = sch.op("gpsimd", lambda g: g.dma_start(
            out=mask_s[:, :].rearrange("p (c q) -> p c q", c=4),
            in_=mask_in.rearrange("c p q -> p c q")), dma=True)
        ld_wq = sch.op("gpsimd", lambda g: g.dma_start(
            out=wq16[:, :].rearrange("p (dc e) -> p dc e", dc=DC),
            in_=wq_in.rearrange("(dc p) e -> p dc e", p=128)), dma=True)
        ld_wk = sch.op("gpsimd", lambda g: g.dma_start(
            out=wk16[:, :].rearrange("p (dc e) -> p dc e", dc=DC),
            in_=wk_in.rearrange("(dc p) e -> p dc e", p=128)), dma=True)
        ld_wv = sch.op("gpsimd", lambda g: g.dma_start(
            out=wv16[:, :].rearrange("p (dc e) -> p dc e", dc=DC),
            in_=wv_in.rearrange("(dc p) e -> p dc e", p=128)), dma=True)
        ms_ones16 = sch.op("vector", lambda v: v.memset(ones16c[:, :], 1.0))
        ms_eps = sch.op("vector", lambda v: v.memset(epsR[:, :], EPS_RMS))
        ms_lnb = sch.op("vector", lambda v: v.memset(lnbias[:, :], -LN_ATTN_SCALE))
        ln1c = ee(nc.sbuf_tensor("ln1c_s", [128, 16], F32))
        ld_ln1 = sch.op("gpsimd", lambda g: g.dma_start(
            out=ln1c[:, :], in_=ln1_in), dma=True)

        i_mn0 = sch.op("vector", lambda v: v.tensor_reduce(
            m4[:, 0:1], sct[:, 0:64], AX.X, OP.min), deps=[lds[0]])
        i_mx0 = sch.op("vector", lambda v: v.tensor_reduce(
            m4[:, 2:3], sct[:, 0:64], AX.X, OP.max))
        i_mn1 = sch.op("vector", lambda v: v.tensor_reduce(
            m4[:, 1:2], sct[:, 64:128], AX.X, OP.min))
        i_mx1 = sch.op("vector", lambda v: v.tensor_reduce(
            m4[:, 3:4], sct[:, 64:128], AX.X, OP.max))
        i_ng = sch.op("vector", lambda v: v.tensor_scalar_mul(
            m4[:, 2:4], m4[:, 2:4], -1.0), deps=[i_mx0, i_mx1])
        tr4 = sch.op("tensor", lambda t: t.transpose(
            ps[0][:, 0:128], m4[:, :], ident[:, :]),
            deps=[i_mn0, i_mn1, i_ng, lds[8]])
        rmn = sch.op("vector", lambda v: v.tensor_reduce(
            r4[0:4, 0:1], ps[0][0:4, 0:128], AX.X, OP.min), deps=[tr4])
        rowm = sch.op("tensor", lambda t: t.matmul(
            ps[1][0:1, 0:4], r4[0:4, 0:1], pick[0:4, 0:4],
            start=True, stop=True), deps=[rmn, lds[9]])
        row4e = sch.op("vector", lambda v: v.tensor_copy(
            scan_a[0:1, 0:4], ps[1][0:1, 0:4]), deps=[rowm])
        rep = sch.op("tensor", lambda t: t.matmul(
            ps[2][:, 0:4], ones_row[:, :], scan_a[0:1, 0:4],
            start=True, stop=True), deps=[row4e, lds[3]])
        cp_lh = sch.op("vector", lambda v: v.tensor_copy(lohi[:, :], ps[2][:, 0:4]),
                       deps=[rep])
        h1 = sch.op("vector", lambda v: v.tensor_scalar_mul(
            mid[:, 0:2], lohi[:, 2:4], -1.0), deps=[cp_lh])
        h2 = sch.op("vector", lambda v: v.tensor_max(
            mid[:, 0:2], mid[:, 0:2], lohi[:, 2:4]), deps=[h1])
        h3 = sch.op("vector", lambda v: v.tensor_scalar(
            mid[:, 0:2], mid[:, 0:2], 1e-4, 1e-20, OP.mult, OP.add), deps=[h2])
        hfin = sch.op("vector", lambda v: v.tensor_add(
            lohi[:, 2:4], lohi[:, 2:4], mid[:, 0:2]), deps=[h3])
        prev = hfin
        prev_mm = rep
        prev_gno = cp_lh
        for it in range(BISECT_ITERS):
            m1 = sch.op("vector", lambda v: v.tensor_scalar_mul(
                mid[:, :], lohi[:, 0:2], 0.5), deps=[prev])
            m2 = sch.op("vector", lambda v: v.scalar_tensor_tensor(
                cnt[:, :], lohi[:, 2:4], 0.5, mid[:, :], OP.mult, OP.add),
                deps=[m1])
            c0 = sch.op("vector", lambda v: v.tensor_scalar(
                cmp_t[:, 0:64], sct[:, 0:64], cnt[:, 0:1], 0.0, OP.is_ge,
                OP.add, accum_out=cntR[:, 0:1]), deps=[m2])
            c1 = sch.op("vector", lambda v: v.tensor_scalar(
                cmp_t[:, 64:128], sct[:, 64:128], cnt[:, 1:2], 0.0, OP.is_ge,
                OP.add, accum_out=cntR[:, 1:2]), deps=[c0])
            mmc = sch.op("tensor", lambda t: t.matmul(
                ps[3][:, 0:2], onesm[:, :], cntR[:, :], start=True, stop=True),
                deps=[c1, lds[2], prev_mm, prev_gno])
            prev_mm = mmc
            geo = sch.op("vector", lambda v: v.tensor_scalar(
                ge_t[:, :], ps[3][:, 0:2], float(SELK), None, OP.is_ge), deps=[mmc])
            gno = sch.op("vector", lambda v: v.tensor_scalar(
                gn_t[:, :], ps[3][:, 0:2], float(SELK), None, OP.is_lt), deps=[geo])
            prev_gno = gno
            s_lo = sch.op("vector", lambda v: v.copy_predicated(
                lohi[:, 0:2], ge_t[:, :], cnt[:, :]), deps=[geo])
            s_hi = sch.op("vector", lambda v: v.copy_predicated(
                lohi[:, 2:4], gn_t[:, :], cnt[:, :]), deps=[gno, s_lo])
            prev = s_hi
        thr_ready = prev

        mks = []
        srk = []
        bca_prev = None
        cpy_prev = None
        for b in range(2):
            mk = sch.op("vector", lambda v, b=b: v.tensor_scalar(
                maskf[:, 64 * b:64 * b + 64], sct[:, 64 * b:64 * b + 64],
                lohi[:, b:b + 1], None, OP.is_ge), deps=[thr_ready])
            mk16 = sch.op("vector", lambda v, b=b: v.tensor_copy(
                mask16[:, 64 * b:64 * b + 64], maskf[:, 64 * b:64 * b + 64]),
                deps=[mk])
            cumm = sch.op("tensor", lambda t, b=b: t.matmul(
                ps[4 + b][:, 0:64], tril[:, :], mask16[:, 64 * b:64 * b + 64],
                start=True, stop=False), deps=[mk16, lds[4], prev_mm])
            cs = sch.op("tensor", lambda t, b=b: t.matmul(
                ps[6][0:1, 64 * b:64 * b + 64], ones16c[:, :],
                mask16[:, 64 * b:64 * b + 64],
                start=True, stop=True), deps=[mk16, ms_ones16, cpy_prev])
            cpy = sch.op("vector", lambda v, b=b: v.tensor_copy(
                scan_a[0:1, :], ps[6][0:1, 64 * b:64 * b + 64]),
                deps=[cs, bca_prev])
            cpy_prev = cpy
            pr = cpy
            cur, oth = scan_a, scan_b
            for sh in (1, 2, 4, 8, 16, 32):
                cp1 = sch.op("vector", lambda v, cur=cur, oth=oth, sh=sh:
                             v.tensor_copy(oth[0:1, 0:sh], cur[0:1, 0:sh]), deps=[pr])
                ad1 = sch.op("vector", lambda v, cur=cur, oth=oth, sh=sh:
                             v.tensor_add(oth[0:1, sh:64], cur[0:1, sh:64],
                                          cur[0:1, 0:64 - sh]), deps=[cp1])
                pr = ad1
                cur, oth = oth, cur
            z0 = sch.op("vector", lambda v, oth=oth: v.memset(oth[0:1, 0:1], 0.0),
                        deps=[pr])
            sh1 = sch.op("vector", lambda v, cur=cur, oth=oth: v.tensor_copy(
                oth[0:1, 1:64], cur[0:1, 0:63]), deps=[z0])
            bca = sch.op("tensor", lambda t, b=b, oth=oth: t.matmul(
                ps[4 + b][:, 0:64], ones_row[:, :], oth[0:1, 0:64],
                start=False, stop=True), deps=[sh1])
            bca_prev = bca
            sr_a = sch.op("vector", lambda v, b=b: v.tensor_scalar_add(
                cmp_t[:, 64 * b:64 * b + 64], ps[4 + b][:, 0:64],
                float(1e6)), deps=[bca, mk])
            sr = sch.op("vector", lambda v, b=b: v.scalar_tensor_tensor(
                srank[:, 64 * b:64 * b + 64], maskf[:, 64 * b:64 * b + 64],
                -1000001.0, cmp_t[:, 64 * b:64 * b + 64], OP.mult, OP.add),
                deps=[sr_a])
            geb = sch.op("vector", lambda v, b=b: v.tensor_scalar(
                cmp_t[:, 64 * b:64 * b + 64], srank[:, 64 * b:64 * b + 64],
                float(1024), None, OP.is_ge), deps=[sr])
            cl = sch.op("vector", lambda v, b=b: v.scalar_tensor_tensor(
                srank[:, 64 * b:64 * b + 64], cmp_t[:, 64 * b:64 * b + 64], 1e6,
                srank[:, 64 * b:64 * b + 64], OP.mult, OP.add), deps=[geb])
            mks.append(mk)
            srk.append(cl)

        MOFF = 10 * S_SEL
        mview = hT[:, MOFF:MOFF + 96 * 128].rearrange(
            "p (col c) -> p c col", c=96)
        mz = sch.op("vector", lambda v: v.memset(hT[:, MOFF:MOFF + 96 * 128], 0.0))
        mc = sch.op("vector", lambda v: v.tensor_copy(ms1[:, :], post[:, :]),
                    deps=[lds[1]])
        for kpow in (4096.0, 2048.0, 1024.0, 512.0, 256.0, 128.0, 64.0):
            mb = sch.op("vector", lambda v, kpow=kpow: v.tensor_scalar(
                ms2[:, :], ms1[:, :], kpow, None, OP.is_ge), deps=[mc])
            mc = sch.op("vector", lambda v, kpow=kpow: v.scalar_tensor_tensor(
                ms1[:, :], ms2[:, :], -kpow, ms1[:, :], OP.mult, OP.add),
                deps=[mb])
        m1 = sch.op("vector", lambda v: v.tensor_sub(
            ms2[:, :], post[:, :], ms1[:, :]), deps=[mc])
        m2 = sch.op("vector", lambda v: v.tensor_scalar_mul(
            ms2[:, :], ms2[:, :], 1.0 / 64.0), deps=[m1])
        m3 = sch.op("vector", lambda v: v.tensor_copy(
            mview[:, 0, 0:128], ms2[:, :]), deps=[m2, mz])
        m4_ = sch.op("vector", lambda v: v.tensor_copy(
            mview[:, 32, 0:128], ms1[:, :]), deps=[mc, mz])
        m5 = sch.op("vector", lambda v: v.tensor_copy(
            mview[:, 64, 0:128], sct[:, :]), deps=[lds[0], mz])
        mp = [m3, m4_, m5]
        prev_oh_mm = {}
        for b in range(2):
            for f in range(64):
                col = 64 * b + f
                g = b * 64 + f
                gb, gf = divmod(g - 2, 64)
                oho = sch.op("vector", lambda v, b=b, f=f, col=col: v.tensor_scalar(
                    oh[f % 2][:, :], siota[:, :], srank[:, col:col + 1], None,
                    OP.is_equal), deps=[srk[b], lds[5],
                                        prev_oh_mm.get((gb, gf))])
                last = None
                for half in range(2):
                    last = sch.op("tensor", lambda t, b=b, f=f, half=half, col=col:
                                  t.matmul(ps[2 * b + half][0:96, 0:512],
                                           hT[:, MOFF + col * 96:
                                              MOFF + (col + 1) * 96],
                                           oh[f % 2][:, 512 * half:512 * half + 512],
                                           start=(f == 0), stop=(f == 63)),
                                  deps=[oho] + mp + ([] if f > 0 else [prev_mm]))
                prev_oh_mm[(b, f)] = last
        ev_pl = []
        for b in range(2):
            for half in range(2):
                sl = slice(1024 * b + 512 * half, 1024 * b + 512 * half + 512)
                pb_ = ps[2 * b + half]
                dep0 = [prev_oh_mm[(b, 63)]]
                e1 = sch.op("vector", lambda v, sl=sl, pb_=pb_: v.tensor_scalar(
                    pos_s[0:1, sl], pb_[0:1, 0:512], 64.0, None, OP.mult),
                    deps=dep0)
                e2 = sch.op("vector", lambda v, sl=sl, pb_=pb_: v.tensor_add(
                    pos_s[0:1, sl], pos_s[0:1, sl], pb_[32:33, 0:512]), deps=[e1])
                e3 = sch.op("vector", lambda v, sl=sl, pb_=pb_: v.tensor_copy(
                    sc_r[0:1, sl], pb_[64:65, 0:512]), deps=[e2])
                ev_pl.append(e3)
        st_pl = sch.op("scalar", lambda s: s.dma_start(
            out=pl_out[0:1, :], in_=pos_s[:, :]), deps=ev_pl, dma=True)
        st_pl2 = sch.op("scalar", lambda s: s.dma_start(
            out=pl_out[1:2, :], in_=sc_r[:, :]), deps=ev_pl, dma=True)

        pb_ev = []
        pbm_prev = None
        for j in range(4):
            pbm = sch.op("tensor", lambda t, j=j: t.matmul(
                ps[4][:, :], ones_row[:, :], pos_s[0:1, j * QT:(j + 1) * QT],
                start=True, stop=True), deps=ev_pl + [pbm_prev])
            pbe = sch.op("vector", lambda v, j=j: v.tensor_copy(
                posB[:, j * QT:(j + 1) * QT], ps[4][:, :]), deps=[pbm])
            pbm_prev = pbe
            pb_ev.append(pbe)
        trp = []
        trp_prev = None
        for c16 in range(16):
            tro = sch.op("tensor", lambda t, c16=c16: t.transpose(
                ps[5][:, 0:128], posB[:, c16 * 128:(c16 + 1) * 128], ident[:, :]),
                deps=[pb_ev[c16 // 4], trp_prev])
            tre = sch.op("vector", lambda v, c16=c16: v.tensor_copy(
                posT[:, c16:c16 + 1], ps[5][:, 0:1]), deps=[tro])
            trp_prev = tre
            trp.append(tre)
        adj = sch.op("vector", lambda v: v.tensor_scalar(
            posT[:, 8:16], posT[:, 8:16], float(T), None, OP.add), deps=trp)
        toi = sch.op("vector", lambda v: v.tensor_copy(idx_u[:, :], posT[:, :]),
                     deps=[adj])

        g_sel_st = []
        g_bf_st = []
        rst_st = []
        gpo = []
        prev_sq = None
        for c16 in range(16):
            gth = sch.op("gpsimd", lambda g, c16=c16: g.indirect_dma_start(
                out=gx[c16 % 2][:, :], out_offset=None, in_=x1f,
                in_offset=IndirectOffsetOnAxis(idx_u[:, c16:c16 + 1], 0)),
                deps=[toi, gpo[-2] if len(gpo) >= 2 else None], dma=True)
            st1_ = sch.op("scalar", lambda s, c16=c16: s.dma_start(
                out=selx_out[c16 * 128:(c16 + 1) * 128, :], in_=gx[c16 % 2][:, :]),
                deps=[gth], dma=True)
            g_sel_st.append(st1_)
            sq = sch.op("scalar", lambda s, c16=c16: s.activation(
                out=tmpT[0][:, :], in_=gx[c16 % 2][:, :], func=AF.Square,
                accum_out=rst[:, c16:c16 + 1]), deps=[gth, prev_sq])
            sq_last = sq
            ms_ = sch.op("vector", lambda v, c16=c16: v.tensor_scalar(
                rst[:, 16 + c16:17 + c16], rst[:, c16:c16 + 1], 1.0 / D, None,
                OP.mult), deps=[sq])
            lnv = sch.op("scalar", lambda s, c16=c16: s.activation(
                out=rst[:, 32 + c16:33 + c16], in_=rst[:, 16 + c16:17 + c16],
                func=AF.Ln, bias=epsR[:, 0:1]), deps=[ms_, ms_eps])
            rsd = sch.op("scalar", lambda s, c16=c16: s.activation(
                out=rst[:, 48 + c16:49 + c16], in_=rst[:, 32 + c16:33 + c16],
                func=AF.Exp, scale=-0.5), deps=[lnv])
            prev_sq = rsd
            rstst = sch.op("scalar", lambda s, c16=c16: s.dma_start(
                out=rstdd.ap()[0:1, c16 * 128:(c16 + 1) * 128],
                in_=rst[:, 48 + c16:49 + c16]),
                deps=[rsd], dma=True)
            rst_st.append(rstst)
            stbf = sch.op("scalar", lambda s, c16=c16: s.dma_start(
                out=selbf.ap()[c16 * 128:(c16 + 1) * 128, :], in_=gx[c16 % 2][:, :]),
                deps=[gth], dma=True)
            g_bf_st.append(stbf)
            gpo.append(sch.op("vector", lambda v, c16=c16: v.engine_nop(),
                              deps=[st1_, sq, stbf]))
        ld_rr = sch.op("sync", lambda q: q.dma_start(out=rstrow[:, :], in_=rstdd.ap()),
                       deps=rst_st, dma=True)
        rb_ev = []
        rbm_prev = None
        for j in range(4):
            rbm = sch.op("tensor", lambda t, j=j: t.matmul(
                ps[6][:, :], ones_row[:, :], rstrow[0:1, j * QT:(j + 1) * QT],
                start=True, stop=True), deps=[ld_rr, rbm_prev])
            rbe = sch.op("vector", lambda v, j=j: v.tensor_copy(
                rstdB[:, j * QT:(j + 1) * QT], ps[6][:, :]), deps=[rbm])
            rbm_prev = rbe
            rb_ev.append(rbe)

        hT_done = []
        for dc in range(DC):
            trd = sch.op("sync", lambda q, dc=dc: q.dma_start_transpose(
                tmpT[dc % 2][:, :], selbf.ap()[:, dc * 128:(dc + 1) * 128]),
                deps=[g_bf_st[-1], sq_last,
                      hT_done[-2] if len(hT_done) >= 2 else None],
                dma=True)
            sc_ = sch.op("vector", lambda v, dc=dc: v.scalar_tensor_tensor(
                hT[:, dc * S_SEL:(dc + 1) * S_SEL], tmpT[dc % 2][:, :],
                ln1c[:, dc:dc + 1], rstdB[:, :], OP.mult, OP.mult),
                deps=[trd, ld_ln1] + rb_ev
                + ([prev_oh_mm[(1, 63)]] if dc >= 10 else []))
            hT_done.append(sc_)

        bvm = sch.op("tensor", lambda t: t.matmul(
            ps[7][:, 0:128], ones_row[:, :], bv_row[0:1, :], start=True, stop=True),
            deps=[lds[12], lds[3]])
        bve = sch.op("vector", lambda v: v.tensor_copy(bvB[:, :], ps[7][:, 0:128]),
                     deps=[bvm])
        qkv_ev = []
        for head in range(2):
            for stile in range(4):
                bank = head * 4 + stile
                last = None
                for dc in range(DC):
                    last = sch.op("tensor", lambda t, head=head, stile=stile, dc=dc,
                                  bank=bank:
                                  t.matmul(ps[bank][:, :],
                                           wq16[:, (dc * 2 + head) * E:
                                                (dc * 2 + head + 1) * E],
                                           hT[:, dc * S_SEL + stile * QT:
                                              dc * S_SEL + (stile + 1) * QT],
                                           start=(dc == 0), stop=(dc == DC - 1)),
                                  deps=[hT_done[dc]] + ([ld_wq, bve] if dc == 0 else []))
                ev = sch.op("vector", lambda v, head=head, stile=stile, bank=bank:
                            v.tensor_scalar_add(
                                qT[head][:, stile * QT:(stile + 1) * QT],
                                ps[bank][:, :], bq_s[:, head:head + 1]),
                            deps=[last, lds[10]])
                qkv_ev.append(ev)
        for stile in range(4):
            bank = stile
            last = None
            for dc in range(DC):
                last = sch.op("tensor", lambda t, stile=stile, dc=dc, bank=bank:
                              t.matmul(ps[bank][:, :],
                                       wk16[:, dc * E:(dc + 1) * E],
                                       hT[:, dc * S_SEL + stile * QT:
                                          dc * S_SEL + (stile + 1) * QT],
                                       start=(dc == 0), stop=(dc == DC - 1)),
                              deps=[qkv_ev[stile], ld_wk] if dc == 0 else [])
            ev = sch.op("vector", lambda v, stile=stile, bank=bank:
                        v.tensor_scalar_add(kT[:, stile * QT:(stile + 1) * QT],
                                            ps[bank][:, :], bk_s[:, 0:1]),
                        deps=[last, lds[11]])
            qkv_ev.append(ev)
        v_ev = []
        for sc16 in range(16):
            bank = 4 + sc16 % 2
            last = None
            for dc in range(DC):
                last = sch.op("tensor", lambda t, sc16=sc16, dc=dc, bank=bank:
                              t.matmul(ps[bank][:, 0:128],
                                       hT[:, dc * S_SEL + sc16 * 128:
                                          dc * S_SEL + (sc16 + 1) * 128],
                                       wv16[:, dc * E:(dc + 1) * E],
                                       start=(dc == 0), stop=(dc == DC - 1)),
                              deps=([ld_wv, qkv_ev[4 + sc16 % 2],
                                     v_ev[sc16 - 2] if sc16 >= 2 else None]
                                    if dc == 0 else []))
            ev = sch.op("vector", lambda v, sc16=sc16, bank=bank:
                        v.tensor_add(v16[:, sc16 * 128:(sc16 + 1) * 128],
                                     ps[bank][:, 0:128], bvB[:, :]),
                        deps=[last, bve])
            v_ev.append(ev)

        INV2PI = float(1.0 / (2 * np.pi))
        C1 = 6.28125
        C2 = float(2 * np.pi - 6.28125)
        itmp = ee(nc.sbuf_tensor("itmp", [128, QT], mybir.dt.int32))
        ang2 = ee(nc.sbuf_tensor("ang2", [128, QT], F32))

        def build_trig(out_tile, base_op, shift, stile):
            if shift != 0.0:
                a0 = sch.op("vector", lambda v: v.tensor_scalar(
                    ang2[:, :], angB[:, :], shift, None, OP.add), deps=[base_op])
                srct = ang2
            else:
                a0 = base_op
                srct = angB
            n0 = sch.op("vector", lambda v, srct=srct: v.tensor_scalar(
                redu[:, :], srct[:, :], INV2PI, None, OP.mult), deps=[a0])
            n1 = sch.op("vector", lambda v: v.tensor_copy(itmp[:, :], redu[:, :]),
                        deps=[n0])
            n2 = sch.op("vector", lambda v: v.tensor_copy(redu[:, :], itmp[:, :]),
                        deps=[n1])
            r1 = sch.op("vector", lambda v, srct=srct: v.scalar_tensor_tensor(
                out_tile[:, :], redu[:, :], -C1, srct[:, :], OP.mult, OP.add),
                deps=[n2])
            r2 = sch.op("vector", lambda v: v.scalar_tensor_tensor(
                out_tile[:, :], redu[:, :], -C2, out_tile[:, :], OP.mult, OP.add),
                deps=[r1])
            f1 = sch.op("vector", lambda v: v.tensor_scalar(
                redu[:, :], out_tile[:, :], float(np.pi), None, OP.is_gt), deps=[r2])
            f2 = sch.op("vector", lambda v: v.scalar_tensor_tensor(
                out_tile[:, :], redu[:, :], float(-2 * np.pi), out_tile[:, :],
                OP.mult, OP.add), deps=[f1])
            so = sch.op("scalar", lambda s: s.activation(
                out=out_tile[:, :], in_=out_tile[:, :], func=AF.Sin), deps=[f2])
            return so

        rope_done = []
        rp_prev = None
        for stile in range(4):
            ango = sch.op("vector", lambda v, stile=stile: v.tensor_scalar(
                angB[:, :], posB[:, stile * QT:(stile + 1) * QT], invf[:, 0:1],
                None, OP.mult), deps=pb_ev + [lds[7], rp_prev])
            sin_done = build_trig(sinB, ango, 0.0, stile)
            cos_done = build_trig(cosB, ango, float(np.pi / 2), stile)
            for tile in (qT[0], qT[1], kT):
                rmm = sch.op("tensor", lambda t, tile=tile, stile=stile: t.matmul(
                    ps[6][:, :], rott[:, :], tile[:, stile * QT:(stile + 1) * QT],
                    start=True, stop=True),
                    deps=[lds[6], sin_done, cos_done, rp_prev] + qkv_ev)
                t1 = sch.op("vector", lambda v, tile=tile, stile=stile: v.tensor_mul(
                    ang2[:, :], tile[:, stile * QT:(stile + 1) * QT], cosB[:, :]),
                    deps=[sin_done, cos_done, rp_prev])
                t2 = sch.op("vector", lambda v: v.tensor_mul(
                    redu[:, :], ps[6][:, :], sinB[:, :]), deps=[rmm, t1])
                t3 = sch.op("vector", lambda v, tile=tile, stile=stile: v.tensor_add(
                    tile[:, stile * QT:(stile + 1) * QT], ang2[:, :], redu[:, :]),
                    deps=[t2])
                rp_prev = t3
                rope_done.append(t3)

        red_ops = {}
        for jj, tile in enumerate((qT[0], kT, qT[1])):
            for st4 in range(4):
                u = jj * 4 + st4
                sqo = sch.op("scalar", lambda s, tile=tile, st4=st4, u=u:
                             s.activation(out=sq_t[u % 2][:, :],
                                          in_=tile[:, st4 * QT:(st4 + 1) * QT],
                                          func=AF.Square),
                             deps=rope_done + [red_ops.get(u - 2)])
                mmo = sch.op("tensor", lambda t, u=u: t.matmul(
                    ps[u % 4][0:1, 0:QT], ones16c[:, :], sq_t[u % 2][:, :],
                    start=True, stop=True),
                    deps=[sqo, ms_ones16, red_ops.get(u - 4),
                          v_ev[-1] if u == 0 else None])
                rd = sch.op("vector", lambda v, u=u: v.reduce_max(
                    ctile[0:1, u % 4:u % 4 + 1], ps[u % 4][0:1, 0:QT], AX.X),
                    deps=[mmo])
                slot = jj
                if st4 == 0:
                    acc = sch.op("vector", lambda v, slot=slot, u=u: v.tensor_copy(
                        cm[0:1, slot:slot + 1], ctile[0:1, u % 4:u % 4 + 1]),
                        deps=[rd])
                else:
                    acc = sch.op("vector", lambda v, slot=slot, u=u: v.tensor_max(
                        cm[0:1, slot:slot + 1], cm[0:1, slot:slot + 1],
                        ctile[0:1, u % 4:u % 4 + 1]), deps=[rd])
                red_ops[u] = acc
        cb_ready = []
        cb_prev = None
        for qh in range(2):
            pr = sch.op("vector", lambda v, qh=qh: v.tensor_mul(
                ctile[0:1, 4 + qh:5 + qh], cm[0:1, 2 * qh:2 * qh + 1],
                cm[0:1, 1:2]), deps=[red_ops[11], cb_prev])
            ln_ = sch.op("scalar", lambda s, qh=qh: s.activation(
                out=ctile[0:1, 4 + qh:5 + qh], in_=ctile[0:1, 4 + qh:5 + qh],
                func=AF.Ln), deps=[pr])
            ex_ = sch.op("scalar", lambda s, qh=qh: s.activation(
                out=ctile[0:1, 4 + qh:5 + qh], in_=ctile[0:1, 4 + qh:5 + qh],
                func=AF.Exp, bias=lnbias[0:1, 0:1], scale=0.5), deps=[ln_, ms_lnb])
            ng = sch.op("vector", lambda v, qh=qh: v.tensor_scalar(
                ctile[0:1, 4 + qh:5 + qh], ctile[0:1, 4 + qh:5 + qh],
                -1.0, 8.0, OP.mult, OP.add), deps=[ex_])
            bc = sch.op("tensor", lambda t, qh=qh: t.matmul(
                ps[4 + qh][:, 0:1], ones_row[:, :], ctile[0:1, 4 + qh:5 + qh],
                start=True, stop=True), deps=[ng, v_ev[-1]])
            cb = sch.op("vector", lambda v, qh=qh: v.tensor_copy(
                cbias[:, qh:qh + 1], ps[4 + qh][:, 0:1]), deps=[bc])
            cb_ready.append(cb)
            cb_prev = cb


        zstores = []
        exp_g = {}
        pv_g = {}
        zn_g = {}
        rc_g = {}
        gci = 0
        gqi = 0
        lb_prev = None
        lc_prev = None
        for qh in range(2):
            for qsi in range(4):
                nk = 4 * qsi
                for kc in range(nk + 4):
                    mslot = kc - nk if kc >= nk else None
                    first = kc == 0
                    last_c = kc == nk + 3
                    sb = ps[gci % 2]
                    zb = ps[2 + gqi % 2]
                    lbank = ps[4 + gqi % 2]
                    d_ = []
                    if gci == 0:
                        d_ += [v_ev[-1]] + cb_ready
                    if gci >= 2:
                        d_.append(exp_g[gci - 2])
                    else:
                        d_ += [red_ops[11]]
                    if first:
                        if gqi >= 2:
                            d_.append(zn_g[gqi - 2])
                            d_.append(rc_g[gqi - 2])
                        else:
                            d_ += cb_ready
                    sc_ = sch.op("tensor", lambda t, sb=sb, qh=qh, kc=kc, qsi=qsi:
                                 t.matmul(sb[:, :], kT[:, kc * 128:(kc + 1) * 128],
                                          qT[qh][:, qsi * QT:(qsi + 1) * QT],
                                          start=True, stop=True), deps=d_)
                    ed = [sc_, cb_ready[qh]]
                    if gci >= 2:
                        ed.append(pv_g[gci - 2])
                    ex = sch.op("scalar", lambda s, sb=sb, gci=gci, qh=qh:
                                s.activation(out=pT[gci % 2][:, :], in_=sb[:, :],
                                             func=AF.Exp, bias=cbias[:, qh:qh + 1],
                                             scale=INV_SCALE), deps=ed)
                    exp_g[gci] = ex
                    pvd = [ex]
                    if mslot is not None:
                        mko = sch.op("vector", lambda v, gci=gci, mslot=mslot:
                                     v.tensor_mul(pT[gci % 2][:, :], pT[gci % 2][:, :],
                                                  mask_s[:, mslot * QT:(mslot + 1) * QT]),
                                     deps=[ex, ld_mask])
                        pvd = [mko]
                    sch.op("tensor", lambda t, zb=zb, kc=kc, gci=gci, first=first,
                           last_c=last_c:
                           t.matmul(zb[:, :], v16[:, kc * 128:(kc + 1) * 128],
                                    pT[gci % 2][:, :], start=first, stop=last_c),
                           deps=pvd)
                    pv = sch.op("tensor", lambda t, lbank=lbank, gci=gci, first=first,
                                last_c=last_c:
                                t.matmul(lbank[0:1, 0:QT], ones16c[:, :],
                                         pT[gci % 2][:, :], start=first, stop=last_c))
                    pv_g[gci] = pv
                    gci += 1
                    if last_c:
                        rcd = [pv, lb_prev]
                        rc = sch.op("vector", lambda v, lbank=lbank: v.reciprocal(
                            lrec[:, :], lbank[0:1, 0:QT]), deps=rcd)
                        rc_g[gqi] = rc
                        lb = sch.op("tensor", lambda t: t.matmul(
                            ps[6][:, :], ones_row[:, :], lrec[:, :],
                            start=True, stop=True), deps=[rc, lc_prev])
                        lb_prev = lb
                        lc = sch.op("vector", lambda v: v.tensor_copy(
                            lrecB[:, :], ps[6][:, :]), deps=[lb])
                        lc_prev = lc
                        znd = [lc]
                        if len(zstores) >= 2:
                            znd.append(zstores[-2])
                        zn = sch.op("vector", lambda v, gqi=gqi, zb=zb: v.tensor_mul(
                            zn_s[gqi % 2][:, :], zb[:, :], lrecB[:, :]), deps=znd)
                        zn_g[gqi] = zn
                        st = sch.op("scalar", lambda s, qh=qh, qsi=qsi, gqi=gqi:
                                    s.dma_start(
                                        out=zsel_out[qh * E:(qh + 1) * E,
                                                     qsi * QT:(qsi + 1) * QT],
                                        in_=zn_s[gqi % 2][:, :]), deps=[zn], dma=True)
                        zstores.append(st)
                        gqi += 1
        sch.op("sync", lambda q: q.nop(),
               deps=zstores + [st_pl, st_pl2] + g_sel_st)
        sch.build(nc, ctx, block)
    return nc


def build_l4a(paranoid=True, serialize_dma=False):
    nc = bass.Bass("TRN2", target_bir_lowering=False, num_devices=NCORES)
    z_in = nc.dram_tensor("z16s", [H, E, 256], F16, kind="ExternalInput").ap()
    wo_in = nc.dram_tensor("wo_f", [D, D], F32, kind="ExternalInput").ap()
    sx_in = nc.dram_tensor("selx16", [256, D], F16, kind="ExternalInput").ap()
    ln2_in = nc.dram_tensor("ln2c", [128, 16], F32, kind="ExternalInput").ap()
    ones_in = nc.dram_tensor("ones128", [1, 128], F32, kind="ExternalInput").ap()
    h2_out = nc.dram_tensor("h2t", [D, 256], F16, kind="ExternalOutput").ap()
    r2_out = nc.dram_tensor("r2t", [D, 256], F32, kind="ExternalOutput").ap()

    sch = Sched(paranoid=paranoid, serialize_dma=serialize_dma)
    with ExitStack() as ctx:
        ee = ctx.enter_context
        z_s = ee(nc.sbuf_tensor("z_s", [128, H * 256], F16))
        wo16 = ee(nc.sbuf_tensor("wo16", [128, H * D], F16))
        sxT = ee(nc.sbuf_tensor("sxT", [128, DC * 256], F16))
        r2T = ee(nc.sbuf_tensor("r2T", [128, DC * 256], F32))
        h2T = ee(nc.sbuf_tensor("h2T", [128, DC * 256], F16))
        ln2c = ee(nc.sbuf_tensor("ln2c_s", [128, 16], F32))
        ones_row = ee(nc.sbuf_tensor("ones_row", [1, 128], F32))
        onesf_col = ee(nc.sbuf_tensor("onesf_col", [128, 1], F32))
        sq32 = ee(nc.sbuf_tensor("sq32", [128, 256], F32))
        msrow = ee(nc.sbuf_tensor("msrow", [1, 256], F32))
        rstdB = ee(nc.sbuf_tensor("rstdB", [128, 256], F32))
        epsR = ee(nc.sbuf_tensor("epsR", [1, 1], F32))
        ps = [ee(nc.psum_tensor(f"ps{i}", [128, QT], F32)) for i in range(6)]
        block = ee(nc.Block())

        ld_z = sch.op("gpsimd", lambda g: g.dma_start(
            out=z_s[:, :].rearrange("p (h s) -> p h s", h=H),
            in_=z_in.rearrange("h p s -> p h s")), dma=True)
        ld_wo = [sch.op("gpsimd", lambda g, hc=hc: g.dma_start(
            out=wo16[:, hc * D:(hc + 1) * D],
            in_=wo_in[hc * 128:(hc + 1) * 128, :]), dma=True) for hc in range(H)]
        ld_ln = sch.op("gpsimd", lambda g: g.dma_start(out=ln2c[:, :], in_=ln2_in),
                       dma=True)
        ld_on = sch.op("gpsimd", lambda g: g.dma_start(out=ones_row[:, :], in_=ones_in),
                       dma=True)
        ms_of = sch.op("vector", lambda v: v.memset(onesf_col[:, :], 1.0))
        ms_ep = sch.op("vector", lambda v: v.memset(epsR[:, :], EPS_RMS))

        sxT_ev = {}
        for dc in range(DC):
            tr = sch.op("sync", lambda q, dc=dc: q.dma_start_transpose(
                sxT[:, dc * 256:(dc + 1) * 256],
                sx_in[:, dc * 128:(dc + 1) * 128]), dma=True)
            sxT_ev[dc] = tr
        sxT_done = sxT_ev[DC - 1]

        r2_ev = {}
        st_r2 = []
        for dc in range(DC):
            bank = dc % 2
            last = None
            for h in range(H):
                d_ = []
                if h == 0:
                    d_ = [ld_z] + ld_wo
                    if dc >= 2:
                        d_.append(r2_ev[dc - 2])
                last = sch.op("tensor", lambda t, h=h, dc=dc, bank=bank: t.matmul(
                    ps[bank][:, 0:256],
                    wo16[:, h * D + dc * 128:h * D + (dc + 1) * 128],
                    z_s[:, h * 256:(h + 1) * 256],
                    start=(h == 0), stop=(h == H - 1)), deps=d_)
            ev = sch.op("vector", lambda v, dc=dc, bank=bank: v.tensor_add(
                r2T[:, dc * 256:(dc + 1) * 256], ps[bank][:, 0:256],
                sxT[:, dc * 256:(dc + 1) * 256]), deps=[last, sxT_ev[dc]])
            r2_ev[dc] = ev
            st = sch.op("sync", lambda q, dc=dc: q.dma_start(
                out=r2_out[dc * 128:(dc + 1) * 128, :],
                in_=r2T[:, dc * 256:(dc + 1) * 256]), deps=[ev], dma=True)
            st_r2.append(st)
        mm_prev = None
        for dc in range(DC):
            sq = sch.op("scalar", lambda s, dc=dc: s.activation(
                out=sq32[:, :], in_=r2T[:, dc * 256:(dc + 1) * 256],
                func=AF.Square), deps=[r2_ev[dc], mm_prev])
            mm = sch.op("tensor", lambda t, dc=dc: t.matmul(
                ps[4][0:1, 0:256], onesf_col[:, :], sq32[:, :],
                start=(dc == 0), stop=(dc == DC - 1)), deps=[sq, ms_of])
            mm_prev = mm
        cst1 = sch.op("vector", lambda v: v.tensor_scalar_mul(
            msrow[0:1, :], ps[4][0:1, 0:256], 1.0 / D), deps=[mm_prev])
        lnv = sch.op("scalar", lambda s: s.activation(
            out=msrow[0:1, :], in_=msrow[0:1, :], func=AF.Ln,
            bias=epsR[0:1, 0:1]), deps=[cst1, ms_ep])
        rsd = sch.op("scalar", lambda s: s.activation(
            out=msrow[0:1, :], in_=msrow[0:1, :], func=AF.Exp, scale=-0.5),
            deps=[lnv])
        bcm = sch.op("tensor", lambda t: t.matmul(
            ps[5][:, 0:256], ones_row[:, :], msrow[0:1, :], start=True, stop=True),
            deps=[rsd, ld_on])
        bce = sch.op("vector", lambda v: v.tensor_copy(rstdB[:, :], ps[5][:, 0:256]),
                     deps=[bcm])
        st_h2 = []
        for dc in range(DC):
            h2 = sch.op("vector", lambda v, dc=dc: v.scalar_tensor_tensor(
                h2T[:, dc * 256:(dc + 1) * 256], r2T[:, dc * 256:(dc + 1) * 256],
                ln2c[:, dc:dc + 1], rstdB[:, :], OP.mult, OP.mult),
                deps=[bce, ld_ln, r2_ev[dc]])
            st = sch.op("sync", lambda q, dc=dc: q.dma_start(
                out=h2_out[dc * 128:(dc + 1) * 128, :],
                in_=h2T[:, dc * 256:(dc + 1) * 256]), deps=[h2], dma=True)
            st_h2.append(st)
        sch.op("sync", lambda q: q.nop(), deps=st_r2 + st_h2)
        sch.build(nc, ctx, block)
    return nc


def build_l4b(paranoid=True, serialize_dma=False):
    nc = bass.Bass("TRN2", target_bir_lowering=False, num_devices=NCORES)
    h2_in = nc.dram_tensor("h2tf", [D, S_SEL], F16, kind="ExternalInput").ap()
    wg_in = nc.dram_tensor("wg_sl", [D, 1024], F32, kind="ExternalInput").ap()
    wu_in = nc.dram_tensor("wu_sl", [D, 1024], F32, kind="ExternalInput").ap()
    at_out = nc.dram_tensor("at", [1024, S_SEL], F16, kind="ExternalOutput").ap()

    NIC = 8
    sch = Sched(paranoid=paranoid, serialize_dma=serialize_dma)
    with ExitStack() as ctx:
        ee = ctx.enter_context
        h2_s = ee(nc.sbuf_tensor("h2_s", [128, DC * S_SEL], F16))
        wg16 = ee(nc.sbuf_tensor("wg16", [128, DC * 1024], F16))
        wu16 = ee(nc.sbuf_tensor("wu16", [128, DC * 1024], F16))
        sil = ee(nc.sbuf_tensor("sil", [128, QT], F16))
        a_s = [ee(nc.sbuf_tensor(f"a_s{i}", [128, QT], F16)) for i in range(2)]
        ps = [ee(nc.psum_tensor(f"ps{i}", [128, QT], F32)) for i in range(4)]
        block = ee(nc.Block())

        ld_h2 = sch.op("gpsimd", lambda g: g.dma_start(
            out=h2_s[:, :].rearrange("p (dc s) -> p dc s", dc=DC),
            in_=h2_in.rearrange("(dc p) s -> p dc s", p=128)), dma=True)
        ld_wg = sch.op("gpsimd", lambda g: g.dma_start(
            out=wg16[:, :].rearrange("p (dc i) -> p dc i", dc=DC),
            in_=wg_in.rearrange("(dc p) i -> p dc i", p=128)), dma=True)
        ld_wu = sch.op("gpsimd", lambda g: g.dma_start(
            out=wu16[:, :].rearrange("p (dc i) -> p dc i", dc=DC),
            in_=wu_in.rearrange("(dc p) i -> p dc i", p=128)), dma=True)

        a_st = []
        sil_prev = None
        aprev = {}
        for ic in range(NIC):
            for st4 in range(4):
                u = ic * 4 + st4
                gb = (u % 2) * 2
                lastg = None
                lastu = None
                for dc in range(DC):
                    d_ = []
                    if dc == 0:
                        d_ = [ld_h2, ld_wg, ld_wu]
                        if u >= 2:
                            d_.append(aprev[u - 2])
                    lastg = sch.op("tensor", lambda t, ic=ic, st4=st4, dc=dc, gb=gb:
                                   t.matmul(ps[gb][:, :],
                                            wg16[:, dc * 1024 + ic * 128:
                                                 dc * 1024 + (ic + 1) * 128],
                                            h2_s[:, dc * S_SEL + st4 * QT:
                                                 dc * S_SEL + (st4 + 1) * QT],
                                            start=(dc == 0), stop=(dc == DC - 1)),
                                   deps=d_)
                    lastu = sch.op("tensor", lambda t, ic=ic, st4=st4, dc=dc, gb=gb:
                                   t.matmul(ps[gb + 1][:, :],
                                            wu16[:, dc * 1024 + ic * 128:
                                                 dc * 1024 + (ic + 1) * 128],
                                            h2_s[:, dc * S_SEL + st4 * QT:
                                                 dc * S_SEL + (st4 + 1) * QT],
                                            start=(dc == 0), stop=(dc == DC - 1)))
                so = sch.op("scalar", lambda s, gb=gb: s.activation(
                    out=sil[:, :], in_=ps[gb][:, :], func=AF.Silu),
                    deps=[lastg, sil_prev])
                ao = sch.op("vector", lambda v, u=u, gb=gb: v.tensor_mul(
                    a_s[u % 2][:, :], sil[:, :], ps[gb + 1][:, :]),
                    deps=[so, lastu, a_st[-2] if len(a_st) >= 2 else None])
                sil_prev = ao
                aprev[u] = ao
                st = sch.op("sync", lambda q, ic=ic, st4=st4, u=u: q.dma_start(
                    out=at_out[ic * 128:(ic + 1) * 128, st4 * QT:(st4 + 1) * QT],
                    in_=a_s[u % 2][:, :]), deps=[ao], dma=True)
                a_st.append(st)
        sch.op("sync", lambda q: q.nop(), deps=a_st)
        sch.build(nc, ctx, block)
    return nc


def build_l4c(paranoid=True, serialize_dma=False):
    nc = bass.Bass("TRN2", target_bir_lowering=False, num_devices=NCORES)
    at_in = nc.dram_tensor("atf", [I_FF, S_SEL], F16, kind="ExternalInput").ap()
    wd_in = nc.dram_tensor("wd_sl", [I_FF, 256], F32, kind="ExternalInput").ap()
    r2_in = nc.dram_tensor("r2sl", [S_SEL, 256], F32, kind="ExternalInput").ap()
    sx_in = nc.dram_tensor("sx_sl", [S_SEL, 256], F16, kind="ExternalInput").ap()
    sc_in = nc.dram_tensor("sct16", [128, 16], F32, kind="ExternalInput").ap()
    g_out = nc.dram_tensor("gated", [S_SEL, 256], F32, kind="ExternalOutput").ap()

    NI = I_FF // 128
    sch = Sched(paranoid=paranoid, serialize_dma=serialize_dma)
    with ExitStack() as ctx:
        ee = ctx.enter_context
        wd16 = ee(nc.sbuf_tensor("wd16", [128, NI * 256], F16))
        a_t = [ee(nc.sbuf_tensor(f"a_t{i}", [128, NI * 512], F16)) for i in range(2)]
        r2_t = [ee(nc.sbuf_tensor(f"r2_t{i}", [128, 256], F32)) for i in range(2)]
        sx_t = [ee(nc.sbuf_tensor(f"sx_t{i}", [128, 256], F16)) for i in range(2)]
        go_t = [ee(nc.sbuf_tensor(f"go_t{i}", [128, 256], F32)) for i in range(2)]
        tsc = ee(nc.sbuf_tensor("tsc", [128, 256], F32))
        g_s = ee(nc.sbuf_tensor("g_s", [128, 16], F32))
        ps = [ee(nc.psum_tensor(f"ps{i}", [128, 256], F32)) for i in range(2)]
        block = ee(nc.Block())

        ld_wd = sch.op("gpsimd", lambda g: g.dma_start(
            out=wd16[:, :].rearrange("p (ic d) -> p ic d", ic=NI),
            in_=wd_in.rearrange("(ic p) d -> p ic d", p=128)), dma=True)
        ld_sc = sch.op("gpsimd", lambda g: g.dma_start(out=g_s[:, :], in_=sc_in),
                       dma=True)
        sig = sch.op("scalar", lambda s: s.activation(
            out=g_s[:, :], in_=g_s[:, :], func=AF.Sigmoid), deps=[ld_sc])

        g_st = []
        ev_prev = {}
        lda = {}
        for sc4 in range(4):
            lda[sc4] = sch.op("sync", lambda q, sc4=sc4: q.dma_start(
                out=a_t[sc4 % 2][:, :].rearrange("p (ic s) -> p ic s", ic=NI),
                in_=at_in.rearrange("(ic p) s -> p ic s", p=128)[
                    :, :, sc4 * 512:(sc4 + 1) * 512]),
                deps=[ev_prev.get(4 * (sc4 - 2) + 3)], dma=True)
            for ss in range(4):
                sc16 = sc4 * 4 + ss
                ldr = sch.op("sync", lambda q, sc16=sc16: q.dma_start(
                    out=r2_t[sc16 % 2][:, :],
                    in_=r2_in[sc16 * 128:(sc16 + 1) * 128, :]),
                    deps=[ev_prev.get(sc16 - 2)], dma=True)
                lds_ = sch.op("sync", lambda q, sc16=sc16: q.dma_start(
                    out=sx_t[sc16 % 2][:, :],
                    in_=sx_in[sc16 * 128:(sc16 + 1) * 128, :]),
                    deps=[ev_prev.get(sc16 - 2)], dma=True)
                last = None
                for ic in range(NI):
                    d_ = []
                    if ic == 0:
                        d_ = [lda[sc4], ld_wd]
                        if sc16 >= 2:
                            d_.append(ev_prev[sc16 - 2])
                    last = sch.op("tensor", lambda t, sc16=sc16, ss=ss, ic=ic:
                                  t.matmul(
                                      ps[sc16 % 2][:, :],
                                      a_t[(sc16 // 4) % 2][:, ic * 512 + ss * 128:
                                           ic * 512 + (ss + 1) * 128],
                                      wd16[:, ic * 256:(ic + 1) * 256],
                                      start=(ic == 0), stop=(ic == NI - 1)),
                                  deps=d_)
                e1 = sch.op("vector", lambda v, sc16=sc16: v.tensor_add(
                    go_t[sc16 % 2][:, :], ps[sc16 % 2][:, :], r2_t[sc16 % 2][:, :]),
                    deps=[last, ldr, g_st[-2] if len(g_st) >= 2 else None])
                e2 = sch.op("vector", lambda v, sc16=sc16: v.tensor_sub(
                    tsc[:, :], go_t[sc16 % 2][:, :], sx_t[sc16 % 2][:, :]),
                    deps=[e1, lds_])
                e3 = sch.op("vector", lambda v, sc16=sc16: v.tensor_scalar(
                    go_t[sc16 % 2][:, :], tsc[:, :], g_s[:, sc16:sc16 + 1], None,
                    OP.mult), deps=[e2, sig])
                ev_prev[sc16] = e3
                st = sch.op("sync", lambda q, sc16=sc16: q.dma_start(
                    out=g_out[sc16 * 128:(sc16 + 1) * 128, :],
                    in_=go_t[sc16 % 2][:, :]), deps=[e3], dma=True)
                g_st.append(st)
        sch.op("sync", lambda q: q.nop(), deps=g_st)
        sch.build(nc, ctx, block)
    return nc


def build_l5(paranoid=True, serialize_dma=False):
    nc = bass.Bass("TRN2", target_bir_lowering=False, num_devices=NCORES)
    x1_in = nc.dram_tensor("x1_own", [2048, D], F32, kind="ExternalInput").ap()
    gf_in = nc.dram_tensor("gatedf", [S_SEL, D], F32, kind="ExternalInput").ap()
    pt_in = nc.dram_tensor("post16", [128, 16], F32, kind="ExternalInput").ap()
    sr_in = nc.dram_tensor("sreli", [128, 16], F32, kind="ExternalInput").ap()
    mb_in = nc.dram_tensor("mybase", [128, 1], F32, kind="ExternalInput").ap()
    out_o = nc.dram_tensor("out", [2048, D], F32, kind="ExternalOutput").ap()
    m_buf = nc.dram_tensor("m_buf", [2 * 2048, D], F32, kind="Internal")

    sch = Sched(paranoid=paranoid, serialize_dma=serialize_dma)
    with ExitStack() as ctx:
        ee = ctx.enter_context
        pt_s = ee(nc.sbuf_tensor("pt_s", [128, 16], F32))
        sr_s = ee(nc.sbuf_tensor("sr_s", [128, 16], F32))
        mb_s = ee(nc.sbuf_tensor("mb_s", [128, 1], F32))
        rel = ee(nc.sbuf_tensor("rel", [128, 16], F32))
        v1 = ee(nc.sbuf_tensor("v1", [128, 16], F32))
        v2 = ee(nc.sbuf_tensor("v2", [128, 16], F32))
        idx_u = ee(nc.sbuf_tensor("idx_u", [128, 16], U32))
        gx = [ee(nc.sbuf_tensor(f"gx{i}", [128, D], F32)) for i in range(2)]
        NB5 = 4
        xo = [ee(nc.sbuf_tensor(f"xo{i}", [128, D], F32)) for i in range(NB5)]
        mo = [ee(nc.sbuf_tensor(f"mo{i}", [128, D], F32)) for i in range(NB5)]
        r_t = ee(nc.sbuf_tensor("r_t", [128, D], F32))
        sqd = ee(nc.sbuf_tensor("sqd", [128, D], F32))
        st1 = ee(nc.sbuf_tensor("st1", [128, 8], F32))
        o_t = [ee(nc.sbuf_tensor(f"o_t{i}", [128, D], F32)) for i in range(NB5)]
        eps_c = ee(nc.sbuf_tensor("eps_c", [128, 1], F32))
        block = ee(nc.Block())

        ld1 = sch.op("gpsimd", lambda g: g.dma_start(out=pt_s[:, :], in_=pt_in), dma=True)
        ld2 = sch.op("gpsimd", lambda g: g.dma_start(out=sr_s[:, :], in_=sr_in), dma=True)
        ld3 = sch.op("gpsimd", lambda g: g.dma_start(out=mb_s[:, :], in_=mb_in), dma=True)
        ms_e = sch.op("vector", lambda v: v.memset(eps_c[:, :], EPS_LN))
        i1 = sch.op("vector", lambda v: v.tensor_scalar(
            rel[:, :], pt_s[:, :], mb_s[:, 0:1], None, OP.subtract),
            deps=[ld1, ld3])
        i2 = sch.op("vector", lambda v: v.tensor_scalar(
            rel[:, 8:16], rel[:, 8:16], 1024.0, None, OP.add), deps=[i1])
        i3 = sch.op("vector", lambda v: v.tensor_scalar(
            v1[:, 0:8], rel[:, 0:8], 0.0, None, OP.is_ge), deps=[i2])
        i4 = sch.op("vector", lambda v: v.tensor_scalar(
            v2[:, 0:8], rel[:, 0:8], 1024.0, None, OP.is_lt), deps=[i3])
        i5 = sch.op("vector", lambda v: v.tensor_scalar(
            v1[:, 8:16], rel[:, 8:16], 1024.0, None, OP.is_ge), deps=[i4])
        i6 = sch.op("vector", lambda v: v.tensor_scalar(
            v2[:, 8:16], rel[:, 8:16], 2048.0, None, OP.is_lt), deps=[i5])
        i7 = sch.op("vector", lambda v: v.tensor_mul(v1[:, :], v1[:, :], v2[:, :]),
                    deps=[i6])
        i8 = sch.op("vector", lambda v: v.tensor_mul(rel[:, :], rel[:, :], v1[:, :]),
                    deps=[i7])
        i9 = sch.op("vector", lambda v: v.tensor_scalar(
            v2[:, :], v1[:, :], -1.0, 1.0, OP.mult, OP.add), deps=[i8])
        i10 = sch.op("vector", lambda v: v.tensor_mul(v2[:, :], v2[:, :], sr_s[:, :]),
                     deps=[i9, ld2])
        i11 = sch.op("vector", lambda v: v.tensor_add(rel[:, :], rel[:, :], v2[:, :]),
                     deps=[i10])
        i12 = sch.op("vector", lambda v: v.tensor_copy(idx_u[:, :], rel[:, :]),
                     deps=[i11])
        zt = ee(nc.sbuf_tensor("zt", [128, D], F32))
        ms_z = sch.op("vector", lambda v: v.memset(zt[:, :], 0.0))
        cp_m = [sch.op("sync", lambda q, i=i: q.dma_start(
            out=m_buf.ap()[i * 128:(i + 1) * 128, :],
            in_=zt[:, :]), deps=[ms_z], dma=True) for i in range(16)]
        scat = []
        for sc16 in range(16):
            ldg = sch.op("sync", lambda q, sc16=sc16: q.dma_start(
                out=gx[sc16 % 2][:, :], in_=gf_in[sc16 * 128:(sc16 + 1) * 128, :]),
                deps=[scat[-2] if len(scat) >= 2 else None], dma=True)
            so = sch.op("gpsimd", lambda g, sc16=sc16: g.indirect_dma_start(
                out=m_buf.ap(), out_offset=IndirectOffsetOnAxis(
                    idx_u[:, sc16:sc16 + 1], 0),
                in_=gx[sc16 % 2][:, :], in_offset=None),
                deps=[ldg, i12] + cp_m, dma=True)
            scat.append(so)
        o_st = []
        ra_hist = []
        x1o_prev = None
        lnv_prev = None
        for tc in range(16):
            ldx = sch.op("sync", lambda q, tc=tc: q.dma_start(
                out=xo[tc % NB5][:, :], in_=x1_in[tc * 128:(tc + 1) * 128, :]),
                deps=[ra_hist[tc - NB5] if tc >= NB5 else None], dma=True)
            ldm = sch.op("scalar", lambda q, tc=tc: q.dma_start(
                out=mo[tc % NB5][:, :], in_=m_buf.ap()[tc * 128:(tc + 1) * 128, :]),
                deps=scat + [ra_hist[tc - NB5] if tc >= NB5 else None], dma=True)
            ra = sch.op("vector", lambda v, tc=tc: v.scalar_tensor_tensor(
                r_t[:, :], xo[tc % NB5][:, :], 2.0, mo[tc % NB5][:, :],
                OP.mult, OP.add),
                deps=[ldx, ldm, x1o_prev])
            ra_hist.append(ra)
            sm = sch.op("vector", lambda v: v.reduce_sum(
                st1[:, 0:1], r_t[:, :], axis=AX.X), deps=[ra])
            s2 = sch.op("scalar", lambda s: s.activation(
                out=sqd[:, :], in_=r_t[:, :], func=AF.Square,
                accum_out=st1[:, 1:2]), deps=[ra, x1o_prev])
            mean = sch.op("vector", lambda v: v.tensor_scalar_mul(
                st1[:, 2:3], st1[:, 0:1], 1.0 / D), deps=[sm])
            msq = sch.op("vector", lambda v: v.tensor_scalar_mul(
                st1[:, 3:4], st1[:, 1:2], 1.0 / D), deps=[s2])
            m2_ = sch.op("vector", lambda v: v.tensor_mul(
                st1[:, 4:5], st1[:, 2:3], st1[:, 2:3]), deps=[mean])
            var = sch.op("vector", lambda v: v.tensor_sub(
                st1[:, 5:6], st1[:, 3:4], st1[:, 4:5]), deps=[msq, m2_, lnv_prev])
            lnv = sch.op("scalar", lambda s: s.activation(
                out=st1[:, 6:7], in_=st1[:, 5:6], func=AF.Ln,
                bias=eps_c[:, 0:1]), deps=[var, ms_e])
            lnv_prev = lnv
            rstd = sch.op("scalar", lambda s: s.activation(
                out=st1[:, 7:8], in_=st1[:, 6:7], func=AF.Exp, scale=-0.5),
                deps=[lnv])
            oo = sch.op("vector", lambda v, tc=tc: v.tensor_scalar(
                o_t[tc % NB5][:, :], r_t[:, :], st1[:, 2:3], st1[:, 7:8],
                OP.subtract, OP.mult),
                deps=[rstd, o_st[-NB5] if len(o_st) >= NB5 else None])
            x1o_prev = oo
            st = sch.op("sync", lambda q, tc=tc: q.dma_start(
                out=out_o[tc * 128:(tc + 1) * 128, :], in_=o_t[tc % NB5][:, :]),
                deps=[oo], dma=True)
            o_st.append(st)
        sch.op("sync", lambda q: q.nop(), deps=o_st)
        sch.build(nc, ctx, block)
    return nc


_CACHE = {}


def _causal_masks_f16():
    m = np.zeros((4, 128, QT), dtype=np.float16)
    q = np.arange(QT)[None, :]
    k = np.arange(128)[:, None]
    for ci in range(4):
        m[ci] = ((128 * ci + k) <= q).astype(np.float16)
    return m


def _run_l1(inputs):
    xt = np.ascontiguousarray(np.transpose(inputs["L0_attn_x"], (0, 2, 1)))
    WQ, WK, WV = inputs["L0_attn_W_Q"], inputs["L0_attn_W_K"], inputs["L0_attn_W_V"]
    bQ, bK, bV = inputs["L0_attn_b_Q"], inputs["L0_attn_b_K"], inputs["L0_attn_b_V"]
    ones128 = np.ones((1, 128), np.float32)
    masks = _causal_masks_f16()
    in_maps = []
    for c in range(NCORES):
        b = c // 4
        hs = slice(4 * (c % 4), 4 * (c % 4) + HPC)
        in_maps.append({
            "x_t": np.ascontiguousarray(xt[b]),
            "wq": np.ascontiguousarray(WQ[hs]),
            "wk": np.ascontiguousarray(WK[hs]),
            "wv_mov": np.ascontiguousarray(
                np.transpose(WV[hs], (1, 0, 2)).reshape(D, HPC * E)),
            "bq_t": np.ascontiguousarray(bQ[hs].T),
            "bk_t": np.ascontiguousarray(bK[hs].T),
            "bv_mov": np.ascontiguousarray(bV[hs].reshape(1, HPC * E)),
            "ones128": ones128,
            "masks": masks,
        })
    if "l1" not in _CACHE:
        _CACHE["l1"] = build_l1()
    res = run_bass_kernel_spmd(_CACHE["l1"], in_maps, core_ids=list(range(NCORES)))
    return [res.results[c]["z"] for c in range(NCORES)]


def _run_l2(inputs, z_shards):
    x = inputs["L0_attn_x"]
    WO = inputs["L0_attn_W_O"]
    bO = inputs["L0_attn_b_O"]
    wrt = inputs["L0_ffn_w_router"]
    ones128 = np.ones((1, 128), np.float32)
    in_maps = []
    for c in range(NCORES):
        sl = slice(1024 * c, 1024 * c + 1024)
        z16 = np.empty((H, E, 2048), dtype=np.float16)
        for h in range(H):
            src, pair = h // 4, h % 4
            z16[h, :, 0:1024] = z_shards[src][pair * E:(pair + 1) * E, sl]
            z16[h, :, 1024:2048] = z_shards[4 + src][pair * E:(pair + 1) * E, sl]
        x_own = np.concatenate([x[0, sl], x[1, sl]], axis=0)
        in_maps.append({
            "z16": z16,
            "wo": np.ascontiguousarray(WO),
            "bo": bO.reshape(1, D),
            "x_own": np.ascontiguousarray(x_own),
            "wrt": wrt.reshape(1, D),
            "ones128": ones128,
        })
    if "l2" not in _CACHE:
        _CACHE["l2"] = build_l2()
    res = run_bass_kernel_spmd(_CACHE["l2"], in_maps, core_ids=list(range(NCORES)))
    return ([res.results[c]["x1"] for c in range(NCORES)],
            [res.results[c]["x1h"] for c in range(NCORES)],
            [res.results[c]["scores"] for c in range(NCORES)])


def _l3_consts():
    c = {}
    c["postile"] = np.tile(
        (np.arange(64)[None, :] * 128 + np.arange(128)[:, None]).astype(np.float32),
        (1, 2))
    c["onesmat"] = np.ones((128, 128), np.float32)
    c["trilT"] = np.triu(np.ones((128, 128), np.float16))
    c["siota"] = np.tile(np.arange(1024, dtype=np.float16)[None, :], (128, 1))
    rot = np.zeros((128, 128), np.float32)
    for r in range(64):
        rot[r, r + 64] = -1.0
    for r in range(64, 128):
        rot[r, r - 64] = 1.0
    c["rotT"] = rot.T.astype(np.float16)
    inv_freq = (1.0 / (10000.0 ** (np.arange(0, 128, 2, dtype=np.float32) / 128.0)
                       )).astype(np.float32)
    c["invf"] = np.concatenate([inv_freq, inv_freq]).reshape(128, 1)
    c["ident"] = np.eye(128, dtype=np.float32)
    c["pickmat"] = np.diag([1.0, 1.0, -1.0, -1.0]).astype(np.float32)
    return c


def _run_l3(inputs, x1h_shards, sc_shards):
    x1f = np.concatenate(
        [np.concatenate([x1h_shards[c][0:1024] for c in range(NCORES)], axis=0),
         np.concatenate([x1h_shards[c][1024:2048] for c in range(NCORES)], axis=0)],
        axis=0)
    sct = np.empty((128, 128), np.float32)
    for c in range(NCORES):
        sct[:, 8 * c:8 * c + 8] = sc_shards[c][:, 0:8]
        sct[:, 64 + 8 * c:64 + 8 * c + 8] = sc_shards[c][:, 8:16]
    cst = _l3_consts()
    wq = inputs["L0_ffn_wq"]
    wk = inputs["L0_ffn_wk"]
    wv = inputs["L0_ffn_wv"]
    bq = inputs["L0_ffn_bq"]
    bk = inputs["L0_ffn_bk"]
    bv = inputs["L0_ffn_bv"]
    ln1 = inputs["L0_ffn_ln1_w"]
    in_maps = []
    for c in range(NCORES):
        in_maps.append({
            "x1f": x1f,
            "sctile": sct,
            "postile": cst["postile"],
            "onesmat": cst["onesmat"],
            "ones128": np.ones((1, 128), np.float32),
            "trilT": cst["trilT"],
            "siota": cst["siota"],
            "rotT": cst["rotT"],
            "invf": cst["invf"],
            "ident": cst["ident"],
            "pickmat": cst["pickmat"],
            "masks": _causal_masks_f16(),
            "wq_sel": np.ascontiguousarray(wq[:, 256 * c:256 * c + 256]),
            "wk_sel": np.ascontiguousarray(wk[:, 128 * c:128 * c + 128]),
            "wv_sel": np.ascontiguousarray(wv[:, 128 * c:128 * c + 128]),
            "bq_sel": np.ascontiguousarray(
                bq[256 * c:256 * c + 256].reshape(2, 128).T),
            "bk_sel": bk[128 * c:128 * c + 128].reshape(128, 1),
            "bv_sel": bv[128 * c:128 * c + 128].reshape(1, 128),
            "ln1c": np.ascontiguousarray(ln1.reshape(16, 128).T),
        })
    if "l3" not in _CACHE:
        _CACHE["l3"] = build_l3()
    res = run_bass_kernel_spmd(_CACHE["l3"], in_maps, core_ids=list(range(NCORES)))
    poslist = res.results[0]["poslist"]
    selx = res.results[0]["selx"]
    z_sel = [res.results[c]["z_sel"] for c in range(NCORES)]
    return poslist, selx, z_sel


def _run_l4a(inputs, zsel_shards, selx):
    wo = inputs["L0_ffn_wo"]
    ln2 = inputs["L0_ffn_ln2_w"]
    ones128 = np.ones((1, 128), np.float32)
    ln2c = np.ascontiguousarray(ln2.reshape(16, 128).T)
    in_maps = []
    for c in range(NCORES):
        z16s = np.empty((H, E, 256), np.float16)
        for h in range(H):
            z16s[h] = zsel_shards[h // 2][(h % 2) * E:(h % 2 + 1) * E,
                                          256 * c:256 * c + 256]
        in_maps.append({
            "z16s": z16s,
            "wo_f": wo,
            "selx16": np.ascontiguousarray(selx[256 * c:256 * c + 256]),
            "ln2c": ln2c,
            "ones128": ones128,
        })
    if "l4a" not in _CACHE:
        _CACHE["l4a"] = build_l4a()
    res = run_bass_kernel_spmd(_CACHE["l4a"], in_maps, core_ids=list(range(NCORES)))
    return ([res.results[c]["h2t"] for c in range(NCORES)],
            [res.results[c]["r2t"] for c in range(NCORES)])


def _run_l4b(inputs, h2t_shards):
    h2tf = np.concatenate(h2t_shards, axis=1)
    wg = inputs["L0_ffn_w_gate"]
    wu = inputs["L0_ffn_w_up"]
    in_maps = []
    for c in range(NCORES):
        in_maps.append({
            "h2tf": h2tf,
            "wg_sl": np.ascontiguousarray(wg[:, 1024 * c:1024 * c + 1024]),
            "wu_sl": np.ascontiguousarray(wu[:, 1024 * c:1024 * c + 1024]),
        })
    if "l4b" not in _CACHE:
        _CACHE["l4b"] = build_l4b()
    res = run_bass_kernel_spmd(_CACHE["l4b"], in_maps, core_ids=list(range(NCORES)))
    return [res.results[c]["at"] for c in range(NCORES)]


def _run_l4c(inputs, at_shards, r2t_shards, selx, poslist):
    atf = np.concatenate(at_shards, axis=0)
    r2sm = np.ascontiguousarray(
        np.concatenate(r2t_shards, axis=1).T)
    wd = inputs["L0_ffn_w_down"]
    sct16 = np.ascontiguousarray(poslist[1].reshape(16, 128).T)
    in_maps = []
    for c in range(NCORES):
        in_maps.append({
            "atf": atf,
            "wd_sl": np.ascontiguousarray(wd[:, 256 * c:256 * c + 256]),
            "r2sl": np.ascontiguousarray(r2sm[:, 256 * c:256 * c + 256]),
            "sx_sl": np.ascontiguousarray(selx[:, 256 * c:256 * c + 256]),
            "sct16": sct16,
        })
    if "l4c" not in _CACHE:
        _CACHE["l4c"] = build_l4c()
    res = run_bass_kernel_spmd(_CACHE["l4c"], in_maps, core_ids=list(range(NCORES)))
    return [res.results[c]["gated"] for c in range(NCORES)]


def _run_l5(inputs, x1_shards, gated_shards, poslist):
    gatedf = np.concatenate(gated_shards, axis=1)
    post16 = np.ascontiguousarray(poslist[0].reshape(16, 128).T)
    sreli = (2048.0 + np.arange(16, dtype=np.float32)[None, :] * 128
             + np.arange(128, dtype=np.float32)[:, None])
    in_maps = []
    for c in range(NCORES):
        in_maps.append({
            "x1_own": x1_shards[c],
            "gatedf": gatedf,
            "post16": post16,
            "sreli": sreli,
            "mybase": np.full((128, 1), 1024.0 * c, np.float32),
        })
    if "l5" not in _CACHE:
        _CACHE["l5"] = build_l5()
    res = run_bass_kernel_spmd(_CACHE["l5"], in_maps, core_ids=list(range(NCORES)))
    out = np.empty((B, T, D), np.float32)
    for c in range(NCORES):
        o = res.results[c]["out"]
        out[0, 1024 * c:1024 * c + 1024] = o[0:1024]
        out[1, 1024 * c:1024 * c + 1024] = o[1024:2048]
    return out


def kernel(**inputs):
    z_shards = _run_l1(inputs)
    x1_shards, x1h_shards, sc_shards = _run_l2(inputs, z_shards)
    poslist, selx, zsel_shards = _run_l3(inputs, x1h_shards, sc_shards)
    h2t_shards, r2t_shards = _run_l4a(inputs, zsel_shards, selx)
    at_shards = _run_l4b(inputs, h2t_shards)
    gated_shards = _run_l4c(inputs, at_shards, r2t_shards, selx, poslist)
    return _run_l5(inputs, x1_shards, gated_shards, poslist)


# version 07
# speedup vs baseline: 1.1621x; 1.1621x over previous
import numpy as np
import ml_dtypes
from contextlib import ExitStack

import concourse.bass as bass
import concourse.mybir as mybir
from concourse.bass import IndirectOffsetOnAxis
from concourse.bass_utils import run_bass_kernel_spmd

F32 = mybir.dt.float32
BF16 = mybir.dt.bfloat16
F16 = mybir.dt.float16
U32 = mybir.dt.uint32
AF = mybir.ActivationFunctionType
OP = mybir.AluOpType
AX = mybir.AxisListType
BF = ml_dtypes.bfloat16

NCORES = 8
B, T, D, H, E = 2, 8192, 2048, 16, 128
HPC = 4
DC = D // 128
QT = 512
NQT = T // QT
INV_SCALE = float(1.0 / np.sqrt(np.float32(128.0)))
LN_ATTN_SCALE = float(np.log(np.sqrt(np.float64(128.0))))

SELK = 1024
S_SEL = 2 * SELK
I_FF = 8192
EPS_LN = 1e-5
EPS_RMS = 1e-6


class Op:
    __slots__ = ("eng", "fn", "deps", "dma", "sem", "milestone_val", "n_global", "dma_class")

    def __init__(self, eng, fn, deps, dma):
        self.eng = eng
        self.fn = fn
        self.deps = deps
        self.dma = dma
        self.sem = False
        self.milestone_val = None
        self.n_global = -1


class Sched:
    ENGINES = ("sync", "scalar", "vector", "tensor", "gpsimd")
    NDMA_CLASSES = 6

    def __init__(self, paranoid=False, serialize_dma=False):
        self.ops = {e: [] for e in self.ENGINES}
        self.n = 0
        self.paranoid = paranoid
        self.serialize_dma = serialize_dma
        self._dma_count = {}
        self._class_prev = {}

    def op(self, eng, fn, deps=(), dma=False):
        deps = [d for d in deps if d is not None]
        h = Op(eng, fn, deps, dma)
        if dma:
            k = self._dma_count.get(eng, 0)
            self._dma_count[eng] = k + 1
            cls = 0 if self.serialize_dma else k % self.NDMA_CLASSES
            h.dma_class = (eng, cls)
            prev = self._class_prev.get(h.dma_class)
            if prev is not None:
                deps.append(prev)
            self._class_prev[h.dma_class] = h
        else:
            h.dma_class = None
        h.deps = deps
        h.n_global = self.n
        self.n += 1
        self.ops[eng].append(h)
        return h

    def _needs_wait(self, dep, eng):
        return dep.eng != eng or dep.dma or self.paranoid

    def build(self, nc, ctx, block):
        for e in self.ENGINES:
            for h in self.ops[e]:
                if h.dma:
                    h.sem = True
                for d in h.deps:
                    if self._needs_wait(d, e):
                        d.sem = True
        sems = {}
        dma_sems = {}
        for e in self.ENGINES:
            if any(o.sem and not o.dma for o in self.ops[e]):
                sems[e] = ctx.enter_context(nc.semaphore(f"prog_{e}"))
            ncls = 1 if self.serialize_dma else self.NDMA_CLASSES
            if any(o.dma for o in self.ops[e]):
                for cl in range(ncls):
                    dma_sems[(e, cl)] = ctx.enter_context(
                        nc.semaphore(f"dma_{e}_{cl}"))
        for e in self.ENGINES:
            c = 0
            dcnt = {}
            for o in self.ops[e]:
                if not o.sem:
                    continue
                if o.dma:
                    k = dcnt.get(o.dma_class, 0) + 16
                    dcnt[o.dma_class] = k
                    o.milestone_val = k
                else:
                    c += 1
                    o.milestone_val = c

        def emit_stream(q, ename):
            waited = {}
            for o in self.ops[ename]:
                for d in o.deps:
                    if not self._needs_wait(d, ename):
                        continue
                    v = d.milestone_val
                    assert v is not None
                    key = d.dma_class if d.dma else d.eng
                    sem = dma_sems[key] if d.dma else sems[key]
                    if v > waited.get(key, 0):
                        q.wait_ge(sem, v)
                        waited[key] = v
                ins = o.fn(q)
                if o.sem:
                    assert ins is not None, f"milestone op on {ename} returned None"
                    if o.dma:
                        ins.then_inc(dma_sems[o.dma_class], 16)
                    else:
                        ins.then_inc(sems[ename], 1)

        streams = {}
        if self.ops["sync"]:
            @block.sync
            def _(q):
                emit_stream(q, "sync")
        if self.ops["scalar"]:
            @block.scalar
            def _(q):
                emit_stream(q, "scalar")
        if self.ops["vector"]:
            @block.vector
            def _(q):
                emit_stream(q, "vector")
        if self.ops["tensor"]:
            @block.tensor
            def _(q):
                emit_stream(q, "tensor")
        if self.ops["gpsimd"]:
            @block.gpsimd
            def _(q):
                emit_stream(q, "gpsimd")
        return streams


def _chunk_schedule():
    out = []
    for qi in range(NQT):
        nk = 4 * qi
        for kc in range(nk + 4):
            out.append((qi, kc, kc - nk if kc >= nk else None))
    return out


CHUNKS = _chunk_schedule()
NCH = len(CHUNKS)


def build_l1(paranoid=True, serialize_dma=False):
    nc = bass.Bass("TRN2", target_bir_lowering=False, num_devices=NCORES)
    x_t = nc.dram_tensor("x_t", [D, T], F32, kind="ExternalInput").ap()
    wq_in = nc.dram_tensor("wq", [HPC, D, E], F32, kind="ExternalInput").ap()
    wk_in = nc.dram_tensor("wk", [HPC, D, E], F32, kind="ExternalInput").ap()
    wv_in = nc.dram_tensor("wv_mov", [D, HPC * E], F32, kind="ExternalInput").ap()
    bq_in = nc.dram_tensor("bq_t", [E, HPC], F32, kind="ExternalInput").ap()
    bk_in = nc.dram_tensor("bk_t", [E, HPC], F32, kind="ExternalInput").ap()
    bv_in = nc.dram_tensor("bv_mov", [1, HPC * E], F32, kind="ExternalInput").ap()
    ones_in = nc.dram_tensor("ones128", [1, 128], F32, kind="ExternalInput").ap()
    mask_in = nc.dram_tensor("masks", [4, 128, QT], F16, kind="ExternalInput").ap()
    z_out = nc.dram_tensor("z", [HPC * E, T], F16, kind="ExternalOutput").ap()
    xbf = nc.dram_tensor("xbf", [D, T], F16, kind="Internal")

    sch = Sched(paranoid=paranoid, serialize_dma=serialize_dma)
    with ExitStack() as ctx:
        ee = ctx.enter_context
        wq_s = ee(nc.sbuf_tensor("wq_s", [128, HPC * DC * E], F16))
        wk_s = ee(nc.sbuf_tensor("wk_s", [128, HPC * DC * E], F16))
        wv_s = ee(nc.sbuf_tensor("wv_s", [128, DC * HPC * E], F16))
        bq_s = ee(nc.sbuf_tensor("bq_s", [128, HPC], F32))
        bk_s = ee(nc.sbuf_tensor("bk_s", [128, HPC], F32))
        bv_row = ee(nc.sbuf_tensor("bv_row", [1, HPC * E], F32))
        bvB = ee(nc.sbuf_tensor("bvB", [128, HPC * E], F32))
        ones_row = ee(nc.sbuf_tensor("ones_row", [1, 128], F32))
        onesb_col = ee(nc.sbuf_tensor("onesb_col", [128, 1], F16))
        mask_s = ee(nc.sbuf_tensor("mask_s", [128, 4 * QT], F16))
        q_s = [ee(nc.sbuf_tensor(f"q_s{i}", [128, T], F16)) for i in range(2)]
        k_s = [ee(nc.sbuf_tensor(f"k_s{i}", [128, T], F16)) for i in range(2)]
        v_s = [ee(nc.sbuf_tensor(f"v_s{i}", [128, T], F16)) for i in range(2)]
        xt_t = [ee(nc.sbuf_tensor(f"xt_t{i}", [128, 4 * QT], F16)) for i in range(2)]
        sq_t = [ee(nc.sbuf_tensor(f"sq_t{i}", [128, QT], F16)) for i in range(2)]
        pT = [ee(nc.sbuf_tensor(f"pT{i}", [128, 2 * QT], F16)) for i in range(2)]
        zn_s = [ee(nc.sbuf_tensor(f"zn_s{i}", [128, QT], F16)) for i in range(2)]
        lrec = ee(nc.sbuf_tensor("lrec", [1, QT], F32))
        lrecB = ee(nc.sbuf_tensor("lrecB", [128, QT], F32))
        cm = ee(nc.sbuf_tensor("cm", [1, 4], F32))
        ctile = ee(nc.sbuf_tensor("ctile", [1, 8], F32))
        lnbias = ee(nc.sbuf_tensor("lnbias", [1, 1], F32))
        cbias = ee(nc.sbuf_tensor("cbias", [128, 2], F32))
        psA = ee(nc.psum_tensor("psA", [128, 2 * QT], F32))
        psB = ee(nc.psum_tensor("psB", [128, 2 * QT], F32))
        psZ = [ee(nc.psum_tensor(f"psZ{i}", [128, QT], F32)) for i in range(2)]
        psL = ee(nc.psum_tensor("psL", [128, QT], F32))
        psR = ee(nc.psum_tensor("psR", [128, QT], F32))
        block = ee(nc.Block())

        def bank(b, c0=0, c1=QT, r0=0, r1=128):
            if b < 4:
                t = psA if b < 2 else psB
                off = QT * (b % 2)
                return t[r0:r1, off + c0:off + c1]
            return [psZ[0], psZ[1], psL, psR][b - 4][r0:r1, c0:c1]

        def wq_tile(h, dc):
            return wq_s[:, (h * DC + dc) * E:(h * DC + dc + 1) * E]

        def wk_tile(h, dc):
            return wk_s[:, (h * DC + dc) * E:(h * DC + dc + 1) * E]

        stage = [sch.op("gpsimd",
                        lambda g, i=i: g.dma_start(
                            out=xbf.ap()[:, i * QT:(i + 1) * QT],
                            in_=x_t[:, i * QT:(i + 1) * QT]), dma=True)
                 for i in range(NQT)]
        wloads = []
        for h in range(HPC):
            wloads.append(sch.op("gpsimd", lambda g, h=h: g.dma_start(
                out=wq_s[:, :].rearrange("p (h dc e) -> p h dc e", h=HPC, dc=DC)[:, h],
                in_=wq_in[h].rearrange("(dc p) e -> p dc e", p=128)), dma=True))
            wloads.append(sch.op("gpsimd", lambda g, h=h: g.dma_start(
                out=wk_s[:, :].rearrange("p (h dc e) -> p h dc e", h=HPC, dc=DC)[:, h],
                in_=wk_in[h].rearrange("(dc p) e -> p dc e", p=128)), dma=True))
        wloads.append(sch.op("gpsimd", lambda g: g.dma_start(
            out=wv_s[:, :].rearrange("p (dc e) -> p dc e", dc=DC),
            in_=wv_in.rearrange("(dc p) e -> p dc e", p=128)), dma=True))
        ld_bq = sch.op("gpsimd", lambda g: g.dma_start(out=bq_s[:, :], in_=bq_in), dma=True)
        ld_bk = sch.op("gpsimd", lambda g: g.dma_start(out=bk_s[:, :], in_=bk_in), dma=True)
        ld_bv = sch.op("gpsimd", lambda g: g.dma_start(out=bv_row[:, :], in_=bv_in), dma=True)
        ld_on = sch.op("gpsimd", lambda g: g.dma_start(out=ones_row[:, :], in_=ones_in), dma=True)
        ld_mk = sch.op("gpsimd", lambda g: g.dma_start(
            out=mask_s[:, :].rearrange("p (c q) -> p c q", c=4),
            in_=mask_in.rearrange("c p q -> p c q")), dma=True)
        ms_onesb = sch.op("vector", lambda v: v.memset(onesb_col[:, :], 1.0))
        ms_lnb = sch.op("vector", lambda v: v.memset(lnbias[:, :], -LN_ATTN_SCALE))
        mm_bvB = sch.op("tensor", lambda t: t.matmul(
            psR[:, 0:HPC * E], ones_row[:, :], bv_row[:, :], start=True, stop=True),
            deps=[ld_bv, ld_on])
        ev_bvB = sch.op("vector", lambda v: v.tensor_copy(bvB[:, :], psR[:, 0:HPC * E]),
                        deps=[mm_bvB])

        tile_user = {}
        g_tile = 0
        zstores = []
        exp_g = {}
        pv_g = {}
        zn_g = {}
        rc_g = {}
        gci = 0
        gqi = 0
        lb_prev = None
        lc_prev = None
        cchain_last = ev_bvB
        attn_prev_tensor = None
        attn_prev_vector = None
        attn_prev_scalar = None

        for grp in range(2):
            qk_evac = {}
            for tt in range(NQT):
                for dq in range(4):
                    i = g_tile
                    deps = [stage[tt]]
                    if i >= 2:
                        deps.append(tile_user[i - 2])
                    ld = sch.op("sync", lambda q, dq=dq, tt=tt, i=i: q.dma_start(
                        out=xt_t[i % 2][:, :].rearrange("p (j t) -> p j t", j=4),
                        in_=xbf.ap().rearrange("(dc p) t -> p dc t", p=128)[
                            :, dq * 4:(dq + 1) * 4, tt * QT:(tt + 1) * QT]),
                        deps=deps, dma=True)
                    last = None
                    for j in range(4):
                        dc = dq * 4 + j
                        for pi in range(2):
                            h = grp * 2 + pi
                            b0 = (tt % 2) * 4 + 2 * pi
                            d_ = [ld]
                            if i == 0 and grp == 0:
                                d_ += wloads
                            if dc == 0:
                                if grp == 0 and tt == 1:
                                    d_.append(ev_bvB)
                                if tt >= 2:
                                    d_.append(qk_evac[(tt - 2, pi, 1)])
                                elif grp == 1:
                                    d_ += [attn_prev_scalar, attn_prev_vector]
                            sch.op("tensor", lambda t, h=h, dc=dc, j=j, b0=b0, i=i:
                                   t.matmul(bank(b0), wq_tile(h, dc),
                                            xt_t[i % 2][:, j * QT:(j + 1) * QT],
                                            start=(dc == 0), stop=(dc == DC - 1)),
                                   deps=d_)
                            last = sch.op("tensor",
                                          lambda t, h=h, dc=dc, j=j, b0=b0, i=i:
                                          t.matmul(bank(b0 + 1), wk_tile(h, dc),
                                                   xt_t[i % 2][:, j * QT:(j + 1) * QT],
                                                   start=(dc == 0),
                                                   stop=(dc == DC - 1)))
                    tile_user[i] = last
                    g_tile += 1
                    if dq == 3:
                        for pi in range(2):
                            h = grp * 2 + pi
                            b0 = (tt % 2) * 4 + 2 * pi
                            d_ = [last, ld_bq, ld_bk]
                            if grp == 1 and tt < 2:
                                d_.append(attn_prev_tensor)
                            e1 = sch.op("vector", lambda v, pi=pi, h=h, b0=b0, tt=tt:
                                        v.tensor_scalar_add(
                                            q_s[pi][:, tt * QT:(tt + 1) * QT],
                                            bank(b0), bq_s[:, h:h + 1]),
                                        deps=d_)
                            e2 = sch.op("vector", lambda v, pi=pi, h=h, b0=b0, tt=tt:
                                        v.tensor_scalar_add(
                                            k_s[pi][:, tt * QT:(tt + 1) * QT],
                                            bank(b0 + 1), bk_s[:, h:h + 1]))
                            qk_evac[(tt, pi, 0)] = e1
                            qk_evac[(tt, pi, 1)] = e2
            red_ops = {}
            sq_ops = {}
            j_list = [(pi, isk) for pi in range(2) for isk in range(2)]
            for jj, (pi, isk) in enumerate(j_list):
                src = (k_s if isk else q_s)[pi]
                for tt in range(NQT):
                    u = jj * NQT + tt
                    d_ = [qk_evac[(tt, pi, isk)]]
                    if u >= 2:
                        d_.append(red_ops[u - 2])
                    sqo = sch.op("scalar", lambda s, src=src, tt=tt, u=u:
                                 s.activation(out=sq_t[u % 2][:, :],
                                              in_=src[:, tt * QT:(tt + 1) * QT],
                                              func=AF.Square), deps=d_)
                    sq_ops[u] = sqo
                    d_ = [sqo, ms_onesb]
                    if u == 0:
                        d_.append(cchain_last)
                    if u < 4:
                        d_.append(qk_evac[(NQT - 2, 1, 1)])
                        d_.append(qk_evac[(NQT - 1, 1, 1)])
                    if u >= 4:
                        d_.append(red_ops[u - 4])
                    mmo = sch.op("tensor", lambda t, u=u: t.matmul(
                        bank(u % 4, r0=0, r1=1), onesb_col[:, :], sq_t[u % 2][:, :],
                        start=True, stop=True), deps=d_)
                    rd = sch.op("vector", lambda v, u=u: v.reduce_max(
                        ctile[0:1, u % 4:u % 4 + 1], bank(u % 4, r0=0, r1=1),
                        axis=AX.X), deps=[mmo])
                    slot = jj
                    if tt == 0:
                        acc = sch.op("vector", lambda v, slot=slot, u=u: v.tensor_copy(
                            cm[0:1, slot:slot + 1], ctile[0:1, u % 4:u % 4 + 1]), deps=[rd])
                    else:
                        acc = sch.op("vector", lambda v, slot=slot, u=u: v.tensor_max(
                            cm[0:1, slot:slot + 1], cm[0:1, slot:slot + 1],
                            ctile[0:1, u % 4:u % 4 + 1]), deps=[rd])
                    red_ops[u] = acc
            cb_ready = []
            for pi in range(2):
                pr = sch.op("vector", lambda v, pi=pi: v.tensor_mul(
                    ctile[0:1, 4 + pi:5 + pi], cm[0:1, 2 * pi:2 * pi + 1],
                    cm[0:1, 2 * pi + 1:2 * pi + 2]),
                    deps=[red_ops[(2 * pi) * NQT + NQT - 1],
                          red_ops[(2 * pi + 1) * NQT + NQT - 1]])
                ln_ = sch.op("scalar", lambda s, pi=pi: s.activation(
                    out=ctile[0:1, 4 + pi:5 + pi], in_=ctile[0:1, 4 + pi:5 + pi],
                    func=AF.Ln), deps=[pr])
                ex_ = sch.op("scalar", lambda s, pi=pi: s.activation(
                    out=ctile[0:1, 4 + pi:5 + pi], in_=ctile[0:1, 4 + pi:5 + pi],
                    func=AF.Exp, bias=lnbias[0:1, 0:1], scale=0.5), deps=[ln_, ms_lnb])
                ng = sch.op("vector", lambda v, pi=pi: v.tensor_scalar(
                    ctile[0:1, 4 + pi:5 + pi], ctile[0:1, 4 + pi:5 + pi],
                    -1.0, 8.0, OP.mult, OP.add), deps=[ex_])
                bcd = [ng]
                if pi == 0:
                    bcd.append(ev_bvB if grp == 0 else lc_prev)
                else:
                    bcd.append(cb_ready[0])
                bc = sch.op("tensor", lambda t, pi=pi: t.matmul(
                    psR[:, 0:1], ones_row[:, :], ctile[0:1, 4 + pi:5 + pi],
                    start=True, stop=True), deps=bcd)
                cb = sch.op("vector", lambda v, pi=pi: v.tensor_copy(
                    cbias[:, pi:pi + 1], psR[:, 0:1]), deps=[bc])
                cb_ready.append(cb)
            v_evac = {}
            for tt in range(NQT):
                for dq in range(4):
                    i = g_tile
                    deps = [stage[tt]]
                    if i >= 2:
                        deps.append(tile_user[i - 2])
                    ld = sch.op("sync", lambda q, dq=dq, tt=tt, i=i: q.dma_start(
                        out=xt_t[i % 2][:, :].rearrange("p (j t) -> p j t", j=4),
                        in_=xbf.ap().rearrange("(dc p) t -> p dc t", p=128)[
                            :, dq * 4:(dq + 1) * 4, tt * QT:(tt + 1) * QT]),
                        deps=deps, dma=True)
                    last = None
                    for j in range(4):
                        dc = dq * 4 + j
                        for ts in range(4):
                            b = (tt % 2) * 4 + ts
                            d_ = [ld]
                            if dc == 0:
                                if tt >= 2:
                                    d_.append(v_evac[(tt - 2, ts, 1)])
                                else:
                                    d_.append(qk_evac[(NQT - 2 + tt, 1, 1)])
                                    d_.append(red_ops[4 * NQT - 1])
                                    d_.append(cb_ready[-1])
                            last = sch.op("tensor",
                                          lambda t, i=i, ts=ts, j=j, b=b, dc=dc,
                                          grp=grp:
                                          t.matmul(bank(b, 0, 2 * E),
                                                   xt_t[i % 2][:, j * QT + ts * 128:
                                                        j * QT + (ts + 1) * 128],
                                                   wv_s[:, dc * HPC * E + grp * 2 * E:
                                                        dc * HPC * E
                                                        + (grp * 2 + 2) * E],
                                                   start=(dc == 0),
                                                   stop=(dc == DC - 1)),
                                          deps=d_)
                    tile_user[i] = last
                    g_tile += 1
                    if dq == 3:
                        for ts in range(4):
                            b = (tt % 2) * 4 + ts
                            tc = tt * 4 + ts
                            for pi in range(2):
                                h = grp * 2 + pi
                                ev = sch.op("vector", lambda v, pi=pi, h=h, b=b, tc=tc:
                                            v.tensor_add(
                                                v_s[pi][:, tc * 128:(tc + 1) * 128],
                                                bank(b, pi * 128, (pi + 1) * 128),
                                                bvB[:, h * 128:(h + 1) * 128]),
                                            deps=[last, ev_bvB])
                                v_evac[(tt, ts, pi)] = ev
            last_v_evac = v_evac[(NQT - 1, 3, 1)]
            lc_prev = cb_ready[1]
            for pi in range(2):
                pair_h = grp * 2 + pi

                def emit_S(n, qi, pl, gp):
                    pp = psA if gp % 2 == 0 else psB
                    d_ = []
                    if gp >= 2 and gp - 2 in exp_g:
                        d_.append(exp_g[gp - 2])
                    else:
                        d_.append(last_v_evac)
                    if n < 2:
                        d_.append(last_v_evac)
                        d_ += cb_ready
                    sch.op("tensor", lambda t, pp=pp, pi=pi, qi=qi, pl=pl:
                           t.matmul(pp[:, 0:QT],
                                    k_s[pi][:, (2 * pl) * 128:(2 * pl + 1) * 128],
                                    q_s[pi][:, qi * QT:(qi + 1) * QT],
                                    start=True, stop=True), deps=d_)
                    sc2 = sch.op("tensor", lambda t, pp=pp, pi=pi, qi=qi, pl=pl:
                                 t.matmul(pp[:, QT:2 * QT],
                                          k_s[pi][:, (2 * pl + 1) * 128:
                                                  (2 * pl + 2) * 128],
                                          q_s[pi][:, qi * QT:(qi + 1) * QT],
                                          start=True, stop=True))
                    return sc2

                def emit_EXP(n, qi, pl, gp, sc2):
                    pp = psA if gp % 2 == 0 else psB
                    ed = [sc2, cb_ready[pi]]
                    if gp >= 2 and gp - 2 in pv_g:
                        ed.append(pv_g[gp - 2])
                    ex = sch.op("scalar", lambda s, pp=pp, gp=gp, pi=pi:
                                s.activation(out=pT[gp % 2][:, :], in_=pp[:, :],
                                             func=AF.Exp, bias=cbias[:, pi:pi + 1],
                                             scale=INV_SCALE), deps=ed)
                    exp_g[gp] = ex
                    return ex

                def emit_PVL(n, qi, pl, npair, gp):
                    nonlocal gqi, lb_prev, lc_prev
                    first = pl == 0
                    last_c = pl == npair - 1
                    zb = psZ[gqi % 2]
                    pvd = [exp_g[gp]]
                    if pl >= npair - 2:
                        mp = pl - (npair - 2)
                        mko = sch.op("vector", lambda v, gp=gp, mp=mp:
                                     v.tensor_mul(pT[gp % 2][:, :], pT[gp % 2][:, :],
                                                  mask_s[:, mp * 2 * QT:
                                                         (mp + 1) * 2 * QT]),
                                     deps=[exp_g[gp], ld_mk])
                        pvd = [mko]
                    if first:
                        if gqi >= 2:
                            pvd.append(zn_g[gqi - 2])
                        else:
                            pvd.append(last_v_evac)
                    for half in range(2):
                        kc = 2 * pl + half
                        sch.op("tensor", lambda t, zb=zb, pi=pi, kc=kc, gp=gp,
                               half=half, st=(first and half == 0),
                               sp=(last_c and half == 1):
                               t.matmul(zb[:, :],
                                        v_s[pi][:, kc * 128:(kc + 1) * 128],
                                        pT[gp % 2][:, half * QT:(half + 1) * QT],
                                        start=st, stop=sp), deps=pvd if half == 0
                               else [])
                    for half in range(2):
                        d_ = []
                        if half == 0 and first:
                            d_ = [rc_g[gqi - 1]] if gqi >= 1 else [last_v_evac]
                        pv = sch.op("tensor", lambda t, gp=gp, half=half,
                                    st=(first and half == 0),
                                    sp=(last_c and half == 1):
                                    t.matmul(psL[0:1, 0:QT], onesb_col[:, :],
                                             pT[gp % 2][:, half * QT:(half + 1) * QT],
                                             start=st, stop=sp), deps=d_)
                    pv_g[gp] = pv
                    if last_c:
                        rcd = [pv]
                        if lb_prev is not None:
                            rcd.append(lb_prev)
                        rc = sch.op("vector", lambda v: v.reciprocal(
                            lrec[:, :], psL[0:1, 0:QT]), deps=rcd)
                        rc_g[gqi] = rc
                        lbd = [rc, lc_prev]
                        lb = sch.op("tensor", lambda t: t.matmul(
                            psR[:, :], ones_row[:, :], lrec[:, :],
                            start=True, stop=True), deps=lbd)
                        lb_prev = lb
                        lc = sch.op("vector", lambda v: v.tensor_copy(
                            lrecB[:, :], psR[:, :]), deps=[lb])
                        lc_prev = lc
                        znd = [lc]
                        if len(zstores) >= 2:
                            znd.append(zstores[-2])
                        zn = sch.op("vector", lambda v, gqi=gqi, zb=zb: v.tensor_mul(
                            zn_s[gqi % 2][:, :], zb[:, :], lrecB[:, :]), deps=znd)
                        zn_g[gqi] = zn
                        st = sch.op("sync", lambda q, pair_h=pair_h, qi=qi, gqi=gqi:
                                    q.dma_start(
                                        out=z_out[pair_h * E:(pair_h + 1) * E,
                                                  qi * QT:(qi + 1) * QT],
                                        in_=zn_s[gqi % 2][:, :]), deps=[zn], dma=True)
                        zstores.append(st)
                        gqi += 1

                plist = []
                for qi in range(NQT):
                    npair = 2 * qi + 2
                    for pl in range(npair):
                        plist.append((qi, pl, npair))
                prev = None
                for n, (qi, pl, npair) in enumerate(plist):
                    gp = gci
                    gci += 1
                    sc2 = emit_S(n, qi, pl, gp)
                    emit_EXP(n, qi, pl, gp, sc2)
                    if prev is not None:
                        emit_PVL(*prev)
                    prev = (n, qi, pl, npair, gp)
                emit_PVL(*prev)
            attn_prev_tensor = pv_g[gci - 1]
            attn_prev_vector = zn_g[gqi - 1]
            attn_prev_scalar = exp_g[gci - 1]
            cchain_last = zn_g[gqi - 1]
        sch.op("sync", lambda q: q.nop(), deps=zstores)
        sch.build(nc, ctx, block)
    return nc


def build_l2(paranoid=True, serialize_dma=False):
    nc = bass.Bass("TRN2", target_bir_lowering=False, num_devices=NCORES)
    z_in = nc.dram_tensor("z16", [H, E, 2048], F16, kind="ExternalInput").ap()
    wo_in = nc.dram_tensor("wo", [H, E, D], F32, kind="ExternalInput").ap()
    bo_in = nc.dram_tensor("bo", [1, D], F32, kind="ExternalInput").ap()
    xo_in = nc.dram_tensor("x_own", [2048, D], F32, kind="ExternalInput").ap()
    wrt_in = nc.dram_tensor("wrt", [1, D], F32, kind="ExternalInput").ap()
    ones_in = nc.dram_tensor("ones128", [1, 128], F32, kind="ExternalInput").ap()
    x1_out = nc.dram_tensor("x1", [2048, D], F32, kind="ExternalOutput").ap()
    x1h_out = nc.dram_tensor("x1h", [2048, D], F16, kind="ExternalOutput").ap()
    sc_out = nc.dram_tensor("scores", [128, 16], F32, kind="ExternalOutput").ap()

    NTC = 16
    sch = Sched(paranoid=paranoid, serialize_dma=serialize_dma)
    with ExitStack() as ctx:
        ee = ctx.enter_context
        z_s = ee(nc.sbuf_tensor("z_s", [128, H * 2048], F16))
        wo_s = ee(nc.sbuf_tensor("wo_s", [128, H * D], F16))
        bo_row = ee(nc.sbuf_tensor("bo_row", [1, D], F32))
        wrt_row = ee(nc.sbuf_tensor("wrt_row", [1, D], F32))
        ones_row = ee(nc.sbuf_tensor("ones_row", [1, 128], F32))
        boB = ee(nc.sbuf_tensor("boB", [128, D], F32))
        wrtB = ee(nc.sbuf_tensor("wrtB", [128, D], F32))
        xo_t = [ee(nc.sbuf_tensor(f"xo_t{i}", [128, D], F32)) for i in range(2)]
        r_t = ee(nc.sbuf_tensor("r_t", [128, D], F32))
        x1_t = [ee(nc.sbuf_tensor(f"x1_t{i}", [128, D], F32)) for i in range(2)]
        st1 = ee(nc.sbuf_tensor("st1", [128, 8], F32))
        sc_s = ee(nc.sbuf_tensor("sc_s", [128, 16], F32))
        eps_c = ee(nc.sbuf_tensor("eps_c", [128, 1], F32))
        ps = [ee(nc.psum_tensor(f"ps{i}", [128, QT], F32)) for i in range(8)]
        block = ee(nc.Block())

        ld_z = [sch.op("gpsimd", lambda g, h=h: g.dma_start(
            out=z_s[:, h * 2048:(h + 1) * 2048], in_=z_in[h]), dma=True)
            for h in range(H)]
        ld_wo = [sch.op("gpsimd", lambda g, h=h: g.dma_start(
            out=wo_s[:, h * D:(h + 1) * D], in_=wo_in[h]), dma=True)
            for h in range(H)]
        ld_bo = sch.op("gpsimd", lambda g: g.dma_start(out=bo_row[:, :], in_=bo_in), dma=True)
        ld_wr = sch.op("gpsimd", lambda g: g.dma_start(out=wrt_row[:, :], in_=wrt_in), dma=True)
        ld_on = sch.op("gpsimd", lambda g: g.dma_start(out=ones_row[:, :], in_=ones_in), dma=True)
        ms_eps = sch.op("vector", lambda v: v.memset(eps_c[:, :], EPS_LN))
        bco = []
        for j in range(4):
            m1 = sch.op("tensor", lambda t, j=j: t.matmul(
                ps[0][:, :], ones_row[:, :], bo_row[0:1, j * QT:(j + 1) * QT],
                start=True, stop=True), deps=[ld_bo, ld_on] + ([bco[-1]] if bco else []))
            c1 = sch.op("vector", lambda v, j=j: v.tensor_copy(
                boB[:, j * QT:(j + 1) * QT], ps[0][:, :]), deps=[m1])
            m2 = sch.op("tensor", lambda t, j=j: t.matmul(
                ps[1][:, :], ones_row[:, :], wrt_row[0:1, j * QT:(j + 1) * QT],
                start=True, stop=True), deps=[ld_wr, c1])
            c2 = sch.op("vector", lambda v, j=j: v.tensor_copy(
                wrtB[:, j * QT:(j + 1) * QT], ps[1][:, :]), deps=[m2])
            bco.append(c2)
        bcast_done = bco[-1]

        x1_stores = []
        x1h_stores = []
        prev_epi = {}
        xb_ops = {}
        rt_free = {}
        x1o_prev = None
        lnv_prev = None
        last_score = None
        for tc in range(NTC):
            par = tc % 2
            ld_x = sch.op("sync", lambda q, tc=tc: q.dma_start(
                out=xo_t[tc % 2][:, :], in_=xo_in[tc * 128:(tc + 1) * 128, :]),
                deps=[xb_ops.get(tc - 2)], dma=True)
            mm_last = {}
            for dblk in range(4):
                bank = par * 4 + dblk
                last = None
                for h in range(H):
                    d_ = []
                    if h == 0:
                        d_ = ld_z + ld_wo + [bcast_done]
                        if tc >= 2:
                            d_.append(prev_epi[par])
                    last = sch.op("tensor", lambda t, h=h, dblk=dblk, bank=bank, tc=tc:
                                  t.matmul(ps[bank][:, :],
                                           z_s[:, h * 2048 + tc * 128:
                                               h * 2048 + (tc + 1) * 128],
                                           wo_s[:, h * D + dblk * QT:
                                                h * D + (dblk + 1) * QT],
                                           start=(h == 0), stop=(h == H - 1)), deps=d_)
                mm_last[dblk] = last
            xb = sch.op("vector", lambda v, tc=tc: v.tensor_add(
                xo_t[tc % 2][:, :], xo_t[tc % 2][:, :], boB[:, :]),
                deps=[ld_x, bcast_done])
            xb_ops[tc] = xb
            ep = None
            for dblk in range(4):
                d_ = [mm_last[dblk], xb]
                if dblk == 0 and tc >= 1:
                    d_.append(rt_free[tc - 1])
                ep = sch.op("vector", lambda v, tc=tc, dblk=dblk, bank=par * 4 + dblk:
                            v.tensor_add(r_t[:, dblk * QT:(dblk + 1) * QT],
                                         ps[bank][:, :],
                                         xo_t[tc % 2][:, dblk * QT:(dblk + 1) * QT]),
                            deps=d_)
            prev_epi[par] = ep
            sm = sch.op("vector", lambda v: v.reduce_sum(
                st1[:, 0:1], r_t[:, :], axis=AX.X), deps=[ep])
            s2 = sch.op("scalar", lambda s, tc=tc: s.activation(
                out=x1_t[tc % 2][:, :], in_=r_t[:, :], func=AF.Square,
                accum_out=st1[:, 1:2]),
                deps=[ep, x1o_prev,
                      x1_stores[-2] if len(x1_stores) >= 2 else None,
                      x1h_stores[-2] if len(x1h_stores) >= 2 else None])
            mean = sch.op("vector", lambda v: v.tensor_scalar_mul(
                st1[:, 2:3], st1[:, 0:1], 1.0 / D), deps=[sm])
            msq = sch.op("vector", lambda v: v.tensor_scalar_mul(
                st1[:, 3:4], st1[:, 1:2], 1.0 / D), deps=[s2])
            m2_ = sch.op("vector", lambda v: v.tensor_mul(
                st1[:, 4:5], st1[:, 2:3], st1[:, 2:3]), deps=[mean])
            var = sch.op("vector", lambda v: v.tensor_sub(
                st1[:, 5:6], st1[:, 3:4], st1[:, 4:5]), deps=[msq, m2_, lnv_prev])
            lnv = sch.op("scalar", lambda s: s.activation(
                out=st1[:, 6:7], in_=st1[:, 5:6], func=AF.Ln,
                bias=eps_c[:, 0:1]), deps=[var, ms_eps])
            lnv_prev = lnv
            rstd = sch.op("scalar", lambda s: s.activation(
                out=st1[:, 7:8], in_=st1[:, 6:7], func=AF.Exp,
                scale=-0.5), deps=[lnv])
            x1o = sch.op("vector", lambda v, tc=tc: v.tensor_scalar(
                x1_t[tc % 2][:, :], r_t[:, :], st1[:, 2:3], st1[:, 7:8],
                OP.subtract, OP.mult),
                deps=[rstd, x1_stores[-2] if len(x1_stores) >= 2 else None,
                      x1h_stores[-2] if len(x1h_stores) >= 2 else None])
            x1o_prev = x1o
            st = sch.op("sync", lambda q, tc=tc: q.dma_start(
                out=x1_out[tc * 128:(tc + 1) * 128, :], in_=x1_t[tc % 2][:, :]),
                deps=[x1o], dma=True)
            sth = sch.op("gpsimd", lambda g, tc=tc: g.dma_start(
                out=x1h_out[tc * 128:(tc + 1) * 128, :], in_=x1_t[tc % 2][:, :]),
                deps=[x1o], dma=True)
            x1_stores.append(st)
            x1h_stores.append(sth)
            scp = sch.op("vector", lambda v, tc=tc: v.tensor_mul(
                r_t[:, :], x1_t[tc % 2][:, :], wrtB[:, :]), deps=[x1o])
            red = sch.op("vector", lambda v, tc=tc: v.reduce_sum(
                sc_s[:, tc:tc + 1], r_t[:, :], axis=AX.X), deps=[scp])
            rt_free[tc] = red
            last_score = red
        st_sc = sch.op("sync", lambda q: q.dma_start(out=sc_out, in_=sc_s[:, :]),
                       deps=[last_score], dma=True)
        sch.op("sync", lambda q: q.nop(),
               deps=x1_stores + x1h_stores + [st_sc])
        sch.build(nc, ctx, block)
    return nc


BISECT_ITERS = 44


def build_l3(paranoid=True, serialize_dma=False):
    nc = bass.Bass("TRN2", target_bir_lowering=False, num_devices=NCORES)
    x1f = nc.dram_tensor("x1f", [2 * T, D], F16, kind="ExternalInput").ap()
    sct_in = nc.dram_tensor("sctile", [128, 128], F32, kind="ExternalInput").ap()
    pos_in = nc.dram_tensor("postile", [128, 128], F32, kind="ExternalInput").ap()
    onesm_in = nc.dram_tensor("onesmat", [128, 128], F32, kind="ExternalInput").ap()
    ones_in = nc.dram_tensor("ones128", [1, 128], F32, kind="ExternalInput").ap()
    tril_in = nc.dram_tensor("trilT", [128, 128], F16, kind="ExternalInput").ap()
    siota_in = nc.dram_tensor("siota", [128, 1024], F16, kind="ExternalInput").ap()
    rot_in = nc.dram_tensor("rotT", [128, 128], F16, kind="ExternalInput").ap()
    invf_in = nc.dram_tensor("invf", [128, 1], F32, kind="ExternalInput").ap()
    ident_in = nc.dram_tensor("ident", [128, 128], F32, kind="ExternalInput").ap()
    pick_in = nc.dram_tensor("pickmat", [4, 4], F32, kind="ExternalInput").ap()
    mask_in = nc.dram_tensor("masks", [4, 128, QT], F16, kind="ExternalInput").ap()
    wq_in = nc.dram_tensor("wq_sel", [D, 2 * E], F32, kind="ExternalInput").ap()
    wk_in = nc.dram_tensor("wk_sel", [D, E], F32, kind="ExternalInput").ap()
    wv_in = nc.dram_tensor("wv_sel", [D, E], F32, kind="ExternalInput").ap()
    bq_in = nc.dram_tensor("bq_sel", [128, 2], F32, kind="ExternalInput").ap()
    bk_in = nc.dram_tensor("bk_sel", [128, 1], F32, kind="ExternalInput").ap()
    bv_in = nc.dram_tensor("bv_sel", [1, 128], F32, kind="ExternalInput").ap()
    ln1_in = nc.dram_tensor("ln1c", [128, 16], F32, kind="ExternalInput").ap()

    pl_out = nc.dram_tensor("poslist", [2, S_SEL], F32, kind="ExternalOutput").ap()
    srg_out = nc.dram_tensor("srg", [128, 128], F32, kind="ExternalOutput").ap()
    selx_out = nc.dram_tensor("selx", [S_SEL, D], F16, kind="ExternalOutput").ap()
    zsel_out = nc.dram_tensor("z_sel", [2 * E, S_SEL], F16, kind="ExternalOutput").ap()
    selbf = nc.dram_tensor("selbf", [S_SEL, D], F16, kind="Internal")
    rstdd = nc.dram_tensor("rstdd", [1, S_SEL], F32, kind="Internal")

    sch = Sched(paranoid=paranoid, serialize_dma=serialize_dma)
    with ExitStack() as ctx:
        ee = ctx.enter_context
        sct = ee(nc.sbuf_tensor("sct", [128, 128], F32))
        post = ee(nc.sbuf_tensor("post", [128, 128], F32))
        onesm = ee(nc.sbuf_tensor("onesm_s", [128, 128], F32))
        ones_row = ee(nc.sbuf_tensor("ones_row", [1, 128], F32))
        ones16c = ee(nc.sbuf_tensor("ones16c", [128, 1], F16))
        tril = ee(nc.sbuf_tensor("tril", [128, 128], F16))
        siota = ee(nc.sbuf_tensor("siota_s", [128, 1024], F16))
        rott = ee(nc.sbuf_tensor("rott", [128, 128], F16))
        invf = ee(nc.sbuf_tensor("invf_s", [128, 1], F32))
        ident = ee(nc.sbuf_tensor("ident_s", [128, 128], F32))
        pick = ee(nc.sbuf_tensor("pick", [4, 4], F32))
        mask_s = ee(nc.sbuf_tensor("mask_s", [128, 4 * QT], F16))
        lohi = ee(nc.sbuf_tensor("lohi", [128, 4], F32))
        mid = ee(nc.sbuf_tensor("mid", [128, 2], F32))
        cmp_t = ee(nc.sbuf_tensor("cmp_t", [128, 128], F32))
        cnt = ee(nc.sbuf_tensor("cnt", [128, 2], F32))
        cntR = ee(nc.sbuf_tensor("cntR", [128, 2], F32))
        ge_t = ee(nc.sbuf_tensor("ge_t", [128, 2], mybir.dt.uint8))
        gn_t = ee(nc.sbuf_tensor("gn_t", [128, 2], mybir.dt.uint8))
        m4 = ee(nc.sbuf_tensor("m4", [128, 128], F32))
        r4 = ee(nc.sbuf_tensor("r4", [4, 1], F32))
        maskf = ee(nc.sbuf_tensor("maskf", [128, 128], F32))
        mask16 = ee(nc.sbuf_tensor("mask16", [128, 128], F16))
        srank = ee(nc.sbuf_tensor("srank", [128, 128], F32))
        srg_s = ee(nc.sbuf_tensor("srg_s", [128, 128], F32))
        scan_a = ee(nc.sbuf_tensor("scan_a", [1, 64], F32))
        scan_b = ee(nc.sbuf_tensor("scan_b", [1, 64], F32))
        ms1 = ee(nc.sbuf_tensor("ms1", [128, 128], F32))
        ms2 = ee(nc.sbuf_tensor("ms2", [128, 128], F32))
        oh = [ee(nc.sbuf_tensor(f"oh{i}", [128, 1024], F16)) for i in range(2)]
        pos_s = ee(nc.sbuf_tensor("pos_s", [1, S_SEL], F32))
        sc_r = ee(nc.sbuf_tensor("sc_r", [1, S_SEL], F32))
        posB = ee(nc.sbuf_tensor("posB", [128, S_SEL], F32))
        posT = ee(nc.sbuf_tensor("posT", [128, 16], F32))
        idx_u = ee(nc.sbuf_tensor("idx_u", [128, 16], U32))
        gx = [ee(nc.sbuf_tensor(f"gx{i}", [128, D], F16)) for i in range(2)]
        rst = ee(nc.sbuf_tensor("rst", [128, 80], F32))
        rstrow = ee(nc.sbuf_tensor("rstrow", [1, S_SEL], F32))
        rstdB = ee(nc.sbuf_tensor("rstdB", [128, S_SEL], F32))
        tmpT = [ee(nc.sbuf_tensor(f"tmpT{i}", [128, D], F16)) for i in range(2)]
        hT = ee(nc.sbuf_tensor("hT", [128, DC * S_SEL], F16))
        wq16 = ee(nc.sbuf_tensor("wq16", [128, DC * 2 * E], F16))
        wk16 = ee(nc.sbuf_tensor("wk16", [128, DC * E], F16))
        wv16 = ee(nc.sbuf_tensor("wv16", [128, DC * E], F16))
        bq_s = ee(nc.sbuf_tensor("bq_s", [128, 2], F32))
        bk_s = ee(nc.sbuf_tensor("bk_s", [128, 1], F32))
        bv_row = ee(nc.sbuf_tensor("bv_row", [1, 128], F32))
        bvB = ee(nc.sbuf_tensor("bvB", [128, 128], F32))
        qT = [ee(nc.sbuf_tensor(f"qT{i}", [128, S_SEL], F16)) for i in range(2)]
        kT = ee(nc.sbuf_tensor("kT", [128, S_SEL], F16))
        v16 = ee(nc.sbuf_tensor("v16", [128, S_SEL], F16))
        angB = ee(nc.sbuf_tensor("angB", [128, QT], F32))
        redu = ee(nc.sbuf_tensor("redu", [128, QT], F32))
        cosB = ee(nc.sbuf_tensor("cosB", [128, QT], F32))
        sinB = ee(nc.sbuf_tensor("sinB", [128, QT], F32))
        sq_t = [ee(nc.sbuf_tensor(f"sq_t{i}", [128, QT], F16)) for i in range(2)]
        ctile = ee(nc.sbuf_tensor("ctile", [1, 8], F32))
        cm = ee(nc.sbuf_tensor("cm", [1, 4], F32))
        lnbias = ee(nc.sbuf_tensor("lnbias", [1, 1], F32))
        cbias = ee(nc.sbuf_tensor("cbias", [128, 2], F32))
        pT = [ee(nc.sbuf_tensor(f"pT{i}", [128, QT], F16)) for i in range(2)]
        zn_s = [ee(nc.sbuf_tensor(f"zn_s{i}", [128, QT], F16)) for i in range(2)]
        lrec = ee(nc.sbuf_tensor("lrec", [1, QT], F32))
        lrecB = ee(nc.sbuf_tensor("lrecB", [128, QT], F32))
        epsR = ee(nc.sbuf_tensor("epsR", [128, 1], F32))
        ps = [ee(nc.psum_tensor(f"ps{i}", [128, QT], F32)) for i in range(8)]
        block = ee(nc.Block())

        lds = []
        for (dst, src) in ((sct, sct_in), (post, pos_in), (onesm, onesm_in),
                           (ones_row, ones_in), (tril, tril_in), (siota, siota_in),
                           (rott, rot_in), (invf, invf_in), (ident, ident_in),
                           (pick, pick_in), (bq_s, bq_in), (bk_s, bk_in),
                           (bv_row, bv_in)):
            lds.append(sch.op("gpsimd", lambda g, dst=dst, src=src:
                              g.dma_start(out=dst[:, :], in_=src), dma=True))
        ld_mask = sch.op("gpsimd", lambda g: g.dma_start(
            out=mask_s[:, :].rearrange("p (c q) -> p c q", c=4),
            in_=mask_in.rearrange("c p q -> p c q")), dma=True)
        ld_wq = sch.op("gpsimd", lambda g: g.dma_start(
            out=wq16[:, :].rearrange("p (dc e) -> p dc e", dc=DC),
            in_=wq_in.rearrange("(dc p) e -> p dc e", p=128)), dma=True)
        ld_wk = sch.op("gpsimd", lambda g: g.dma_start(
            out=wk16[:, :].rearrange("p (dc e) -> p dc e", dc=DC),
            in_=wk_in.rearrange("(dc p) e -> p dc e", p=128)), dma=True)
        ld_wv = sch.op("gpsimd", lambda g: g.dma_start(
            out=wv16[:, :].rearrange("p (dc e) -> p dc e", dc=DC),
            in_=wv_in.rearrange("(dc p) e -> p dc e", p=128)), dma=True)
        ms_ones16 = sch.op("vector", lambda v: v.memset(ones16c[:, :], 1.0))
        ms_eps = sch.op("vector", lambda v: v.memset(epsR[:, :], EPS_RMS))
        ms_lnb = sch.op("vector", lambda v: v.memset(lnbias[:, :], -LN_ATTN_SCALE))
        ln1c = ee(nc.sbuf_tensor("ln1c_s", [128, 16], F32))
        ld_ln1 = sch.op("gpsimd", lambda g: g.dma_start(
            out=ln1c[:, :], in_=ln1_in), dma=True)

        i_mn0 = sch.op("vector", lambda v: v.tensor_reduce(
            m4[:, 0:1], sct[:, 0:64], AX.X, OP.min), deps=[lds[0]])
        i_mx0 = sch.op("vector", lambda v: v.tensor_reduce(
            m4[:, 2:3], sct[:, 0:64], AX.X, OP.max))
        i_mn1 = sch.op("vector", lambda v: v.tensor_reduce(
            m4[:, 1:2], sct[:, 64:128], AX.X, OP.min))
        i_mx1 = sch.op("vector", lambda v: v.tensor_reduce(
            m4[:, 3:4], sct[:, 64:128], AX.X, OP.max))
        i_ng = sch.op("vector", lambda v: v.tensor_scalar_mul(
            m4[:, 2:4], m4[:, 2:4], -1.0), deps=[i_mx0, i_mx1])
        tr4 = sch.op("tensor", lambda t: t.transpose(
            ps[0][:, 0:128], m4[:, :], ident[:, :]),
            deps=[i_mn0, i_mn1, i_ng, lds[8]])
        rmn = sch.op("vector", lambda v: v.tensor_reduce(
            r4[0:4, 0:1], ps[0][0:4, 0:128], AX.X, OP.min), deps=[tr4])
        rowm = sch.op("tensor", lambda t: t.matmul(
            ps[1][0:1, 0:4], r4[0:4, 0:1], pick[0:4, 0:4],
            start=True, stop=True), deps=[rmn, lds[9]])
        row4e = sch.op("vector", lambda v: v.tensor_copy(
            scan_a[0:1, 0:4], ps[1][0:1, 0:4]), deps=[rowm])
        rep = sch.op("tensor", lambda t: t.matmul(
            ps[2][:, 0:4], ones_row[:, :], scan_a[0:1, 0:4],
            start=True, stop=True), deps=[row4e, lds[3]])
        cp_lh = sch.op("vector", lambda v: v.tensor_copy(lohi[:, :], ps[2][:, 0:4]),
                       deps=[rep])
        h1 = sch.op("vector", lambda v: v.tensor_scalar_mul(
            mid[:, 0:2], lohi[:, 2:4], -1.0), deps=[cp_lh])
        h2 = sch.op("vector", lambda v: v.tensor_max(
            mid[:, 0:2], mid[:, 0:2], lohi[:, 2:4]), deps=[h1])
        h3 = sch.op("vector", lambda v: v.tensor_scalar(
            mid[:, 0:2], mid[:, 0:2], 1e-4, 1e-20, OP.mult, OP.add), deps=[h2])
        hfin = sch.op("vector", lambda v: v.tensor_add(
            lohi[:, 2:4], lohi[:, 2:4], mid[:, 0:2]), deps=[h3])
        prev = hfin
        prev_mm = rep
        prev_gno = cp_lh
        for it in range(BISECT_ITERS):
            m1 = sch.op("vector", lambda v: v.tensor_scalar_mul(
                mid[:, :], lohi[:, 0:2], 0.5), deps=[prev])
            m2 = sch.op("vector", lambda v: v.scalar_tensor_tensor(
                cnt[:, :], lohi[:, 2:4], 0.5, mid[:, :], OP.mult, OP.add),
                deps=[m1])
            c0 = sch.op("vector", lambda v: v.tensor_scalar(
                cmp_t[:, 0:64], sct[:, 0:64], cnt[:, 0:1], 0.0, OP.is_ge,
                OP.add, accum_out=cntR[:, 0:1]), deps=[m2])
            c1 = sch.op("vector", lambda v: v.tensor_scalar(
                cmp_t[:, 64:128], sct[:, 64:128], cnt[:, 1:2], 0.0, OP.is_ge,
                OP.add, accum_out=cntR[:, 1:2]), deps=[c0])
            mmc = sch.op("tensor", lambda t: t.matmul(
                ps[3][:, 0:2], onesm[:, :], cntR[:, :], start=True, stop=True),
                deps=[c1, lds[2], prev_mm, prev_gno])
            prev_mm = mmc
            geo = sch.op("vector", lambda v: v.tensor_scalar(
                ge_t[:, :], ps[3][:, 0:2], float(SELK), None, OP.is_ge), deps=[mmc])
            gno = sch.op("vector", lambda v: v.tensor_scalar(
                gn_t[:, :], ps[3][:, 0:2], float(SELK), None, OP.is_lt), deps=[geo])
            prev_gno = gno
            s_lo = sch.op("vector", lambda v: v.copy_predicated(
                lohi[:, 0:2], ge_t[:, :], cnt[:, :]), deps=[geo])
            s_hi = sch.op("vector", lambda v: v.copy_predicated(
                lohi[:, 2:4], gn_t[:, :], cnt[:, :]), deps=[gno, s_lo])
            prev = s_hi
        thr_ready = prev

        mks = []
        srk = []
        bca_prev = None
        cpy_prev = None
        for b in range(2):
            mk = sch.op("vector", lambda v, b=b: v.tensor_scalar(
                maskf[:, 64 * b:64 * b + 64], sct[:, 64 * b:64 * b + 64],
                lohi[:, b:b + 1], None, OP.is_ge), deps=[thr_ready])
            mk16 = sch.op("vector", lambda v, b=b: v.tensor_copy(
                mask16[:, 64 * b:64 * b + 64], maskf[:, 64 * b:64 * b + 64]),
                deps=[mk])
            cumm = sch.op("tensor", lambda t, b=b: t.matmul(
                ps[4 + b][:, 0:64], tril[:, :], mask16[:, 64 * b:64 * b + 64],
                start=True, stop=False), deps=[mk16, lds[4], prev_mm])
            cs = sch.op("tensor", lambda t, b=b: t.matmul(
                ps[6][0:1, 64 * b:64 * b + 64], ones16c[:, :],
                mask16[:, 64 * b:64 * b + 64],
                start=True, stop=True), deps=[mk16, ms_ones16, cpy_prev])
            cpy = sch.op("vector", lambda v, b=b: v.tensor_copy(
                scan_a[0:1, :], ps[6][0:1, 64 * b:64 * b + 64]),
                deps=[cs, bca_prev])
            cpy_prev = cpy
            pr = cpy
            cur, oth = scan_a, scan_b
            for sh in (1, 2, 4, 8, 16, 32):
                cp1 = sch.op("vector", lambda v, cur=cur, oth=oth, sh=sh:
                             v.tensor_copy(oth[0:1, 0:sh], cur[0:1, 0:sh]), deps=[pr])
                ad1 = sch.op("vector", lambda v, cur=cur, oth=oth, sh=sh:
                             v.tensor_add(oth[0:1, sh:64], cur[0:1, sh:64],
                                          cur[0:1, 0:64 - sh]), deps=[cp1])
                pr = ad1
                cur, oth = oth, cur
            z0 = sch.op("vector", lambda v, oth=oth: v.memset(oth[0:1, 0:1], 0.0),
                        deps=[pr])
            sh1 = sch.op("vector", lambda v, cur=cur, oth=oth: v.tensor_copy(
                oth[0:1, 1:64], cur[0:1, 0:63]), deps=[z0])
            bca = sch.op("tensor", lambda t, b=b, oth=oth: t.matmul(
                ps[4 + b][:, 0:64], ones_row[:, :], oth[0:1, 0:64],
                start=False, stop=True), deps=[sh1])
            bca_prev = bca
            sr_a = sch.op("vector", lambda v, b=b: v.tensor_scalar_add(
                cmp_t[:, 64 * b:64 * b + 64], ps[4 + b][:, 0:64],
                float(1e6)), deps=[bca, mk])
            sr = sch.op("vector", lambda v, b=b: v.scalar_tensor_tensor(
                srank[:, 64 * b:64 * b + 64], maskf[:, 64 * b:64 * b + 64],
                -1000001.0, cmp_t[:, 64 * b:64 * b + 64], OP.mult, OP.add),
                deps=[sr_a])
            geb = sch.op("vector", lambda v, b=b: v.tensor_scalar(
                cmp_t[:, 64 * b:64 * b + 64], srank[:, 64 * b:64 * b + 64],
                float(1024), None, OP.is_ge), deps=[sr])
            cl = sch.op("vector", lambda v, b=b: v.scalar_tensor_tensor(
                srank[:, 64 * b:64 * b + 64], cmp_t[:, 64 * b:64 * b + 64], 1e6,
                srank[:, 64 * b:64 * b + 64], OP.mult, OP.add), deps=[geb])
            mks.append(mk)
            srk.append(cl)

        sg1 = sch.op("vector", lambda v: v.tensor_copy(
            srg_s[:, 0:64], srank[:, 0:64]), deps=[srk[0]])
        sg2 = sch.op("vector", lambda v: v.tensor_scalar(
            srg_s[:, 64:128], srank[:, 64:128], 1024.0, None, OP.add),
            deps=[srk[1]])
        st_sg = sch.op("sync", lambda q: q.dma_start(
            out=srg_out, in_=srg_s[:, :]), deps=[sg1, sg2], dma=True)

        MOFF = 10 * S_SEL
        mview = hT[:, MOFF:MOFF + 96 * 128].rearrange(
            "p (col c) -> p c col", c=96)
        mz = sch.op("vector", lambda v: v.memset(hT[:, MOFF:MOFF + 96 * 128], 0.0))
        mc = sch.op("vector", lambda v: v.tensor_copy(ms1[:, :], post[:, :]),
                    deps=[lds[1]])
        for kpow in (4096.0, 2048.0, 1024.0, 512.0, 256.0, 128.0, 64.0):
            mb = sch.op("vector", lambda v, kpow=kpow: v.tensor_scalar(
                ms2[:, :], ms1[:, :], kpow, None, OP.is_ge), deps=[mc])
            mc = sch.op("vector", lambda v, kpow=kpow: v.scalar_tensor_tensor(
                ms1[:, :], ms2[:, :], -kpow, ms1[:, :], OP.mult, OP.add),
                deps=[mb])
        m1 = sch.op("vector", lambda v: v.tensor_sub(
            ms2[:, :], post[:, :], ms1[:, :]), deps=[mc])
        m2 = sch.op("vector", lambda v: v.tensor_scalar_mul(
            ms2[:, :], ms2[:, :], 1.0 / 64.0), deps=[m1])
        m3 = sch.op("vector", lambda v: v.tensor_copy(
            mview[:, 0, 0:128], ms2[:, :]), deps=[m2, mz])
        m4_ = sch.op("vector", lambda v: v.tensor_copy(
            mview[:, 32, 0:128], ms1[:, :]), deps=[mc, mz])
        m5 = sch.op("vector", lambda v: v.tensor_copy(
            mview[:, 64, 0:128], sct[:, :]), deps=[lds[0], mz])
        mp = [m3, m4_, m5]
        prev_oh_mm = {}
        for b in range(2):
            for f in range(64):
                col = 64 * b + f
                g = b * 64 + f
                gb, gf = divmod(g - 2, 64)
                oho = sch.op("vector", lambda v, b=b, f=f, col=col: v.tensor_scalar(
                    oh[f % 2][:, :], siota[:, :], srank[:, col:col + 1], None,
                    OP.is_equal), deps=[srk[b], lds[5],
                                        prev_oh_mm.get((gb, gf))])
                last = None
                for half in range(2):
                    last = sch.op("tensor", lambda t, b=b, f=f, half=half, col=col:
                                  t.matmul(ps[2 * b + half][0:96, 0:512],
                                           hT[:, MOFF + col * 96:
                                              MOFF + (col + 1) * 96],
                                           oh[f % 2][:, 512 * half:512 * half + 512],
                                           start=(f == 0), stop=(f == 63)),
                                  deps=[oho] + mp + ([] if f > 0 else [prev_mm]))
                prev_oh_mm[(b, f)] = last
        ev_pl = []
        for b in range(2):
            for half in range(2):
                sl = slice(1024 * b + 512 * half, 1024 * b + 512 * half + 512)
                pb_ = ps[2 * b + half]
                dep0 = [prev_oh_mm[(b, 63)]]
                e1 = sch.op("vector", lambda v, sl=sl, pb_=pb_: v.tensor_scalar(
                    pos_s[0:1, sl], pb_[0:1, 0:512], 64.0, None, OP.mult),
                    deps=dep0)
                e2 = sch.op("vector", lambda v, sl=sl, pb_=pb_: v.tensor_add(
                    pos_s[0:1, sl], pos_s[0:1, sl], pb_[32:33, 0:512]), deps=[e1])
                e3 = sch.op("vector", lambda v, sl=sl, pb_=pb_: v.tensor_copy(
                    sc_r[0:1, sl], pb_[64:65, 0:512]), deps=[e2])
                ev_pl.append(e3)
        st_pl = sch.op("scalar", lambda s: s.dma_start(
            out=pl_out[0:1, :], in_=pos_s[:, :]), deps=ev_pl, dma=True)
        st_pl2 = sch.op("scalar", lambda s: s.dma_start(
            out=pl_out[1:2, :], in_=sc_r[:, :]), deps=ev_pl, dma=True)

        pb_ev = []
        pbm_prev = None
        for j in range(4):
            pbm = sch.op("tensor", lambda t, j=j: t.matmul(
                ps[4][:, :], ones_row[:, :], pos_s[0:1, j * QT:(j + 1) * QT],
                start=True, stop=True), deps=ev_pl + [pbm_prev])
            pbe = sch.op("vector", lambda v, j=j: v.tensor_copy(
                posB[:, j * QT:(j + 1) * QT], ps[4][:, :]), deps=[pbm])
            pbm_prev = pbe
            pb_ev.append(pbe)
        trp = []
        trp_prev = None
        for c16 in range(16):
            tro = sch.op("tensor", lambda t, c16=c16: t.transpose(
                ps[5][:, 0:128], posB[:, c16 * 128:(c16 + 1) * 128], ident[:, :]),
                deps=[pb_ev[c16 // 4], trp_prev])
            tre = sch.op("vector", lambda v, c16=c16: v.tensor_copy(
                posT[:, c16:c16 + 1], ps[5][:, 0:1]), deps=[tro])
            trp_prev = tre
            trp.append(tre)
        adj = sch.op("vector", lambda v: v.tensor_scalar(
            posT[:, 8:16], posT[:, 8:16], float(T), None, OP.add), deps=trp)
        toi = sch.op("vector", lambda v: v.tensor_copy(idx_u[:, :], posT[:, :]),
                     deps=[adj])

        g_sel_st = []
        g_bf_st = []
        rst_st = []
        gpo = []
        prev_sq = None
        for c16 in range(16):
            gth = sch.op("gpsimd", lambda g, c16=c16: g.indirect_dma_start(
                out=gx[c16 % 2][:, :], out_offset=None, in_=x1f,
                in_offset=IndirectOffsetOnAxis(idx_u[:, c16:c16 + 1], 0)),
                deps=[toi, gpo[-2] if len(gpo) >= 2 else None], dma=True)
            st1_ = sch.op("scalar", lambda s, c16=c16: s.dma_start(
                out=selx_out[c16 * 128:(c16 + 1) * 128, :], in_=gx[c16 % 2][:, :]),
                deps=[gth], dma=True)
            g_sel_st.append(st1_)
            sq = sch.op("scalar", lambda s, c16=c16: s.activation(
                out=tmpT[0][:, :], in_=gx[c16 % 2][:, :], func=AF.Square,
                accum_out=rst[:, c16:c16 + 1]), deps=[gth, prev_sq])
            sq_last = sq
            ms_ = sch.op("vector", lambda v, c16=c16: v.tensor_scalar(
                rst[:, 16 + c16:17 + c16], rst[:, c16:c16 + 1], 1.0 / D, None,
                OP.mult), deps=[sq])
            lnv = sch.op("scalar", lambda s, c16=c16: s.activation(
                out=rst[:, 32 + c16:33 + c16], in_=rst[:, 16 + c16:17 + c16],
                func=AF.Ln, bias=epsR[:, 0:1]), deps=[ms_, ms_eps])
            rsd = sch.op("scalar", lambda s, c16=c16: s.activation(
                out=rst[:, 48 + c16:49 + c16], in_=rst[:, 32 + c16:33 + c16],
                func=AF.Exp, scale=-0.5), deps=[lnv])
            prev_sq = rsd
            rstst = sch.op("scalar", lambda s, c16=c16: s.dma_start(
                out=rstdd.ap()[0:1, c16 * 128:(c16 + 1) * 128],
                in_=rst[:, 48 + c16:49 + c16]),
                deps=[rsd], dma=True)
            rst_st.append(rstst)
            stbf = sch.op("scalar", lambda s, c16=c16: s.dma_start(
                out=selbf.ap()[c16 * 128:(c16 + 1) * 128, :], in_=gx[c16 % 2][:, :]),
                deps=[gth], dma=True)
            g_bf_st.append(stbf)
            gpo.append(sch.op("vector", lambda v, c16=c16: v.engine_nop(),
                              deps=[st1_, sq, stbf]))
        ld_rr = sch.op("sync", lambda q: q.dma_start(out=rstrow[:, :], in_=rstdd.ap()),
                       deps=rst_st, dma=True)
        rb_ev = []
        rbm_prev = None
        for j in range(4):
            rbm = sch.op("tensor", lambda t, j=j: t.matmul(
                ps[6][:, :], ones_row[:, :], rstrow[0:1, j * QT:(j + 1) * QT],
                start=True, stop=True), deps=[ld_rr, rbm_prev])
            rbe = sch.op("vector", lambda v, j=j: v.tensor_copy(
                rstdB[:, j * QT:(j + 1) * QT], ps[6][:, :]), deps=[rbm])
            rbm_prev = rbe
            rb_ev.append(rbe)

        hT_done = []
        for dc in range(DC):
            trd = sch.op("sync", lambda q, dc=dc: q.dma_start_transpose(
                tmpT[dc % 2][:, :], selbf.ap()[:, dc * 128:(dc + 1) * 128]),
                deps=[g_bf_st[-1], sq_last,
                      hT_done[-2] if len(hT_done) >= 2 else None],
                dma=True)
            sc_ = sch.op("vector", lambda v, dc=dc: v.scalar_tensor_tensor(
                hT[:, dc * S_SEL:(dc + 1) * S_SEL], tmpT[dc % 2][:, :],
                ln1c[:, dc:dc + 1], rstdB[:, :], OP.mult, OP.mult),
                deps=[trd, ld_ln1] + rb_ev
                + ([prev_oh_mm[(1, 63)]] if dc >= 10 else []))
            hT_done.append(sc_)

        bvm = sch.op("tensor", lambda t: t.matmul(
            ps[7][:, 0:128], ones_row[:, :], bv_row[0:1, :], start=True, stop=True),
            deps=[lds[12], lds[3]])
        bve = sch.op("vector", lambda v: v.tensor_copy(bvB[:, :], ps[7][:, 0:128]),
                     deps=[bvm])
        qkv_ev = []
        for head in range(2):
            for stile in range(4):
                bank = head * 4 + stile
                last = None
                for dc in range(DC):
                    last = sch.op("tensor", lambda t, head=head, stile=stile, dc=dc,
                                  bank=bank:
                                  t.matmul(ps[bank][:, :],
                                           wq16[:, (dc * 2 + head) * E:
                                                (dc * 2 + head + 1) * E],
                                           hT[:, dc * S_SEL + stile * QT:
                                              dc * S_SEL + (stile + 1) * QT],
                                           start=(dc == 0), stop=(dc == DC - 1)),
                                  deps=[hT_done[dc]] + ([ld_wq, bve] if dc == 0 else []))
                ev = sch.op("vector", lambda v, head=head, stile=stile, bank=bank:
                            v.tensor_scalar_add(
                                qT[head][:, stile * QT:(stile + 1) * QT],
                                ps[bank][:, :], bq_s[:, head:head + 1]),
                            deps=[last, lds[10]])
                qkv_ev.append(ev)
        for stile in range(4):
            bank = stile
            last = None
            for dc in range(DC):
                last = sch.op("tensor", lambda t, stile=stile, dc=dc, bank=bank:
                              t.matmul(ps[bank][:, :],
                                       wk16[:, dc * E:(dc + 1) * E],
                                       hT[:, dc * S_SEL + stile * QT:
                                          dc * S_SEL + (stile + 1) * QT],
                                       start=(dc == 0), stop=(dc == DC - 1)),
                              deps=[qkv_ev[stile], ld_wk] if dc == 0 else [])
            ev = sch.op("vector", lambda v, stile=stile, bank=bank:
                        v.tensor_scalar_add(kT[:, stile * QT:(stile + 1) * QT],
                                            ps[bank][:, :], bk_s[:, 0:1]),
                        deps=[last, lds[11]])
            qkv_ev.append(ev)
        v_ev = []
        for sc16 in range(16):
            bank = 4 + sc16 % 2
            last = None
            for dc in range(DC):
                last = sch.op("tensor", lambda t, sc16=sc16, dc=dc, bank=bank:
                              t.matmul(ps[bank][:, 0:128],
                                       hT[:, dc * S_SEL + sc16 * 128:
                                          dc * S_SEL + (sc16 + 1) * 128],
                                       wv16[:, dc * E:(dc + 1) * E],
                                       start=(dc == 0), stop=(dc == DC - 1)),
                              deps=([ld_wv, qkv_ev[4 + sc16 % 2],
                                     v_ev[sc16 - 2] if sc16 >= 2 else None]
                                    if dc == 0 else []))
            ev = sch.op("vector", lambda v, sc16=sc16, bank=bank:
                        v.tensor_add(v16[:, sc16 * 128:(sc16 + 1) * 128],
                                     ps[bank][:, 0:128], bvB[:, :]),
                        deps=[last, bve])
            v_ev.append(ev)

        INV2PI = float(1.0 / (2 * np.pi))
        C1 = 6.28125
        C2 = float(2 * np.pi - 6.28125)
        itmp = ee(nc.sbuf_tensor("itmp", [128, QT], mybir.dt.int32))
        ang2 = ee(nc.sbuf_tensor("ang2", [128, QT], F32))

        def build_trig(out_tile, base_op, shift, stile):
            if shift != 0.0:
                a0 = sch.op("vector", lambda v: v.tensor_scalar(
                    ang2[:, :], angB[:, :], shift, None, OP.add), deps=[base_op])
                srct = ang2
            else:
                a0 = base_op
                srct = angB
            n0 = sch.op("vector", lambda v, srct=srct: v.tensor_scalar(
                redu[:, :], srct[:, :], INV2PI, None, OP.mult), deps=[a0])
            n1 = sch.op("vector", lambda v: v.tensor_copy(itmp[:, :], redu[:, :]),
                        deps=[n0])
            n2 = sch.op("vector", lambda v: v.tensor_copy(redu[:, :], itmp[:, :]),
                        deps=[n1])
            r1 = sch.op("vector", lambda v, srct=srct: v.scalar_tensor_tensor(
                out_tile[:, :], redu[:, :], -C1, srct[:, :], OP.mult, OP.add),
                deps=[n2])
            r2 = sch.op("vector", lambda v: v.scalar_tensor_tensor(
                out_tile[:, :], redu[:, :], -C2, out_tile[:, :], OP.mult, OP.add),
                deps=[r1])
            f1 = sch.op("vector", lambda v: v.tensor_scalar(
                redu[:, :], out_tile[:, :], float(np.pi), None, OP.is_gt), deps=[r2])
            f2 = sch.op("vector", lambda v: v.scalar_tensor_tensor(
                out_tile[:, :], redu[:, :], float(-2 * np.pi), out_tile[:, :],
                OP.mult, OP.add), deps=[f1])
            so = sch.op("scalar", lambda s: s.activation(
                out=out_tile[:, :], in_=out_tile[:, :], func=AF.Sin), deps=[f2])
            return so

        rope_done = []
        rp_prev = None
        for stile in range(4):
            ango = sch.op("vector", lambda v, stile=stile: v.tensor_scalar(
                angB[:, :], posB[:, stile * QT:(stile + 1) * QT], invf[:, 0:1],
                None, OP.mult), deps=pb_ev + [lds[7], rp_prev])
            sin_done = build_trig(sinB, ango, 0.0, stile)
            cos_done = build_trig(cosB, ango, float(np.pi / 2), stile)
            for tile in (qT[0], qT[1], kT):
                rmm = sch.op("tensor", lambda t, tile=tile, stile=stile: t.matmul(
                    ps[6][:, :], rott[:, :], tile[:, stile * QT:(stile + 1) * QT],
                    start=True, stop=True),
                    deps=[lds[6], sin_done, cos_done, rp_prev] + qkv_ev)
                t1 = sch.op("vector", lambda v, tile=tile, stile=stile: v.tensor_mul(
                    ang2[:, :], tile[:, stile * QT:(stile + 1) * QT], cosB[:, :]),
                    deps=[sin_done, cos_done, rp_prev])
                t2 = sch.op("vector", lambda v: v.tensor_mul(
                    redu[:, :], ps[6][:, :], sinB[:, :]), deps=[rmm, t1])
                t3 = sch.op("vector", lambda v, tile=tile, stile=stile: v.tensor_add(
                    tile[:, stile * QT:(stile + 1) * QT], ang2[:, :], redu[:, :]),
                    deps=[t2])
                rp_prev = t3
                rope_done.append(t3)

        red_ops = {}
        for jj, tile in enumerate((qT[0], kT, qT[1])):
            for st4 in range(4):
                u = jj * 4 + st4
                sqo = sch.op("scalar", lambda s, tile=tile, st4=st4, u=u:
                             s.activation(out=sq_t[u % 2][:, :],
                                          in_=tile[:, st4 * QT:(st4 + 1) * QT],
                                          func=AF.Square),
                             deps=rope_done + [red_ops.get(u - 2)])
                mmo = sch.op("tensor", lambda t, u=u: t.matmul(
                    ps[u % 4][0:1, 0:QT], ones16c[:, :], sq_t[u % 2][:, :],
                    start=True, stop=True),
                    deps=[sqo, ms_ones16, red_ops.get(u - 4),
                          v_ev[-1] if u == 0 else None])
                rd = sch.op("vector", lambda v, u=u: v.reduce_max(
                    ctile[0:1, u % 4:u % 4 + 1], ps[u % 4][0:1, 0:QT], AX.X),
                    deps=[mmo])
                slot = jj
                if st4 == 0:
                    acc = sch.op("vector", lambda v, slot=slot, u=u: v.tensor_copy(
                        cm[0:1, slot:slot + 1], ctile[0:1, u % 4:u % 4 + 1]),
                        deps=[rd])
                else:
                    acc = sch.op("vector", lambda v, slot=slot, u=u: v.tensor_max(
                        cm[0:1, slot:slot + 1], cm[0:1, slot:slot + 1],
                        ctile[0:1, u % 4:u % 4 + 1]), deps=[rd])
                red_ops[u] = acc
        cb_ready = []
        cb_prev = None
        for qh in range(2):
            pr = sch.op("vector", lambda v, qh=qh: v.tensor_mul(
                ctile[0:1, 4 + qh:5 + qh], cm[0:1, 2 * qh:2 * qh + 1],
                cm[0:1, 1:2]), deps=[red_ops[11], cb_prev])
            ln_ = sch.op("scalar", lambda s, qh=qh: s.activation(
                out=ctile[0:1, 4 + qh:5 + qh], in_=ctile[0:1, 4 + qh:5 + qh],
                func=AF.Ln), deps=[pr])
            ex_ = sch.op("scalar", lambda s, qh=qh: s.activation(
                out=ctile[0:1, 4 + qh:5 + qh], in_=ctile[0:1, 4 + qh:5 + qh],
                func=AF.Exp, bias=lnbias[0:1, 0:1], scale=0.5), deps=[ln_, ms_lnb])
            ng = sch.op("vector", lambda v, qh=qh: v.tensor_scalar(
                ctile[0:1, 4 + qh:5 + qh], ctile[0:1, 4 + qh:5 + qh],
                -1.0, 8.0, OP.mult, OP.add), deps=[ex_])
            bc = sch.op("tensor", lambda t, qh=qh: t.matmul(
                ps[4 + qh][:, 0:1], ones_row[:, :], ctile[0:1, 4 + qh:5 + qh],
                start=True, stop=True), deps=[ng, v_ev[-1]])
            cb = sch.op("vector", lambda v, qh=qh: v.tensor_copy(
                cbias[:, qh:qh + 1], ps[4 + qh][:, 0:1]), deps=[bc])
            cb_ready.append(cb)
            cb_prev = cb


        zstores = []
        exp_g = {}
        pv_g = {}
        zn_g = {}
        rc_g = {}
        gci = 0
        gqi = 0
        lb_prev = None
        lc_prev = None
        for qh in range(2):
            for qsi in range(4):
                nk = 4 * qsi
                for kc in range(nk + 4):
                    mslot = kc - nk if kc >= nk else None
                    first = kc == 0
                    last_c = kc == nk + 3
                    sb = ps[gci % 2]
                    zb = ps[2 + gqi % 2]
                    lbank = ps[4 + gqi % 2]
                    d_ = []
                    if gci == 0:
                        d_ += [v_ev[-1]] + cb_ready
                    if gci >= 2:
                        d_.append(exp_g[gci - 2])
                    else:
                        d_ += [red_ops[11]]
                    if first:
                        if gqi >= 2:
                            d_.append(zn_g[gqi - 2])
                            d_.append(rc_g[gqi - 2])
                        else:
                            d_ += cb_ready
                    sc_ = sch.op("tensor", lambda t, sb=sb, qh=qh, kc=kc, qsi=qsi:
                                 t.matmul(sb[:, :], kT[:, kc * 128:(kc + 1) * 128],
                                          qT[qh][:, qsi * QT:(qsi + 1) * QT],
                                          start=True, stop=True), deps=d_)
                    ed = [sc_, cb_ready[qh]]
                    if gci >= 2:
                        ed.append(pv_g[gci - 2])
                    ex = sch.op("scalar", lambda s, sb=sb, gci=gci, qh=qh:
                                s.activation(out=pT[gci % 2][:, :], in_=sb[:, :],
                                             func=AF.Exp, bias=cbias[:, qh:qh + 1],
                                             scale=INV_SCALE), deps=ed)
                    exp_g[gci] = ex
                    pvd = [ex]
                    if mslot is not None:
                        mko = sch.op("vector", lambda v, gci=gci, mslot=mslot:
                                     v.tensor_mul(pT[gci % 2][:, :], pT[gci % 2][:, :],
                                                  mask_s[:, mslot * QT:(mslot + 1) * QT]),
                                     deps=[ex, ld_mask])
                        pvd = [mko]
                    sch.op("tensor", lambda t, zb=zb, kc=kc, gci=gci, first=first,
                           last_c=last_c:
                           t.matmul(zb[:, :], v16[:, kc * 128:(kc + 1) * 128],
                                    pT[gci % 2][:, :], start=first, stop=last_c),
                           deps=pvd)
                    pv = sch.op("tensor", lambda t, lbank=lbank, gci=gci, first=first,
                                last_c=last_c:
                                t.matmul(lbank[0:1, 0:QT], ones16c[:, :],
                                         pT[gci % 2][:, :], start=first, stop=last_c))
                    pv_g[gci] = pv
                    gci += 1
                    if last_c:
                        rcd = [pv, lb_prev]
                        rc = sch.op("vector", lambda v, lbank=lbank: v.reciprocal(
                            lrec[:, :], lbank[0:1, 0:QT]), deps=rcd)
                        rc_g[gqi] = rc
                        lb = sch.op("tensor", lambda t: t.matmul(
                            ps[6][:, :], ones_row[:, :], lrec[:, :],
                            start=True, stop=True), deps=[rc, lc_prev])
                        lb_prev = lb
                        lc = sch.op("vector", lambda v: v.tensor_copy(
                            lrecB[:, :], ps[6][:, :]), deps=[lb])
                        lc_prev = lc
                        znd = [lc]
                        if len(zstores) >= 2:
                            znd.append(zstores[-2])
                        zn = sch.op("vector", lambda v, gqi=gqi, zb=zb: v.tensor_mul(
                            zn_s[gqi % 2][:, :], zb[:, :], lrecB[:, :]), deps=znd)
                        zn_g[gqi] = zn
                        st = sch.op("scalar", lambda s, qh=qh, qsi=qsi, gqi=gqi:
                                    s.dma_start(
                                        out=zsel_out[qh * E:(qh + 1) * E,
                                                     qsi * QT:(qsi + 1) * QT],
                                        in_=zn_s[gqi % 2][:, :]), deps=[zn], dma=True)
                        zstores.append(st)
                        gqi += 1
        sch.op("sync", lambda q: q.nop(),
               deps=zstores + [st_pl, st_pl2, st_sg] + g_sel_st)
        sch.build(nc, ctx, block)
    return nc


def build_l4a(paranoid=True, serialize_dma=False):
    nc = bass.Bass("TRN2", target_bir_lowering=False, num_devices=NCORES)
    z_in = nc.dram_tensor("z16s", [H, E, 256], F16, kind="ExternalInput").ap()
    wo_in = nc.dram_tensor("wo_f", [D, D], F32, kind="ExternalInput").ap()
    sx_in = nc.dram_tensor("selx16", [256, D], F16, kind="ExternalInput").ap()
    ln2_in = nc.dram_tensor("ln2c", [128, 16], F32, kind="ExternalInput").ap()
    ones_in = nc.dram_tensor("ones128", [1, 128], F32, kind="ExternalInput").ap()
    h2_out = nc.dram_tensor("h2t", [D, 256], F16, kind="ExternalOutput").ap()
    r2_out = nc.dram_tensor("r2t", [D, 256], F32, kind="ExternalOutput").ap()

    sch = Sched(paranoid=paranoid, serialize_dma=serialize_dma)
    with ExitStack() as ctx:
        ee = ctx.enter_context
        z_s = ee(nc.sbuf_tensor("z_s", [128, H * 256], F16))
        wo16 = ee(nc.sbuf_tensor("wo16", [128, H * D], F16))
        sxT = ee(nc.sbuf_tensor("sxT", [128, DC * 256], F16))
        r2T = ee(nc.sbuf_tensor("r2T", [128, DC * 256], F32))
        h2T = ee(nc.sbuf_tensor("h2T", [128, DC * 256], F16))
        ln2c = ee(nc.sbuf_tensor("ln2c_s", [128, 16], F32))
        ones_row = ee(nc.sbuf_tensor("ones_row", [1, 128], F32))
        onesf_col = ee(nc.sbuf_tensor("onesf_col", [128, 1], F32))
        sq32 = ee(nc.sbuf_tensor("sq32", [128, 256], F32))
        msrow = ee(nc.sbuf_tensor("msrow", [1, 256], F32))
        rstdB = ee(nc.sbuf_tensor("rstdB", [128, 256], F32))
        epsR = ee(nc.sbuf_tensor("epsR", [1, 1], F32))
        ps = [ee(nc.psum_tensor(f"ps{i}", [128, QT], F32)) for i in range(6)]
        block = ee(nc.Block())

        ld_z = sch.op("gpsimd", lambda g: g.dma_start(
            out=z_s[:, :].rearrange("p (h s) -> p h s", h=H),
            in_=z_in.rearrange("h p s -> p h s")), dma=True)
        ld_wo = [sch.op("gpsimd", lambda g, hc=hc: g.dma_start(
            out=wo16[:, hc * D:(hc + 1) * D],
            in_=wo_in[hc * 128:(hc + 1) * 128, :]), dma=True) for hc in range(H)]
        ld_ln = sch.op("gpsimd", lambda g: g.dma_start(out=ln2c[:, :], in_=ln2_in),
                       dma=True)
        ld_on = sch.op("gpsimd", lambda g: g.dma_start(out=ones_row[:, :], in_=ones_in),
                       dma=True)
        ms_of = sch.op("vector", lambda v: v.memset(onesf_col[:, :], 1.0))
        ms_ep = sch.op("vector", lambda v: v.memset(epsR[:, :], EPS_RMS))

        sxT_ev = {}
        for dc in range(DC):
            tr = sch.op("sync", lambda q, dc=dc: q.dma_start_transpose(
                sxT[:, dc * 256:(dc + 1) * 256],
                sx_in[:, dc * 128:(dc + 1) * 128]), dma=True)
            sxT_ev[dc] = tr
        sxT_done = sxT_ev[DC - 1]

        r2_ev = {}
        st_r2 = []
        for dc in range(DC):
            bank = dc % 2
            last = None
            for h in range(H):
                d_ = []
                if h == 0:
                    d_ = [ld_z] + ld_wo
                    if dc >= 2:
                        d_.append(r2_ev[dc - 2])
                last = sch.op("tensor", lambda t, h=h, dc=dc, bank=bank: t.matmul(
                    ps[bank][:, 0:256],
                    wo16[:, h * D + dc * 128:h * D + (dc + 1) * 128],
                    z_s[:, h * 256:(h + 1) * 256],
                    start=(h == 0), stop=(h == H - 1)), deps=d_)
            ev = sch.op("vector", lambda v, dc=dc, bank=bank: v.tensor_add(
                r2T[:, dc * 256:(dc + 1) * 256], ps[bank][:, 0:256],
                sxT[:, dc * 256:(dc + 1) * 256]), deps=[last, sxT_ev[dc]])
            r2_ev[dc] = ev
            st = sch.op("sync", lambda q, dc=dc: q.dma_start(
                out=r2_out[dc * 128:(dc + 1) * 128, :],
                in_=r2T[:, dc * 256:(dc + 1) * 256]), deps=[ev], dma=True)
            st_r2.append(st)
        mm_prev = None
        for dc in range(DC):
            sq = sch.op("scalar", lambda s, dc=dc: s.activation(
                out=sq32[:, :], in_=r2T[:, dc * 256:(dc + 1) * 256],
                func=AF.Square), deps=[r2_ev[dc], mm_prev])
            mm = sch.op("tensor", lambda t, dc=dc: t.matmul(
                ps[4][0:1, 0:256], onesf_col[:, :], sq32[:, :],
                start=(dc == 0), stop=(dc == DC - 1)), deps=[sq, ms_of])
            mm_prev = mm
        cst1 = sch.op("vector", lambda v: v.tensor_scalar_mul(
            msrow[0:1, :], ps[4][0:1, 0:256], 1.0 / D), deps=[mm_prev])
        lnv = sch.op("scalar", lambda s: s.activation(
            out=msrow[0:1, :], in_=msrow[0:1, :], func=AF.Ln,
            bias=epsR[0:1, 0:1]), deps=[cst1, ms_ep])
        rsd = sch.op("scalar", lambda s: s.activation(
            out=msrow[0:1, :], in_=msrow[0:1, :], func=AF.Exp, scale=-0.5),
            deps=[lnv])
        bcm = sch.op("tensor", lambda t: t.matmul(
            ps[5][:, 0:256], ones_row[:, :], msrow[0:1, :], start=True, stop=True),
            deps=[rsd, ld_on])
        bce = sch.op("vector", lambda v: v.tensor_copy(rstdB[:, :], ps[5][:, 0:256]),
                     deps=[bcm])
        st_h2 = []
        for dc in range(DC):
            h2 = sch.op("vector", lambda v, dc=dc: v.scalar_tensor_tensor(
                h2T[:, dc * 256:(dc + 1) * 256], r2T[:, dc * 256:(dc + 1) * 256],
                ln2c[:, dc:dc + 1], rstdB[:, :], OP.mult, OP.mult),
                deps=[bce, ld_ln, r2_ev[dc]])
            st = sch.op("sync", lambda q, dc=dc: q.dma_start(
                out=h2_out[dc * 128:(dc + 1) * 128, :],
                in_=h2T[:, dc * 256:(dc + 1) * 256]), deps=[h2], dma=True)
            st_h2.append(st)
        sch.op("sync", lambda q: q.nop(), deps=st_r2 + st_h2)
        sch.build(nc, ctx, block)
    return nc


def build_l4b(paranoid=True, serialize_dma=False):
    nc = bass.Bass("TRN2", target_bir_lowering=False, num_devices=NCORES)
    h2_in = nc.dram_tensor("h2tf", [D, S_SEL], F16, kind="ExternalInput").ap()
    wg_in = nc.dram_tensor("wg_sl", [D, 1024], F32, kind="ExternalInput").ap()
    wu_in = nc.dram_tensor("wu_sl", [D, 1024], F32, kind="ExternalInput").ap()
    at_out = nc.dram_tensor("at", [1024, S_SEL], F16, kind="ExternalOutput").ap()

    NIC = 8
    sch = Sched(paranoid=paranoid, serialize_dma=serialize_dma)
    with ExitStack() as ctx:
        ee = ctx.enter_context
        h2_s = ee(nc.sbuf_tensor("h2_s", [128, DC * S_SEL], F16))
        wg16 = ee(nc.sbuf_tensor("wg16", [128, DC * 1024], F16))
        wu16 = ee(nc.sbuf_tensor("wu16", [128, DC * 1024], F16))
        sil = ee(nc.sbuf_tensor("sil", [128, QT], F16))
        a_s = [ee(nc.sbuf_tensor(f"a_s{i}", [128, QT], F16)) for i in range(2)]
        ps = [ee(nc.psum_tensor(f"ps{i}", [128, QT], F32)) for i in range(4)]
        block = ee(nc.Block())

        ld_h2 = sch.op("gpsimd", lambda g: g.dma_start(
            out=h2_s[:, :].rearrange("p (dc s) -> p dc s", dc=DC),
            in_=h2_in.rearrange("(dc p) s -> p dc s", p=128)), dma=True)
        ld_wg = sch.op("gpsimd", lambda g: g.dma_start(
            out=wg16[:, :].rearrange("p (dc i) -> p dc i", dc=DC),
            in_=wg_in.rearrange("(dc p) i -> p dc i", p=128)), dma=True)
        ld_wu = sch.op("gpsimd", lambda g: g.dma_start(
            out=wu16[:, :].rearrange("p (dc i) -> p dc i", dc=DC),
            in_=wu_in.rearrange("(dc p) i -> p dc i", p=128)), dma=True)

        a_st = []
        sil_prev = None
        aprev = {}
        for ic in range(NIC):
            for st4 in range(4):
                u = ic * 4 + st4
                gb = (u % 2) * 2
                lastg = None
                lastu = None
                for dc in range(DC):
                    d_ = []
                    if dc == 0:
                        d_ = [ld_h2, ld_wg, ld_wu]
                        if u >= 2:
                            d_.append(aprev[u - 2])
                    lastg = sch.op("tensor", lambda t, ic=ic, st4=st4, dc=dc, gb=gb:
                                   t.matmul(ps[gb][:, :],
                                            wg16[:, dc * 1024 + ic * 128:
                                                 dc * 1024 + (ic + 1) * 128],
                                            h2_s[:, dc * S_SEL + st4 * QT:
                                                 dc * S_SEL + (st4 + 1) * QT],
                                            start=(dc == 0), stop=(dc == DC - 1)),
                                   deps=d_)
                    lastu = sch.op("tensor", lambda t, ic=ic, st4=st4, dc=dc, gb=gb:
                                   t.matmul(ps[gb + 1][:, :],
                                            wu16[:, dc * 1024 + ic * 128:
                                                 dc * 1024 + (ic + 1) * 128],
                                            h2_s[:, dc * S_SEL + st4 * QT:
                                                 dc * S_SEL + (st4 + 1) * QT],
                                            start=(dc == 0), stop=(dc == DC - 1)))
                so = sch.op("scalar", lambda s, gb=gb: s.activation(
                    out=sil[:, :], in_=ps[gb][:, :], func=AF.Silu),
                    deps=[lastg, sil_prev])
                ao = sch.op("vector", lambda v, u=u, gb=gb: v.tensor_mul(
                    a_s[u % 2][:, :], sil[:, :], ps[gb + 1][:, :]),
                    deps=[so, lastu, a_st[-2] if len(a_st) >= 2 else None])
                sil_prev = ao
                aprev[u] = ao
                st = sch.op("sync", lambda q, ic=ic, st4=st4, u=u: q.dma_start(
                    out=at_out[ic * 128:(ic + 1) * 128, st4 * QT:(st4 + 1) * QT],
                    in_=a_s[u % 2][:, :]), deps=[ao], dma=True)
                a_st.append(st)
        sch.op("sync", lambda q: q.nop(), deps=a_st)
        sch.build(nc, ctx, block)
    return nc


def build_l4c(paranoid=True, serialize_dma=False):
    nc = bass.Bass("TRN2", target_bir_lowering=False, num_devices=NCORES)
    at_in = nc.dram_tensor("atf", [I_FF, S_SEL], F16, kind="ExternalInput").ap()
    wd_in = nc.dram_tensor("wd_sl", [I_FF, 256], F32, kind="ExternalInput").ap()
    r2_in = nc.dram_tensor("r2sl", [S_SEL, 256], F32, kind="ExternalInput").ap()
    sx_in = nc.dram_tensor("sx_sl", [S_SEL, 256], F16, kind="ExternalInput").ap()
    sc_in = nc.dram_tensor("sct16", [128, 16], F32, kind="ExternalInput").ap()
    g_out = nc.dram_tensor("gated", [S_SEL, 256], F32, kind="ExternalOutput").ap()

    NI = I_FF // 128
    sch = Sched(paranoid=paranoid, serialize_dma=serialize_dma)
    with ExitStack() as ctx:
        ee = ctx.enter_context
        wd16 = ee(nc.sbuf_tensor("wd16", [128, NI * 256], F16))
        a_t = [ee(nc.sbuf_tensor(f"a_t{i}", [128, NI * 512], F16)) for i in range(2)]
        r2_t = [ee(nc.sbuf_tensor(f"r2_t{i}", [128, 256], F32)) for i in range(2)]
        sx_t = [ee(nc.sbuf_tensor(f"sx_t{i}", [128, 256], F16)) for i in range(2)]
        go_t = [ee(nc.sbuf_tensor(f"go_t{i}", [128, 256], F32)) for i in range(2)]
        tsc = ee(nc.sbuf_tensor("tsc", [128, 256], F32))
        g_s = ee(nc.sbuf_tensor("g_s", [128, 16], F32))
        ps = [ee(nc.psum_tensor(f"ps{i}", [128, 256], F32)) for i in range(2)]
        block = ee(nc.Block())

        ld_wd = sch.op("gpsimd", lambda g: g.dma_start(
            out=wd16[:, :].rearrange("p (ic d) -> p ic d", ic=NI),
            in_=wd_in.rearrange("(ic p) d -> p ic d", p=128)), dma=True)
        ld_sc = sch.op("gpsimd", lambda g: g.dma_start(out=g_s[:, :], in_=sc_in),
                       dma=True)
        sig = sch.op("scalar", lambda s: s.activation(
            out=g_s[:, :], in_=g_s[:, :], func=AF.Sigmoid), deps=[ld_sc])

        g_st = []
        ev_prev = {}
        lda = {}
        for sc4 in range(4):
            lda[sc4] = sch.op("sync", lambda q, sc4=sc4: q.dma_start(
                out=a_t[sc4 % 2][:, :].rearrange("p (ic s) -> p ic s", ic=NI),
                in_=at_in.rearrange("(ic p) s -> p ic s", p=128)[
                    :, :, sc4 * 512:(sc4 + 1) * 512]),
                deps=[ev_prev.get(4 * (sc4 - 2) + 3)], dma=True)
            for ss in range(4):
                sc16 = sc4 * 4 + ss
                ldr = sch.op("sync", lambda q, sc16=sc16: q.dma_start(
                    out=r2_t[sc16 % 2][:, :],
                    in_=r2_in[sc16 * 128:(sc16 + 1) * 128, :]),
                    deps=[ev_prev.get(sc16 - 2)], dma=True)
                lds_ = sch.op("sync", lambda q, sc16=sc16: q.dma_start(
                    out=sx_t[sc16 % 2][:, :],
                    in_=sx_in[sc16 * 128:(sc16 + 1) * 128, :]),
                    deps=[ev_prev.get(sc16 - 2)], dma=True)
                last = None
                for ic in range(NI):
                    d_ = []
                    if ic == 0:
                        d_ = [lda[sc4], ld_wd]
                        if sc16 >= 2:
                            d_.append(ev_prev[sc16 - 2])
                    last = sch.op("tensor", lambda t, sc16=sc16, ss=ss, ic=ic:
                                  t.matmul(
                                      ps[sc16 % 2][:, :],
                                      a_t[(sc16 // 4) % 2][:, ic * 512 + ss * 128:
                                           ic * 512 + (ss + 1) * 128],
                                      wd16[:, ic * 256:(ic + 1) * 256],
                                      start=(ic == 0), stop=(ic == NI - 1)),
                                  deps=d_)
                e1 = sch.op("vector", lambda v, sc16=sc16: v.tensor_add(
                    go_t[sc16 % 2][:, :], ps[sc16 % 2][:, :], r2_t[sc16 % 2][:, :]),
                    deps=[last, ldr, g_st[-2] if len(g_st) >= 2 else None])
                e2 = sch.op("vector", lambda v, sc16=sc16: v.tensor_sub(
                    tsc[:, :], go_t[sc16 % 2][:, :], sx_t[sc16 % 2][:, :]),
                    deps=[e1, lds_])
                e3 = sch.op("vector", lambda v, sc16=sc16: v.tensor_scalar(
                    go_t[sc16 % 2][:, :], tsc[:, :], g_s[:, sc16:sc16 + 1], None,
                    OP.mult), deps=[e2, sig])
                ev_prev[sc16] = e3
                st = sch.op("sync", lambda q, sc16=sc16: q.dma_start(
                    out=g_out[sc16 * 128:(sc16 + 1) * 128, :],
                    in_=go_t[sc16 % 2][:, :]), deps=[e3], dma=True)
                g_st.append(st)
        sch.op("sync", lambda q: q.nop(), deps=g_st)
        sch.build(nc, ctx, block)
    return nc


def build_l5(paranoid=True, serialize_dma=False):
    nc = bass.Bass("TRN2", target_bir_lowering=False, num_devices=NCORES)
    x1_in = nc.dram_tensor("x1_own", [2048, D], F32, kind="ExternalInput").ap()
    gf_in = nc.dram_tensor("gatedf2", [S_SEL + 128, D], F32,
                           kind="ExternalInput").ap()
    sr_in = nc.dram_tensor("sri", [128, 16], F32, kind="ExternalInput").ap()
    io_in = nc.dram_tensor("ioc", [128, 1], F32, kind="ExternalInput").ap()
    out_o = nc.dram_tensor("out", [2048, D], F32, kind="ExternalOutput").ap()

    sch = Sched(paranoid=paranoid, serialize_dma=serialize_dma)
    with ExitStack() as ctx:
        ee = ctx.enter_context
        sr_s = ee(nc.sbuf_tensor("sr_s", [128, 16], F32))
        io_s = ee(nc.sbuf_tensor("io_s", [128, 1], F32))
        rel = ee(nc.sbuf_tensor("rel", [128, 16], F32))
        v1 = ee(nc.sbuf_tensor("v1", [128, 16], F32))
        v2 = ee(nc.sbuf_tensor("v2", [128, 16], F32))
        idx_u = ee(nc.sbuf_tensor("idx_u", [128, 16], U32))
        NB5 = 4
        xo = [ee(nc.sbuf_tensor(f"xo{i}", [128, D], F32)) for i in range(NB5)]
        mo = [ee(nc.sbuf_tensor(f"mo{i}", [128, D], F32)) for i in range(NB5)]
        r_t = ee(nc.sbuf_tensor("r_t", [128, D], F32))
        sqd = ee(nc.sbuf_tensor("sqd", [128, D], F32))
        st1 = ee(nc.sbuf_tensor("st1", [128, 8], F32))
        o_t = [ee(nc.sbuf_tensor(f"o_t{i}", [128, D], F32)) for i in range(NB5)]
        eps_c = ee(nc.sbuf_tensor("eps_c", [128, 1], F32))
        block = ee(nc.Block())

        ld1 = sch.op("gpsimd", lambda g: g.dma_start(out=sr_s[:, :], in_=sr_in),
                     dma=True)
        ld2 = sch.op("gpsimd", lambda g: g.dma_start(out=io_s[:, :], in_=io_in),
                     dma=True)
        ms_e = sch.op("vector", lambda v: v.memset(eps_c[:, :], EPS_LN))
        i1 = sch.op("vector", lambda v: v.tensor_scalar(
            v1[:, :], sr_s[:, :], 2048.0, None, OP.is_ge), deps=[ld1])
        i2 = sch.op("vector", lambda v: v.tensor_scalar(
            v2[:, :], v1[:, :], io_s[:, 0:1], None, OP.mult), deps=[i1, ld2])
        i3 = sch.op("vector", lambda v: v.tensor_scalar(
            v1[:, :], v1[:, :], -1.0, 1.0, OP.mult, OP.add), deps=[i2])
        i4 = sch.op("vector", lambda v: v.tensor_mul(
            rel[:, :], sr_s[:, :], v1[:, :]), deps=[i3])
        i5 = sch.op("vector", lambda v: v.tensor_add(
            rel[:, :], rel[:, :], v2[:, :]), deps=[i4])
        i6 = sch.op("vector", lambda v: v.tensor_copy(idx_u[:, :], rel[:, :]),
                    deps=[i5])
        o_st = []
        ra_hist = []
        x1o_prev = None
        lnv_prev = None
        for tc in range(16):
            ldx = sch.op("sync", lambda q, tc=tc: q.dma_start(
                out=xo[tc % NB5][:, :], in_=x1_in[tc * 128:(tc + 1) * 128, :]),
                deps=[ra_hist[tc - NB5] if tc >= NB5 else None], dma=True)
            ldm = sch.op("gpsimd", lambda g, tc=tc: g.indirect_dma_start(
                out=mo[tc % NB5][:, :], out_offset=None, in_=gf_in,
                in_offset=IndirectOffsetOnAxis(idx_u[:, tc:tc + 1], 0)),
                deps=[i6, ra_hist[tc - NB5] if tc >= NB5 else None], dma=True)
            ra = sch.op("vector", lambda v, tc=tc: v.scalar_tensor_tensor(
                r_t[:, :], xo[tc % NB5][:, :], 2.0, mo[tc % NB5][:, :],
                OP.mult, OP.add),
                deps=[ldx, ldm, x1o_prev])
            ra_hist.append(ra)
            sm = sch.op("vector", lambda v: v.reduce_sum(
                st1[:, 0:1], r_t[:, :], axis=AX.X), deps=[ra])
            s2 = sch.op("scalar", lambda s: s.activation(
                out=sqd[:, :], in_=r_t[:, :], func=AF.Square,
                accum_out=st1[:, 1:2]), deps=[ra, x1o_prev])
            mean = sch.op("vector", lambda v: v.tensor_scalar_mul(
                st1[:, 2:3], st1[:, 0:1], 1.0 / D), deps=[sm])
            msq = sch.op("vector", lambda v: v.tensor_scalar_mul(
                st1[:, 3:4], st1[:, 1:2], 1.0 / D), deps=[s2])
            m2_ = sch.op("vector", lambda v: v.tensor_mul(
                st1[:, 4:5], st1[:, 2:3], st1[:, 2:3]), deps=[mean])
            var = sch.op("vector", lambda v: v.tensor_sub(
                st1[:, 5:6], st1[:, 3:4], st1[:, 4:5]), deps=[msq, m2_, lnv_prev])
            lnv = sch.op("scalar", lambda s: s.activation(
                out=st1[:, 6:7], in_=st1[:, 5:6], func=AF.Ln,
                bias=eps_c[:, 0:1]), deps=[var, ms_e])
            lnv_prev = lnv
            rstd = sch.op("scalar", lambda s: s.activation(
                out=st1[:, 7:8], in_=st1[:, 6:7], func=AF.Exp, scale=-0.5),
                deps=[lnv])
            oo = sch.op("vector", lambda v, tc=tc: v.tensor_scalar(
                o_t[tc % NB5][:, :], r_t[:, :], st1[:, 2:3], st1[:, 7:8],
                OP.subtract, OP.mult),
                deps=[rstd, o_st[-NB5] if len(o_st) >= NB5 else None])
            x1o_prev = oo
            st = sch.op("sync", lambda q, tc=tc: q.dma_start(
                out=out_o[tc * 128:(tc + 1) * 128, :], in_=o_t[tc % NB5][:, :]),
                deps=[oo], dma=True)
            o_st.append(st)
        sch.op("sync", lambda q: q.nop(), deps=o_st)
        sch.build(nc, ctx, block)
    return nc


_CACHE = {}


def _causal_masks_f16():
    m = np.zeros((4, 128, QT), dtype=np.float16)
    q = np.arange(QT)[None, :]
    k = np.arange(128)[:, None]
    for ci in range(4):
        m[ci] = ((128 * ci + k) <= q).astype(np.float16)
    return m


def _run_l1(inputs):
    xt = np.ascontiguousarray(np.transpose(inputs["L0_attn_x"], (0, 2, 1)))
    WQ, WK, WV = inputs["L0_attn_W_Q"], inputs["L0_attn_W_K"], inputs["L0_attn_W_V"]
    bQ, bK, bV = inputs["L0_attn_b_Q"], inputs["L0_attn_b_K"], inputs["L0_attn_b_V"]
    ones128 = np.ones((1, 128), np.float32)
    masks = _causal_masks_f16()
    in_maps = []
    for c in range(NCORES):
        b = c // 4
        hs = slice(4 * (c % 4), 4 * (c % 4) + HPC)
        in_maps.append({
            "x_t": np.ascontiguousarray(xt[b]),
            "wq": np.ascontiguousarray(WQ[hs]),
            "wk": np.ascontiguousarray(WK[hs]),
            "wv_mov": np.ascontiguousarray(
                np.transpose(WV[hs], (1, 0, 2)).reshape(D, HPC * E)),
            "bq_t": np.ascontiguousarray(bQ[hs].T),
            "bk_t": np.ascontiguousarray(bK[hs].T),
            "bv_mov": np.ascontiguousarray(bV[hs].reshape(1, HPC * E)),
            "ones128": ones128,
            "masks": masks,
        })
    if "l1" not in _CACHE:
        _CACHE["l1"] = build_l1()
    res = run_bass_kernel_spmd(_CACHE["l1"], in_maps, core_ids=list(range(NCORES)))
    return [res.results[c]["z"] for c in range(NCORES)]


def _run_l2(inputs, z_shards):
    x = inputs["L0_attn_x"]
    WO = inputs["L0_attn_W_O"]
    bO = inputs["L0_attn_b_O"]
    wrt = inputs["L0_ffn_w_router"]
    ones128 = np.ones((1, 128), np.float32)
    in_maps = []
    for c in range(NCORES):
        sl = slice(1024 * c, 1024 * c + 1024)
        z16 = np.empty((H, E, 2048), dtype=np.float16)
        for h in range(H):
            src, pair = h // 4, h % 4
            z16[h, :, 0:1024] = z_shards[src][pair * E:(pair + 1) * E, sl]
            z16[h, :, 1024:2048] = z_shards[4 + src][pair * E:(pair + 1) * E, sl]
        x_own = np.concatenate([x[0, sl], x[1, sl]], axis=0)
        in_maps.append({
            "z16": z16,
            "wo": np.ascontiguousarray(WO),
            "bo": bO.reshape(1, D),
            "x_own": np.ascontiguousarray(x_own),
            "wrt": wrt.reshape(1, D),
            "ones128": ones128,
        })
    if "l2" not in _CACHE:
        _CACHE["l2"] = build_l2()
    res = run_bass_kernel_spmd(_CACHE["l2"], in_maps, core_ids=list(range(NCORES)))
    return ([res.results[c]["x1"] for c in range(NCORES)],
            [res.results[c]["x1h"] for c in range(NCORES)],
            [res.results[c]["scores"] for c in range(NCORES)])


def _l3_consts():
    c = {}
    c["postile"] = np.tile(
        (np.arange(64)[None, :] * 128 + np.arange(128)[:, None]).astype(np.float32),
        (1, 2))
    c["onesmat"] = np.ones((128, 128), np.float32)
    c["trilT"] = np.triu(np.ones((128, 128), np.float16))
    c["siota"] = np.tile(np.arange(1024, dtype=np.float16)[None, :], (128, 1))
    rot = np.zeros((128, 128), np.float32)
    for r in range(64):
        rot[r, r + 64] = -1.0
    for r in range(64, 128):
        rot[r, r - 64] = 1.0
    c["rotT"] = rot.T.astype(np.float16)
    inv_freq = (1.0 / (10000.0 ** (np.arange(0, 128, 2, dtype=np.float32) / 128.0)
                       )).astype(np.float32)
    c["invf"] = np.concatenate([inv_freq, inv_freq]).reshape(128, 1)
    c["ident"] = np.eye(128, dtype=np.float32)
    c["pickmat"] = np.diag([1.0, 1.0, -1.0, -1.0]).astype(np.float32)
    return c


def _run_l3(inputs, x1h_shards, sc_shards):
    x1f = np.concatenate(
        [np.concatenate([x1h_shards[c][0:1024] for c in range(NCORES)], axis=0),
         np.concatenate([x1h_shards[c][1024:2048] for c in range(NCORES)], axis=0)],
        axis=0)
    sct = np.empty((128, 128), np.float32)
    for c in range(NCORES):
        sct[:, 8 * c:8 * c + 8] = sc_shards[c][:, 0:8]
        sct[:, 64 + 8 * c:64 + 8 * c + 8] = sc_shards[c][:, 8:16]
    cst = _l3_consts()
    wq = inputs["L0_ffn_wq"]
    wk = inputs["L0_ffn_wk"]
    wv = inputs["L0_ffn_wv"]
    bq = inputs["L0_ffn_bq"]
    bk = inputs["L0_ffn_bk"]
    bv = inputs["L0_ffn_bv"]
    ln1 = inputs["L0_ffn_ln1_w"]
    in_maps = []
    for c in range(NCORES):
        in_maps.append({
            "x1f": x1f,
            "sctile": sct,
            "postile": cst["postile"],
            "onesmat": cst["onesmat"],
            "ones128": np.ones((1, 128), np.float32),
            "trilT": cst["trilT"],
            "siota": cst["siota"],
            "rotT": cst["rotT"],
            "invf": cst["invf"],
            "ident": cst["ident"],
            "pickmat": cst["pickmat"],
            "masks": _causal_masks_f16(),
            "wq_sel": np.ascontiguousarray(wq[:, 256 * c:256 * c + 256]),
            "wk_sel": np.ascontiguousarray(wk[:, 128 * c:128 * c + 128]),
            "wv_sel": np.ascontiguousarray(wv[:, 128 * c:128 * c + 128]),
            "bq_sel": np.ascontiguousarray(
                bq[256 * c:256 * c + 256].reshape(2, 128).T),
            "bk_sel": bk[128 * c:128 * c + 128].reshape(128, 1),
            "bv_sel": bv[128 * c:128 * c + 128].reshape(1, 128),
            "ln1c": np.ascontiguousarray(ln1.reshape(16, 128).T),
        })
    if "l3" not in _CACHE:
        _CACHE["l3"] = build_l3()
    res = run_bass_kernel_spmd(_CACHE["l3"], in_maps, core_ids=list(range(NCORES)))
    poslist = res.results[0]["poslist"]
    selx = res.results[0]["selx"]
    srg = res.results[0]["srg"]
    z_sel = [res.results[c]["z_sel"] for c in range(NCORES)]
    return poslist, selx, srg, z_sel


def _run_l4a(inputs, zsel_shards, selx):
    wo = inputs["L0_ffn_wo"]
    ln2 = inputs["L0_ffn_ln2_w"]
    ones128 = np.ones((1, 128), np.float32)
    ln2c = np.ascontiguousarray(ln2.reshape(16, 128).T)
    in_maps = []
    for c in range(NCORES):
        z16s = np.empty((H, E, 256), np.float16)
        for h in range(H):
            z16s[h] = zsel_shards[h // 2][(h % 2) * E:(h % 2 + 1) * E,
                                          256 * c:256 * c + 256]
        in_maps.append({
            "z16s": z16s,
            "wo_f": wo,
            "selx16": np.ascontiguousarray(selx[256 * c:256 * c + 256]),
            "ln2c": ln2c,
            "ones128": ones128,
        })
    if "l4a" not in _CACHE:
        _CACHE["l4a"] = build_l4a()
    res = run_bass_kernel_spmd(_CACHE["l4a"], in_maps, core_ids=list(range(NCORES)))
    return ([res.results[c]["h2t"] for c in range(NCORES)],
            [res.results[c]["r2t"] for c in range(NCORES)])


def _run_l4b(inputs, h2t_shards):
    h2tf = np.concatenate(h2t_shards, axis=1)
    wg = inputs["L0_ffn_w_gate"]
    wu = inputs["L0_ffn_w_up"]
    in_maps = []
    for c in range(NCORES):
        in_maps.append({
            "h2tf": h2tf,
            "wg_sl": np.ascontiguousarray(wg[:, 1024 * c:1024 * c + 1024]),
            "wu_sl": np.ascontiguousarray(wu[:, 1024 * c:1024 * c + 1024]),
        })
    if "l4b" not in _CACHE:
        _CACHE["l4b"] = build_l4b()
    res = run_bass_kernel_spmd(_CACHE["l4b"], in_maps, core_ids=list(range(NCORES)))
    return [res.results[c]["at"] for c in range(NCORES)]


def _run_l4c(inputs, at_shards, r2t_shards, selx, poslist):
    atf = np.concatenate(at_shards, axis=0)
    r2sm = np.ascontiguousarray(
        np.concatenate(r2t_shards, axis=1).T)
    wd = inputs["L0_ffn_w_down"]
    sct16 = np.ascontiguousarray(poslist[1].reshape(16, 128).T)
    in_maps = []
    for c in range(NCORES):
        in_maps.append({
            "atf": atf,
            "wd_sl": np.ascontiguousarray(wd[:, 256 * c:256 * c + 256]),
            "r2sl": np.ascontiguousarray(r2sm[:, 256 * c:256 * c + 256]),
            "sx_sl": np.ascontiguousarray(selx[:, 256 * c:256 * c + 256]),
            "sct16": sct16,
        })
    if "l4c" not in _CACHE:
        _CACHE["l4c"] = build_l4c()
    res = run_bass_kernel_spmd(_CACHE["l4c"], in_maps, core_ids=list(range(NCORES)))
    return [res.results[c]["gated"] for c in range(NCORES)]


def _run_l5(inputs, x1_shards, gated_shards, srg):
    gatedf2 = np.concatenate(
        [np.concatenate(gated_shards, axis=1),
         np.zeros((128, D), np.float32)], axis=0)
    ioc = (2048.0 + np.arange(128, dtype=np.float32)[:, None])
    in_maps = []
    for c in range(NCORES):
        sri = np.empty((128, 16), np.float32)
        sri[:, 0:8] = srg[:, 8 * c:8 * c + 8]
        sri[:, 8:16] = srg[:, 64 + 8 * c:64 + 8 * c + 8]
        in_maps.append({
            "x1_own": x1_shards[c],
            "gatedf2": gatedf2,
            "sri": np.ascontiguousarray(sri),
            "ioc": ioc,
        })
    if "l5" not in _CACHE:
        _CACHE["l5"] = build_l5()
    res = run_bass_kernel_spmd(_CACHE["l5"], in_maps, core_ids=list(range(NCORES)))
    out = np.empty((B, T, D), np.float32)
    for c in range(NCORES):
        o = res.results[c]["out"]
        out[0, 1024 * c:1024 * c + 1024] = o[0:1024]
        out[1, 1024 * c:1024 * c + 1024] = o[1024:2048]
    return out


def kernel(**inputs):
    z_shards = _run_l1(inputs)
    x1_shards, x1h_shards, sc_shards = _run_l2(inputs, z_shards)
    poslist, selx, srg, zsel_shards = _run_l3(inputs, x1h_shards, sc_shards)
    h2t_shards, r2t_shards = _run_l4a(inputs, zsel_shards, selx)
    at_shards = _run_l4b(inputs, h2t_shards)
    gated_shards = _run_l4c(inputs, at_shards, r2t_shards, selx, poslist)
    return _run_l5(inputs, x1_shards, gated_shards, srg)


# version 09
# speedup vs baseline: 1.1846x; 1.0193x over previous
import numpy as np
import ml_dtypes
from contextlib import ExitStack

import concourse.bass as bass
import concourse.mybir as mybir
from concourse.bass import IndirectOffsetOnAxis
from concourse.bass_utils import run_bass_kernel_spmd

F32 = mybir.dt.float32
BF16 = mybir.dt.bfloat16
F16 = mybir.dt.float16
U32 = mybir.dt.uint32
AF = mybir.ActivationFunctionType
OP = mybir.AluOpType
AX = mybir.AxisListType
BF = ml_dtypes.bfloat16

NCORES = 8
B, T, D, H, E = 2, 8192, 2048, 16, 128
HPC = 4
DC = D // 128
QT = 512
NQT = T // QT
INV_SCALE = float(1.0 / np.sqrt(np.float32(128.0)))
LN_ATTN_SCALE = float(np.log(np.sqrt(np.float64(128.0))))

SELK = 1024
S_SEL = 2 * SELK
I_FF = 8192
EPS_LN = 1e-5
EPS_RMS = 1e-6


class Op:
    __slots__ = ("eng", "fn", "deps", "dma", "sem", "milestone_val", "n_global", "dma_class")

    def __init__(self, eng, fn, deps, dma):
        self.eng = eng
        self.fn = fn
        self.deps = deps
        self.dma = dma
        self.sem = False
        self.milestone_val = None
        self.n_global = -1


class Sched:
    ENGINES = ("sync", "scalar", "vector", "tensor", "gpsimd")
    NDMA_CLASSES = 6

    def __init__(self, paranoid=False, serialize_dma=False):
        self.ops = {e: [] for e in self.ENGINES}
        self.n = 0
        self.paranoid = paranoid
        self.serialize_dma = serialize_dma
        self._dma_count = {}
        self._class_prev = {}

    def op(self, eng, fn, deps=(), dma=False):
        deps = [d for d in deps if d is not None]
        h = Op(eng, fn, deps, dma)
        if dma:
            k = self._dma_count.get(eng, 0)
            self._dma_count[eng] = k + 1
            cls = 0 if self.serialize_dma else k % self.NDMA_CLASSES
            h.dma_class = (eng, cls)
            prev = self._class_prev.get(h.dma_class)
            if prev is not None:
                deps.append(prev)
            self._class_prev[h.dma_class] = h
        else:
            h.dma_class = None
        h.deps = deps
        h.n_global = self.n
        self.n += 1
        self.ops[eng].append(h)
        return h

    def _needs_wait(self, dep, eng):
        return dep.eng != eng or dep.dma or self.paranoid

    def build(self, nc, ctx, block):
        for e in self.ENGINES:
            for h in self.ops[e]:
                if h.dma:
                    h.sem = True
                for d in h.deps:
                    if self._needs_wait(d, e):
                        d.sem = True
        sems = {}
        dma_sems = {}
        for e in self.ENGINES:
            if any(o.sem and not o.dma for o in self.ops[e]):
                sems[e] = ctx.enter_context(nc.semaphore(f"prog_{e}"))
            ncls = 1 if self.serialize_dma else self.NDMA_CLASSES
            if any(o.dma for o in self.ops[e]):
                for cl in range(ncls):
                    dma_sems[(e, cl)] = ctx.enter_context(
                        nc.semaphore(f"dma_{e}_{cl}"))
        for e in self.ENGINES:
            c = 0
            dcnt = {}
            for o in self.ops[e]:
                if not o.sem:
                    continue
                if o.dma:
                    k = dcnt.get(o.dma_class, 0) + 16
                    dcnt[o.dma_class] = k
                    o.milestone_val = k
                else:
                    c += 1
                    o.milestone_val = c

        def emit_stream(q, ename):
            waited = {}
            for o in self.ops[ename]:
                for d in o.deps:
                    if not self._needs_wait(d, ename):
                        continue
                    v = d.milestone_val
                    assert v is not None
                    key = d.dma_class if d.dma else d.eng
                    sem = dma_sems[key] if d.dma else sems[key]
                    if v > waited.get(key, 0):
                        q.wait_ge(sem, v)
                        waited[key] = v
                ins = o.fn(q)
                if o.sem:
                    assert ins is not None, f"milestone op on {ename} returned None"
                    if o.dma:
                        ins.then_inc(dma_sems[o.dma_class], 16)
                    else:
                        ins.then_inc(sems[ename], 1)

        streams = {}
        if self.ops["sync"]:
            @block.sync
            def _(q):
                emit_stream(q, "sync")
        if self.ops["scalar"]:
            @block.scalar
            def _(q):
                emit_stream(q, "scalar")
        if self.ops["vector"]:
            @block.vector
            def _(q):
                emit_stream(q, "vector")
        if self.ops["tensor"]:
            @block.tensor
            def _(q):
                emit_stream(q, "tensor")
        if self.ops["gpsimd"]:
            @block.gpsimd
            def _(q):
                emit_stream(q, "gpsimd")
        return streams


def _chunk_schedule():
    out = []
    for qi in range(NQT):
        nk = 4 * qi
        for kc in range(nk + 4):
            out.append((qi, kc, kc - nk if kc >= nk else None))
    return out


CHUNKS = _chunk_schedule()
NCH = len(CHUNKS)


def build_l1(paranoid=True, serialize_dma=False):
    nc = bass.Bass("TRN2", target_bir_lowering=False, num_devices=NCORES)
    x_t = nc.dram_tensor("x_t", [D, T], F32, kind="ExternalInput").ap()
    wq_in = nc.dram_tensor("wq", [HPC, D, E], F32, kind="ExternalInput").ap()
    wk_in = nc.dram_tensor("wk", [HPC, D, E], F32, kind="ExternalInput").ap()
    wv_in = nc.dram_tensor("wv_mov", [D, HPC * E], F32, kind="ExternalInput").ap()
    bq_in = nc.dram_tensor("bq_t", [E, HPC], F32, kind="ExternalInput").ap()
    bk_in = nc.dram_tensor("bk_t", [E, HPC], F32, kind="ExternalInput").ap()
    bv_in = nc.dram_tensor("bv_mov", [1, HPC * E], F32, kind="ExternalInput").ap()
    ones_in = nc.dram_tensor("ones128", [1, 128], F32, kind="ExternalInput").ap()
    mask_in = nc.dram_tensor("masks", [4, 128, QT], F16, kind="ExternalInput").ap()
    z_out = nc.dram_tensor("z", [HPC * E, T], F16, kind="ExternalOutput").ap()
    xbf = nc.dram_tensor("xbf", [D, T], F16, kind="Internal")

    sch = Sched(paranoid=paranoid, serialize_dma=serialize_dma)
    with ExitStack() as ctx:
        ee = ctx.enter_context
        wq_s = ee(nc.sbuf_tensor("wq_s", [128, HPC * DC * E], F16))
        wk_s = ee(nc.sbuf_tensor("wk_s", [128, HPC * DC * E], F16))
        wv_s = ee(nc.sbuf_tensor("wv_s", [128, DC * HPC * E], F16))
        bq_s = ee(nc.sbuf_tensor("bq_s", [128, HPC], F32))
        bk_s = ee(nc.sbuf_tensor("bk_s", [128, HPC], F32))
        bv_row = ee(nc.sbuf_tensor("bv_row", [1, HPC * E], F32))
        bvB = ee(nc.sbuf_tensor("bvB", [128, HPC * E], F32))
        ones_row = ee(nc.sbuf_tensor("ones_row", [1, 128], F32))
        onesb_col = ee(nc.sbuf_tensor("onesb_col", [128, 1], F16))
        mask_s = ee(nc.sbuf_tensor("mask_s", [128, 4 * QT], F16))
        q_s = [ee(nc.sbuf_tensor(f"q_s{i}", [128, T], F16)) for i in range(2)]
        k_s = [ee(nc.sbuf_tensor(f"k_s{i}", [128, T], F16)) for i in range(2)]
        v_s = [ee(nc.sbuf_tensor(f"v_s{i}", [128, T], F16)) for i in range(2)]
        xt_t = [ee(nc.sbuf_tensor(f"xt_t{i}", [128, 4 * QT], F16)) for i in range(2)]
        sq_t = [ee(nc.sbuf_tensor(f"sq_t{i}", [128, QT], F16)) for i in range(2)]
        pT = [ee(nc.sbuf_tensor(f"pT{i}", [128, 2 * QT], F16)) for i in range(2)]
        zn_s = [ee(nc.sbuf_tensor(f"zn_s{i}", [128, QT], F16)) for i in range(2)]
        lrec = ee(nc.sbuf_tensor("lrec", [1, QT], F32))
        lrecB = ee(nc.sbuf_tensor("lrecB", [128, QT], F32))
        cm = ee(nc.sbuf_tensor("cm", [1, 4], F32))
        ctile = ee(nc.sbuf_tensor("ctile", [1, 8], F32))
        lnbias = ee(nc.sbuf_tensor("lnbias", [1, 1], F32))
        cbias = ee(nc.sbuf_tensor("cbias", [128, 2], F32))
        psA = ee(nc.psum_tensor("psA", [128, 2 * QT], F32))
        psB = ee(nc.psum_tensor("psB", [128, 2 * QT], F32))
        psZ = [ee(nc.psum_tensor(f"psZ{i}", [128, QT], F32)) for i in range(2)]
        psL = ee(nc.psum_tensor("psL", [128, QT], F32))
        psR = ee(nc.psum_tensor("psR", [128, QT], F32))
        block = ee(nc.Block())

        def bank(b, c0=0, c1=QT, r0=0, r1=128):
            if b < 4:
                t = psA if b < 2 else psB
                off = QT * (b % 2)
                return t[r0:r1, off + c0:off + c1]
            return [psZ[0], psZ[1], psL, psR][b - 4][r0:r1, c0:c1]

        def wq_tile(h, dc):
            return wq_s[:, (h * DC + dc) * E:(h * DC + dc + 1) * E]

        def wk_tile(h, dc):
            return wk_s[:, (h * DC + dc) * E:(h * DC + dc + 1) * E]

        stage = [sch.op("gpsimd",
                        lambda g, i=i: g.dma_start(
                            out=xbf.ap()[:, i * QT:(i + 1) * QT],
                            in_=x_t[:, i * QT:(i + 1) * QT]), dma=True)
                 for i in range(NQT)]
        wloads = []
        for h in range(HPC):
            wloads.append(sch.op("gpsimd", lambda g, h=h: g.dma_start(
                out=wq_s[:, :].rearrange("p (h dc e) -> p h dc e", h=HPC, dc=DC)[:, h],
                in_=wq_in[h].rearrange("(dc p) e -> p dc e", p=128)), dma=True))
            wloads.append(sch.op("gpsimd", lambda g, h=h: g.dma_start(
                out=wk_s[:, :].rearrange("p (h dc e) -> p h dc e", h=HPC, dc=DC)[:, h],
                in_=wk_in[h].rearrange("(dc p) e -> p dc e", p=128)), dma=True))
        wloads.append(sch.op("gpsimd", lambda g: g.dma_start(
            out=wv_s[:, :].rearrange("p (dc e) -> p dc e", dc=DC),
            in_=wv_in.rearrange("(dc p) e -> p dc e", p=128)), dma=True))
        ld_bq = sch.op("gpsimd", lambda g: g.dma_start(out=bq_s[:, :], in_=bq_in), dma=True)
        ld_bk = sch.op("gpsimd", lambda g: g.dma_start(out=bk_s[:, :], in_=bk_in), dma=True)
        ld_bv = sch.op("gpsimd", lambda g: g.dma_start(out=bv_row[:, :], in_=bv_in), dma=True)
        ld_on = sch.op("gpsimd", lambda g: g.dma_start(out=ones_row[:, :], in_=ones_in), dma=True)
        ld_mk = sch.op("gpsimd", lambda g: g.dma_start(
            out=mask_s[:, :].rearrange("p (c q) -> p c q", c=4),
            in_=mask_in.rearrange("c p q -> p c q")), dma=True)
        ms_onesb = sch.op("vector", lambda v: v.memset(onesb_col[:, :], 1.0))
        ms_lnb = sch.op("vector", lambda v: v.memset(lnbias[:, :], -LN_ATTN_SCALE))
        mm_bvB = sch.op("tensor", lambda t: t.matmul(
            psR[:, 0:HPC * E], ones_row[:, :], bv_row[:, :], start=True, stop=True),
            deps=[ld_bv, ld_on])
        ev_bvB = sch.op("vector", lambda v: v.tensor_copy(bvB[:, :], psR[:, 0:HPC * E]),
                        deps=[mm_bvB])

        tile_user = {}
        g_tile = 0
        zstores = []
        exp_g = {}
        pv_g = {}
        zn_g = {}
        rc_g = {}
        gci = 0
        gqi = 0
        lb_prev = None
        lc_prev = None
        cchain_last = ev_bvB
        attn_prev_tensor = None
        attn_prev_vector = None
        attn_prev_scalar = None

        for grp in range(2):
            qk_evac = {}
            for tt in range(NQT):
                for dq in range(4):
                    i = g_tile
                    deps = [stage[tt]]
                    if i >= 2:
                        deps.append(tile_user[i - 2])
                    ld = sch.op("sync", lambda q, dq=dq, tt=tt, i=i: q.dma_start(
                        out=xt_t[i % 2][:, :].rearrange("p (j t) -> p j t", j=4),
                        in_=xbf.ap().rearrange("(dc p) t -> p dc t", p=128)[
                            :, dq * 4:(dq + 1) * 4, tt * QT:(tt + 1) * QT]),
                        deps=deps, dma=True)
                    last = None
                    for j in range(4):
                        dc = dq * 4 + j
                        for pi in range(2):
                            h = grp * 2 + pi
                            b0 = (tt % 2) * 4 + 2 * pi
                            d_ = [ld]
                            if i == 0 and grp == 0:
                                d_ += wloads[0:4]
                            if dc == 0:
                                if grp == 0 and tt == 1:
                                    d_.append(ev_bvB)
                                if grp == 1 and tt == 0:
                                    d_ += wloads[4:8]
                                if tt >= 2:
                                    d_.append(qk_evac[(tt - 2, pi, 1)])
                                elif grp == 1:
                                    d_ += [attn_prev_scalar, attn_prev_vector]
                            sch.op("tensor", lambda t, h=h, dc=dc, j=j, b0=b0, i=i:
                                   t.matmul(bank(b0), wq_tile(h, dc),
                                            xt_t[i % 2][:, j * QT:(j + 1) * QT],
                                            start=(dc == 0), stop=(dc == DC - 1)),
                                   deps=d_)
                            last = sch.op("tensor",
                                          lambda t, h=h, dc=dc, j=j, b0=b0, i=i:
                                          t.matmul(bank(b0 + 1), wk_tile(h, dc),
                                                   xt_t[i % 2][:, j * QT:(j + 1) * QT],
                                                   start=(dc == 0),
                                                   stop=(dc == DC - 1)))
                    tile_user[i] = last
                    g_tile += 1
                    if dq == 3:
                        for pi in range(2):
                            h = grp * 2 + pi
                            b0 = (tt % 2) * 4 + 2 * pi
                            d_ = [last, ld_bq, ld_bk]
                            if grp == 1 and tt < 2:
                                d_.append(attn_prev_tensor)
                            e1 = sch.op("vector", lambda v, pi=pi, h=h, b0=b0, tt=tt:
                                        v.tensor_scalar_add(
                                            q_s[pi][:, tt * QT:(tt + 1) * QT],
                                            bank(b0), bq_s[:, h:h + 1]),
                                        deps=d_)
                            e2 = sch.op("vector", lambda v, pi=pi, h=h, b0=b0, tt=tt:
                                        v.tensor_scalar_add(
                                            k_s[pi][:, tt * QT:(tt + 1) * QT],
                                            bank(b0 + 1), bk_s[:, h:h + 1]))
                            qk_evac[(tt, pi, 0)] = e1
                            qk_evac[(tt, pi, 1)] = e2
            red_ops = {}
            sq_ops = {}
            j_list = [(pi, isk) for pi in range(2) for isk in range(2)]
            for jj, (pi, isk) in enumerate(j_list):
                src = (k_s if isk else q_s)[pi]
                for tt in range(NQT):
                    u = jj * NQT + tt
                    d_ = [qk_evac[(tt, pi, isk)]]
                    if u >= 2:
                        d_.append(red_ops[u - 2])
                    sqo = sch.op("scalar", lambda s, src=src, tt=tt, u=u:
                                 s.activation(out=sq_t[u % 2][:, :],
                                              in_=src[:, tt * QT:(tt + 1) * QT],
                                              func=AF.Square), deps=d_)
                    sq_ops[u] = sqo
                    d_ = [sqo, ms_onesb]
                    if u == 0:
                        d_.append(cchain_last)
                    if u < 4:
                        d_.append(qk_evac[(NQT - 2, 1, 1)])
                        d_.append(qk_evac[(NQT - 1, 1, 1)])
                    if u >= 4:
                        d_.append(red_ops[u - 4])
                    mmo = sch.op("tensor", lambda t, u=u: t.matmul(
                        bank(u % 4, r0=0, r1=1), onesb_col[:, :], sq_t[u % 2][:, :],
                        start=True, stop=True), deps=d_)
                    rd = sch.op("vector", lambda v, u=u: v.reduce_max(
                        ctile[0:1, u % 4:u % 4 + 1], bank(u % 4, r0=0, r1=1),
                        axis=AX.X), deps=[mmo])
                    slot = jj
                    if tt == 0:
                        acc = sch.op("vector", lambda v, slot=slot, u=u: v.tensor_copy(
                            cm[0:1, slot:slot + 1], ctile[0:1, u % 4:u % 4 + 1]), deps=[rd])
                    else:
                        acc = sch.op("vector", lambda v, slot=slot, u=u: v.tensor_max(
                            cm[0:1, slot:slot + 1], cm[0:1, slot:slot + 1],
                            ctile[0:1, u % 4:u % 4 + 1]), deps=[rd])
                    red_ops[u] = acc
            cb_ready = []
            for pi in range(2):
                pr = sch.op("vector", lambda v, pi=pi: v.tensor_mul(
                    ctile[0:1, 4 + pi:5 + pi], cm[0:1, 2 * pi:2 * pi + 1],
                    cm[0:1, 2 * pi + 1:2 * pi + 2]),
                    deps=[red_ops[(2 * pi) * NQT + NQT - 1],
                          red_ops[(2 * pi + 1) * NQT + NQT - 1]])
                ln_ = sch.op("scalar", lambda s, pi=pi: s.activation(
                    out=ctile[0:1, 4 + pi:5 + pi], in_=ctile[0:1, 4 + pi:5 + pi],
                    func=AF.Ln), deps=[pr])
                ex_ = sch.op("scalar", lambda s, pi=pi: s.activation(
                    out=ctile[0:1, 4 + pi:5 + pi], in_=ctile[0:1, 4 + pi:5 + pi],
                    func=AF.Exp, bias=lnbias[0:1, 0:1], scale=0.5), deps=[ln_, ms_lnb])
                ng = sch.op("vector", lambda v, pi=pi: v.tensor_scalar(
                    ctile[0:1, 4 + pi:5 + pi], ctile[0:1, 4 + pi:5 + pi],
                    -1.0, 8.0, OP.mult, OP.add), deps=[ex_])
                bcd = [ng]
                if pi == 0:
                    bcd.append(ev_bvB if grp == 0 else lc_prev)
                else:
                    bcd.append(cb_ready[0])
                bc = sch.op("tensor", lambda t, pi=pi: t.matmul(
                    psR[:, 0:1], ones_row[:, :], ctile[0:1, 4 + pi:5 + pi],
                    start=True, stop=True), deps=bcd)
                cb = sch.op("vector", lambda v, pi=pi: v.tensor_copy(
                    cbias[:, pi:pi + 1], psR[:, 0:1]), deps=[bc])
                cb_ready.append(cb)
            v_evac = {}
            for tt in range(NQT):
                for dq in range(4):
                    i = g_tile
                    deps = [stage[tt]]
                    if i >= 2:
                        deps.append(tile_user[i - 2])
                    ld = sch.op("sync", lambda q, dq=dq, tt=tt, i=i: q.dma_start(
                        out=xt_t[i % 2][:, :].rearrange("p (j t) -> p j t", j=4),
                        in_=xbf.ap().rearrange("(dc p) t -> p dc t", p=128)[
                            :, dq * 4:(dq + 1) * 4, tt * QT:(tt + 1) * QT]),
                        deps=deps, dma=True)
                    last = None
                    for j in range(4):
                        dc = dq * 4 + j
                        for ts in range(4):
                            b = (tt % 2) * 4 + ts
                            d_ = [ld]
                            if dc == 0:
                                if grp == 0 and tt == 0:
                                    d_.append(wloads[-1])
                                if tt >= 2:
                                    d_.append(v_evac[(tt - 2, ts, 1)])
                                else:
                                    d_.append(qk_evac[(NQT - 2 + tt, 1, 1)])
                                    d_.append(red_ops[4 * NQT - 1])
                                    d_.append(cb_ready[-1])
                            last = sch.op("tensor",
                                          lambda t, i=i, ts=ts, j=j, b=b, dc=dc,
                                          grp=grp:
                                          t.matmul(bank(b, 0, 2 * E),
                                                   xt_t[i % 2][:, j * QT + ts * 128:
                                                        j * QT + (ts + 1) * 128],
                                                   wv_s[:, dc * HPC * E + grp * 2 * E:
                                                        dc * HPC * E
                                                        + (grp * 2 + 2) * E],
                                                   start=(dc == 0),
                                                   stop=(dc == DC - 1)),
                                          deps=d_)
                    tile_user[i] = last
                    g_tile += 1
                    if dq == 3:
                        for ts in range(4):
                            b = (tt % 2) * 4 + ts
                            tc = tt * 4 + ts
                            for pi in range(2):
                                h = grp * 2 + pi
                                ev = sch.op("vector", lambda v, pi=pi, h=h, b=b, tc=tc:
                                            v.tensor_add(
                                                v_s[pi][:, tc * 128:(tc + 1) * 128],
                                                bank(b, pi * 128, (pi + 1) * 128),
                                                bvB[:, h * 128:(h + 1) * 128]),
                                            deps=[last, ev_bvB])
                                v_evac[(tt, ts, pi)] = ev
            last_v_evac = v_evac[(NQT - 1, 3, 1)]
            lc_prev = cb_ready[1]
            for pi in range(2):
                pair_h = grp * 2 + pi

                def emit_S(n, qi, pl, gp):
                    pp = psA if gp % 2 == 0 else psB
                    d_ = []
                    if gp >= 2 and gp - 2 in exp_g:
                        d_.append(exp_g[gp - 2])
                    else:
                        d_.append(last_v_evac)
                    if n < 2:
                        d_.append(last_v_evac)
                        d_ += cb_ready
                    sch.op("tensor", lambda t, pp=pp, pi=pi, qi=qi, pl=pl:
                           t.matmul(pp[:, 0:QT],
                                    k_s[pi][:, (2 * pl) * 128:(2 * pl + 1) * 128],
                                    q_s[pi][:, qi * QT:(qi + 1) * QT],
                                    start=True, stop=True), deps=d_)
                    sc2 = sch.op("tensor", lambda t, pp=pp, pi=pi, qi=qi, pl=pl:
                                 t.matmul(pp[:, QT:2 * QT],
                                          k_s[pi][:, (2 * pl + 1) * 128:
                                                  (2 * pl + 2) * 128],
                                          q_s[pi][:, qi * QT:(qi + 1) * QT],
                                          start=True, stop=True))
                    return sc2

                def emit_EXP(n, qi, pl, gp, sc2):
                    pp = psA if gp % 2 == 0 else psB
                    ed = [sc2, cb_ready[pi]]
                    if gp >= 2 and gp - 2 in pv_g:
                        ed.append(pv_g[gp - 2])
                    ex = sch.op("scalar", lambda s, pp=pp, gp=gp, pi=pi:
                                s.activation(out=pT[gp % 2][:, :], in_=pp[:, :],
                                             func=AF.Exp, bias=cbias[:, pi:pi + 1],
                                             scale=INV_SCALE), deps=ed)
                    exp_g[gp] = ex
                    return ex

                def emit_PVL(n, qi, pl, npair, gp):
                    nonlocal gqi, lb_prev, lc_prev
                    first = pl == 0
                    last_c = pl == npair - 1
                    zb = psZ[gqi % 2]
                    pvd = [exp_g[gp]]
                    if pl >= npair - 2:
                        mp = pl - (npair - 2)
                        mko = sch.op("vector", lambda v, gp=gp, mp=mp:
                                     v.tensor_mul(pT[gp % 2][:, :], pT[gp % 2][:, :],
                                                  mask_s[:, mp * 2 * QT:
                                                         (mp + 1) * 2 * QT]),
                                     deps=[exp_g[gp], ld_mk])
                        pvd = [mko]
                    if first:
                        if gqi >= 2:
                            pvd.append(zn_g[gqi - 2])
                        else:
                            pvd.append(last_v_evac)
                    for half in range(2):
                        kc = 2 * pl + half
                        sch.op("tensor", lambda t, zb=zb, pi=pi, kc=kc, gp=gp,
                               half=half, st=(first and half == 0),
                               sp=(last_c and half == 1):
                               t.matmul(zb[:, :],
                                        v_s[pi][:, kc * 128:(kc + 1) * 128],
                                        pT[gp % 2][:, half * QT:(half + 1) * QT],
                                        start=st, stop=sp), deps=pvd if half == 0
                               else [])
                    for half in range(2):
                        d_ = []
                        if half == 0 and first:
                            d_ = [rc_g[gqi - 1]] if gqi >= 1 else [last_v_evac]
                        pv = sch.op("tensor", lambda t, gp=gp, half=half,
                                    st=(first and half == 0),
                                    sp=(last_c and half == 1):
                                    t.matmul(psL[0:1, 0:QT], onesb_col[:, :],
                                             pT[gp % 2][:, half * QT:(half + 1) * QT],
                                             start=st, stop=sp), deps=d_)
                    pv_g[gp] = pv
                    if last_c:
                        rcd = [pv]
                        if lb_prev is not None:
                            rcd.append(lb_prev)
                        rc = sch.op("vector", lambda v: v.reciprocal(
                            lrec[:, :], psL[0:1, 0:QT]), deps=rcd)
                        rc_g[gqi] = rc
                        lbd = [rc, lc_prev]
                        lb = sch.op("tensor", lambda t: t.matmul(
                            psR[:, :], ones_row[:, :], lrec[:, :],
                            start=True, stop=True), deps=lbd)
                        lb_prev = lb
                        lc = sch.op("vector", lambda v: v.tensor_copy(
                            lrecB[:, :], psR[:, :]), deps=[lb])
                        lc_prev = lc
                        znd = [lc]
                        if len(zstores) >= 2:
                            znd.append(zstores[-2])
                        zn = sch.op("vector", lambda v, gqi=gqi, zb=zb: v.tensor_mul(
                            zn_s[gqi % 2][:, :], zb[:, :], lrecB[:, :]), deps=znd)
                        zn_g[gqi] = zn
                        st = sch.op("sync", lambda q, pair_h=pair_h, qi=qi, gqi=gqi:
                                    q.dma_start(
                                        out=z_out[pair_h * E:(pair_h + 1) * E,
                                                  qi * QT:(qi + 1) * QT],
                                        in_=zn_s[gqi % 2][:, :]), deps=[zn], dma=True)
                        zstores.append(st)
                        gqi += 1

                plist = []
                for qi in range(NQT):
                    npair = 2 * qi + 2
                    for pl in range(npair):
                        plist.append((qi, pl, npair))
                prev = None
                for n, (qi, pl, npair) in enumerate(plist):
                    gp = gci
                    gci += 1
                    sc2 = emit_S(n, qi, pl, gp)
                    emit_EXP(n, qi, pl, gp, sc2)
                    if prev is not None:
                        emit_PVL(*prev)
                    prev = (n, qi, pl, npair, gp)
                emit_PVL(*prev)
            attn_prev_tensor = pv_g[gci - 1]
            attn_prev_vector = zn_g[gqi - 1]
            attn_prev_scalar = exp_g[gci - 1]
            cchain_last = zn_g[gqi - 1]
        sch.op("sync", lambda q: q.nop(), deps=zstores)
        sch.build(nc, ctx, block)
    return nc


def build_l2(paranoid=True, serialize_dma=False):
    nc = bass.Bass("TRN2", target_bir_lowering=False, num_devices=NCORES)
    z_in = nc.dram_tensor("z16", [H, E, 2048], F16, kind="ExternalInput").ap()
    wo_in = nc.dram_tensor("wo", [H, E, D], F32, kind="ExternalInput").ap()
    bo_in = nc.dram_tensor("bo", [1, D], F32, kind="ExternalInput").ap()
    xo_in = nc.dram_tensor("x_own", [2048, D], F32, kind="ExternalInput").ap()
    wrt_in = nc.dram_tensor("wrt", [1, D], F32, kind="ExternalInput").ap()
    ones_in = nc.dram_tensor("ones128", [1, 128], F32, kind="ExternalInput").ap()
    x1_out = nc.dram_tensor("x1", [2048, D], F32, kind="ExternalOutput").ap()
    x1h_out = nc.dram_tensor("x1h", [2048, D], F16, kind="ExternalOutput").ap()
    sc_out = nc.dram_tensor("scores", [128, 16], F32, kind="ExternalOutput").ap()

    NTC = 16
    sch = Sched(paranoid=paranoid, serialize_dma=serialize_dma)
    with ExitStack() as ctx:
        ee = ctx.enter_context
        z_s = ee(nc.sbuf_tensor("z_s", [128, H * 2048], F16))
        wo_s = ee(nc.sbuf_tensor("wo_s", [128, H * D], F16))
        bo_row = ee(nc.sbuf_tensor("bo_row", [1, D], F32))
        wrt_row = ee(nc.sbuf_tensor("wrt_row", [1, D], F32))
        ones_row = ee(nc.sbuf_tensor("ones_row", [1, 128], F32))
        boB = ee(nc.sbuf_tensor("boB", [128, D], F32))
        wrtB = ee(nc.sbuf_tensor("wrtB", [128, D], F32))
        xo_t = [ee(nc.sbuf_tensor(f"xo_t{i}", [128, D], F32)) for i in range(2)]
        r_t = ee(nc.sbuf_tensor("r_t", [128, D], F32))
        x1_t = [ee(nc.sbuf_tensor(f"x1_t{i}", [128, D], F32)) for i in range(2)]
        st1 = ee(nc.sbuf_tensor("st1", [128, 8], F32))
        sc_s = ee(nc.sbuf_tensor("sc_s", [128, 16], F32))
        eps_c = ee(nc.sbuf_tensor("eps_c", [128, 1], F32))
        ps = [ee(nc.psum_tensor(f"ps{i}", [128, QT], F32)) for i in range(8)]
        block = ee(nc.Block())

        ld_z = [sch.op("gpsimd", lambda g, h=h: g.dma_start(
            out=z_s[:, h * 2048:(h + 1) * 2048], in_=z_in[h]), dma=True)
            for h in range(H)]
        ld_wo = [sch.op("gpsimd", lambda g, h=h: g.dma_start(
            out=wo_s[:, h * D:(h + 1) * D], in_=wo_in[h]), dma=True)
            for h in range(H)]
        ld_bo = sch.op("gpsimd", lambda g: g.dma_start(out=bo_row[:, :], in_=bo_in), dma=True)
        ld_wr = sch.op("gpsimd", lambda g: g.dma_start(out=wrt_row[:, :], in_=wrt_in), dma=True)
        ld_on = sch.op("gpsimd", lambda g: g.dma_start(out=ones_row[:, :], in_=ones_in), dma=True)
        ms_eps = sch.op("vector", lambda v: v.memset(eps_c[:, :], EPS_LN))
        bco = []
        for j in range(4):
            m1 = sch.op("tensor", lambda t, j=j: t.matmul(
                ps[0][:, :], ones_row[:, :], bo_row[0:1, j * QT:(j + 1) * QT],
                start=True, stop=True), deps=[ld_bo, ld_on] + ([bco[-1]] if bco else []))
            c1 = sch.op("vector", lambda v, j=j: v.tensor_copy(
                boB[:, j * QT:(j + 1) * QT], ps[0][:, :]), deps=[m1])
            m2 = sch.op("tensor", lambda t, j=j: t.matmul(
                ps[1][:, :], ones_row[:, :], wrt_row[0:1, j * QT:(j + 1) * QT],
                start=True, stop=True), deps=[ld_wr, c1])
            c2 = sch.op("vector", lambda v, j=j: v.tensor_copy(
                wrtB[:, j * QT:(j + 1) * QT], ps[1][:, :]), deps=[m2])
            bco.append(c2)
        bcast_done = bco[-1]

        x1_stores = []
        x1h_stores = []
        prev_epi = {}
        xb_ops = {}
        rt_free = {}
        x1o_prev = None
        lnv_prev = None
        last_score = None
        for tc in range(NTC):
            par = tc % 2
            ld_x = sch.op("sync", lambda q, tc=tc: q.dma_start(
                out=xo_t[tc % 2][:, :], in_=xo_in[tc * 128:(tc + 1) * 128, :]),
                deps=[xb_ops.get(tc - 2)], dma=True)
            mm_last = {}
            for dblk in range(4):
                bank = par * 4 + dblk
                last = None
                for h in range(H):
                    d_ = [ld_z[h], ld_wo[h]] if tc == 0 else []
                    if h == 0:
                        d_ += [bcast_done]
                        if tc >= 2:
                            d_.append(prev_epi[par])
                    last = sch.op("tensor", lambda t, h=h, dblk=dblk, bank=bank, tc=tc:
                                  t.matmul(ps[bank][:, :],
                                           z_s[:, h * 2048 + tc * 128:
                                               h * 2048 + (tc + 1) * 128],
                                           wo_s[:, h * D + dblk * QT:
                                                h * D + (dblk + 1) * QT],
                                           start=(h == 0), stop=(h == H - 1)), deps=d_)
                mm_last[dblk] = last
            xb = sch.op("vector", lambda v, tc=tc: v.tensor_add(
                xo_t[tc % 2][:, :], xo_t[tc % 2][:, :], boB[:, :]),
                deps=[ld_x, bcast_done])
            xb_ops[tc] = xb
            ep = None
            for dblk in range(4):
                d_ = [mm_last[dblk], xb]
                if dblk == 0 and tc >= 1:
                    d_.append(rt_free[tc - 1])
                ep = sch.op("vector", lambda v, tc=tc, dblk=dblk, bank=par * 4 + dblk:
                            v.tensor_add(r_t[:, dblk * QT:(dblk + 1) * QT],
                                         ps[bank][:, :],
                                         xo_t[tc % 2][:, dblk * QT:(dblk + 1) * QT]),
                            deps=d_)
            prev_epi[par] = ep
            sm = sch.op("vector", lambda v: v.reduce_sum(
                st1[:, 0:1], r_t[:, :], axis=AX.X), deps=[ep])
            s2 = sch.op("scalar", lambda s, tc=tc: s.activation(
                out=x1_t[tc % 2][:, :], in_=r_t[:, :], func=AF.Square,
                accum_out=st1[:, 1:2]),
                deps=[ep, x1o_prev,
                      x1_stores[-2] if len(x1_stores) >= 2 else None,
                      x1h_stores[-2] if len(x1h_stores) >= 2 else None])
            mean = sch.op("vector", lambda v: v.tensor_scalar_mul(
                st1[:, 2:3], st1[:, 0:1], 1.0 / D), deps=[sm])
            msq = sch.op("vector", lambda v: v.tensor_scalar_mul(
                st1[:, 3:4], st1[:, 1:2], 1.0 / D), deps=[s2])
            m2_ = sch.op("vector", lambda v: v.tensor_mul(
                st1[:, 4:5], st1[:, 2:3], st1[:, 2:3]), deps=[mean])
            var = sch.op("vector", lambda v: v.tensor_sub(
                st1[:, 5:6], st1[:, 3:4], st1[:, 4:5]), deps=[msq, m2_, lnv_prev])
            lnv = sch.op("scalar", lambda s: s.activation(
                out=st1[:, 6:7], in_=st1[:, 5:6], func=AF.Ln,
                bias=eps_c[:, 0:1]), deps=[var, ms_eps])
            lnv_prev = lnv
            rstd = sch.op("scalar", lambda s: s.activation(
                out=st1[:, 7:8], in_=st1[:, 6:7], func=AF.Exp,
                scale=-0.5), deps=[lnv])
            x1o = sch.op("vector", lambda v, tc=tc: v.tensor_scalar(
                x1_t[tc % 2][:, :], r_t[:, :], st1[:, 2:3], st1[:, 7:8],
                OP.subtract, OP.mult),
                deps=[rstd, x1_stores[-2] if len(x1_stores) >= 2 else None,
                      x1h_stores[-2] if len(x1h_stores) >= 2 else None])
            x1o_prev = x1o
            st = sch.op("sync", lambda q, tc=tc: q.dma_start(
                out=x1_out[tc * 128:(tc + 1) * 128, :], in_=x1_t[tc % 2][:, :]),
                deps=[x1o], dma=True)
            sth = sch.op("gpsimd", lambda g, tc=tc: g.dma_start(
                out=x1h_out[tc * 128:(tc + 1) * 128, :], in_=x1_t[tc % 2][:, :]),
                deps=[x1o], dma=True)
            x1_stores.append(st)
            x1h_stores.append(sth)
            scp = sch.op("vector", lambda v, tc=tc: v.tensor_mul(
                r_t[:, :], x1_t[tc % 2][:, :], wrtB[:, :]), deps=[x1o])
            red = sch.op("vector", lambda v, tc=tc: v.reduce_sum(
                sc_s[:, tc:tc + 1], r_t[:, :], axis=AX.X), deps=[scp])
            rt_free[tc] = red
            last_score = red
        st_sc = sch.op("sync", lambda q: q.dma_start(out=sc_out, in_=sc_s[:, :]),
                       deps=[last_score], dma=True)
        sch.op("sync", lambda q: q.nop(),
               deps=x1_stores + x1h_stores + [st_sc])
        sch.build(nc, ctx, block)
    return nc


BISECT_ITERS = 44


def build_l3(paranoid=True, serialize_dma=False):
    nc = bass.Bass("TRN2", target_bir_lowering=False, num_devices=NCORES)
    x1f = nc.dram_tensor("x1f", [2 * T, D], F16, kind="ExternalInput").ap()
    sct_in = nc.dram_tensor("sctile", [128, 128], F32, kind="ExternalInput").ap()
    pos_in = nc.dram_tensor("postile", [128, 128], F32, kind="ExternalInput").ap()
    onesm_in = nc.dram_tensor("onesmat", [128, 128], F32, kind="ExternalInput").ap()
    ones_in = nc.dram_tensor("ones128", [1, 128], F32, kind="ExternalInput").ap()
    tril_in = nc.dram_tensor("trilT", [128, 128], F16, kind="ExternalInput").ap()
    siota_in = nc.dram_tensor("siota", [128, 1024], F16, kind="ExternalInput").ap()
    rot_in = nc.dram_tensor("rotT", [128, 128], F16, kind="ExternalInput").ap()
    invf_in = nc.dram_tensor("invf", [128, 1], F32, kind="ExternalInput").ap()
    ident_in = nc.dram_tensor("ident", [128, 128], F32, kind="ExternalInput").ap()
    identh_in = nc.dram_tensor("identh", [128, 128], F16, kind="ExternalInput").ap()
    pick_in = nc.dram_tensor("pickmat", [4, 4], F32, kind="ExternalInput").ap()
    mask_in = nc.dram_tensor("masks", [4, 128, QT], F16, kind="ExternalInput").ap()
    wq_in = nc.dram_tensor("wq_sel", [D, 2 * E], F32, kind="ExternalInput").ap()
    wk_in = nc.dram_tensor("wk_sel", [D, E], F32, kind="ExternalInput").ap()
    wv_in = nc.dram_tensor("wv_sel", [D, E], F32, kind="ExternalInput").ap()
    bq_in = nc.dram_tensor("bq_sel", [128, 2], F32, kind="ExternalInput").ap()
    bk_in = nc.dram_tensor("bk_sel", [128, 1], F32, kind="ExternalInput").ap()
    bv_in = nc.dram_tensor("bv_sel", [1, 128], F32, kind="ExternalInput").ap()
    ln1_in = nc.dram_tensor("ln1c", [128, 16], F32, kind="ExternalInput").ap()

    pl_out = nc.dram_tensor("poslist", [2, S_SEL], F32, kind="ExternalOutput").ap()
    srg_out = nc.dram_tensor("srg", [128, 128], F32, kind="ExternalOutput").ap()
    selx_out = nc.dram_tensor("selx", [S_SEL, D], F16, kind="ExternalOutput").ap()
    zsel_out = nc.dram_tensor("z_sel", [2 * E, S_SEL], F16, kind="ExternalOutput").ap()
    selbf = nc.dram_tensor("selbf", [S_SEL, D], F16, kind="Internal")
    rstdd = nc.dram_tensor("rstdd", [1, S_SEL], F32, kind="Internal")

    sch = Sched(paranoid=paranoid, serialize_dma=serialize_dma)
    with ExitStack() as ctx:
        ee = ctx.enter_context
        sct = ee(nc.sbuf_tensor("sct", [128, 128], F32))
        post = ee(nc.sbuf_tensor("post", [128, 128], F32))
        onesm = ee(nc.sbuf_tensor("onesm_s", [128, 128], F32))
        ones_row = ee(nc.sbuf_tensor("ones_row", [1, 128], F32))
        ones16c = ee(nc.sbuf_tensor("ones16c", [128, 1], F16))
        tril = ee(nc.sbuf_tensor("tril", [128, 128], F16))
        siota = ee(nc.sbuf_tensor("siota_s", [128, 1024], F16))
        rott = ee(nc.sbuf_tensor("rott", [128, 128], F16))
        invf = ee(nc.sbuf_tensor("invf_s", [128, 1], F32))
        ident = ee(nc.sbuf_tensor("ident_s", [128, 128], F32))
        identh = ee(nc.sbuf_tensor("identh_s", [128, 128], F16))
        pick = ee(nc.sbuf_tensor("pick", [4, 4], F32))
        mask_s = ee(nc.sbuf_tensor("mask_s", [128, 4 * QT], F16))
        lohi = ee(nc.sbuf_tensor("lohi", [128, 4], F32))
        mid = ee(nc.sbuf_tensor("mid", [128, 2], F32))
        cmp_t = ee(nc.sbuf_tensor("cmp_t", [128, 128], F32))
        cnt = ee(nc.sbuf_tensor("cnt", [128, 2], F32))
        cntR = ee(nc.sbuf_tensor("cntR", [128, 2], F32))
        ge_t = ee(nc.sbuf_tensor("ge_t", [128, 2], mybir.dt.uint8))
        gn_t = ee(nc.sbuf_tensor("gn_t", [128, 2], mybir.dt.uint8))
        m4 = ee(nc.sbuf_tensor("m4", [128, 128], F32))
        r4 = ee(nc.sbuf_tensor("r4", [4, 1], F32))
        maskf = ee(nc.sbuf_tensor("maskf", [128, 128], F32))
        mask16 = ee(nc.sbuf_tensor("mask16", [128, 128], F16))
        srank = ee(nc.sbuf_tensor("srank", [128, 128], F32))
        srg_s = ee(nc.sbuf_tensor("srg_s", [128, 128], F32))
        scan_a = ee(nc.sbuf_tensor("scan_a", [1, 64], F32))
        scan_b = ee(nc.sbuf_tensor("scan_b", [1, 64], F32))
        ms1 = ee(nc.sbuf_tensor("ms1", [128, 128], F32))
        ms2 = ee(nc.sbuf_tensor("ms2", [128, 128], F32))
        oh = [ee(nc.sbuf_tensor(f"oh{i}", [128, 1024], F16)) for i in range(2)]
        pos_s = ee(nc.sbuf_tensor("pos_s", [1, S_SEL], F32))
        sc_r = ee(nc.sbuf_tensor("sc_r", [1, S_SEL], F32))
        posB = ee(nc.sbuf_tensor("posB", [128, S_SEL], F32))
        posT = ee(nc.sbuf_tensor("posT", [128, 16], F32))
        idx_u = ee(nc.sbuf_tensor("idx_u", [128, 16], U32))
        gx = [ee(nc.sbuf_tensor(f"gx{i}", [128, D], F16)) for i in range(2)]
        rst = ee(nc.sbuf_tensor("rst", [128, 80], F32))
        rstrow = ee(nc.sbuf_tensor("rstrow", [1, S_SEL], F32))
        rstdB = ee(nc.sbuf_tensor("rstdB", [128, S_SEL], F32))
        tmpT = [ee(nc.sbuf_tensor(f"tmpT{i}", [128, D], F16)) for i in range(2)]
        hT = ee(nc.sbuf_tensor("hT", [128, DC * S_SEL], F16))
        wq16 = ee(nc.sbuf_tensor("wq16", [128, DC * 2 * E], F16))
        wk16 = ee(nc.sbuf_tensor("wk16", [128, DC * E], F16))
        wv16 = ee(nc.sbuf_tensor("wv16", [128, DC * E], F16))
        bq_s = ee(nc.sbuf_tensor("bq_s", [128, 2], F32))
        bk_s = ee(nc.sbuf_tensor("bk_s", [128, 1], F32))
        bv_row = ee(nc.sbuf_tensor("bv_row", [1, 128], F32))
        bvB = ee(nc.sbuf_tensor("bvB", [128, 128], F32))
        qT = [ee(nc.sbuf_tensor(f"qT{i}", [128, S_SEL], F16)) for i in range(2)]
        kT = ee(nc.sbuf_tensor("kT", [128, S_SEL], F16))
        v16 = ee(nc.sbuf_tensor("v16", [128, S_SEL], F16))
        angB = ee(nc.sbuf_tensor("angB", [128, QT], F32))
        redu = ee(nc.sbuf_tensor("redu", [128, QT], F32))
        cosB = ee(nc.sbuf_tensor("cosB", [128, QT], F32))
        sinB = ee(nc.sbuf_tensor("sinB", [128, QT], F32))
        sq_t = [ee(nc.sbuf_tensor(f"sq_t{i}", [128, QT], F16)) for i in range(2)]
        ctile = ee(nc.sbuf_tensor("ctile", [1, 8], F32))
        cm = ee(nc.sbuf_tensor("cm", [1, 4], F32))
        lnbias = ee(nc.sbuf_tensor("lnbias", [1, 1], F32))
        cbias = ee(nc.sbuf_tensor("cbias", [128, 2], F32))
        pT = [ee(nc.sbuf_tensor(f"pT{i}", [128, QT], F16)) for i in range(2)]
        zn_s = [ee(nc.sbuf_tensor(f"zn_s{i}", [128, QT], F16)) for i in range(2)]
        lrec = ee(nc.sbuf_tensor("lrec", [1, QT], F32))
        lrecB = ee(nc.sbuf_tensor("lrecB", [128, QT], F32))
        epsR = ee(nc.sbuf_tensor("epsR", [128, 1], F32))
        ps = [ee(nc.psum_tensor(f"ps{i}", [128, QT], F32)) for i in range(8)]
        block = ee(nc.Block())

        lds = []
        for (dst, src) in ((sct, sct_in), (post, pos_in), (onesm, onesm_in),
                           (ones_row, ones_in), (tril, tril_in), (siota, siota_in),
                           (rott, rot_in), (invf, invf_in), (ident, ident_in),
                           (pick, pick_in), (bq_s, bq_in), (bk_s, bk_in),
                           (bv_row, bv_in)):
            lds.append(sch.op("gpsimd", lambda g, dst=dst, src=src:
                              g.dma_start(out=dst[:, :], in_=src), dma=True))
        ld_idh = sch.op("gpsimd", lambda g: g.dma_start(
            out=identh[:, :], in_=identh_in), dma=True)
        ld_mask = sch.op("gpsimd", lambda g: g.dma_start(
            out=mask_s[:, :].rearrange("p (c q) -> p c q", c=4),
            in_=mask_in.rearrange("c p q -> p c q")), dma=True)
        ld_wq = sch.op("gpsimd", lambda g: g.dma_start(
            out=wq16[:, :].rearrange("p (dc e) -> p dc e", dc=DC),
            in_=wq_in.rearrange("(dc p) e -> p dc e", p=128)), dma=True)
        ld_wk = sch.op("gpsimd", lambda g: g.dma_start(
            out=wk16[:, :].rearrange("p (dc e) -> p dc e", dc=DC),
            in_=wk_in.rearrange("(dc p) e -> p dc e", p=128)), dma=True)
        ld_wv = sch.op("gpsimd", lambda g: g.dma_start(
            out=wv16[:, :].rearrange("p (dc e) -> p dc e", dc=DC),
            in_=wv_in.rearrange("(dc p) e -> p dc e", p=128)), dma=True)
        ms_ones16 = sch.op("vector", lambda v: v.memset(ones16c[:, :], 1.0))
        ms_eps = sch.op("vector", lambda v: v.memset(epsR[:, :], EPS_RMS))
        ms_lnb = sch.op("vector", lambda v: v.memset(lnbias[:, :], -LN_ATTN_SCALE))
        ln1c = ee(nc.sbuf_tensor("ln1c_s", [128, 16], F32))
        ld_ln1 = sch.op("gpsimd", lambda g: g.dma_start(
            out=ln1c[:, :], in_=ln1_in), dma=True)

        i_mn0 = sch.op("vector", lambda v: v.tensor_reduce(
            m4[:, 0:1], sct[:, 0:64], AX.X, OP.min), deps=[lds[0]])
        i_mx0 = sch.op("vector", lambda v: v.tensor_reduce(
            m4[:, 2:3], sct[:, 0:64], AX.X, OP.max))
        i_mn1 = sch.op("vector", lambda v: v.tensor_reduce(
            m4[:, 1:2], sct[:, 64:128], AX.X, OP.min))
        i_mx1 = sch.op("vector", lambda v: v.tensor_reduce(
            m4[:, 3:4], sct[:, 64:128], AX.X, OP.max))
        i_ng = sch.op("vector", lambda v: v.tensor_scalar_mul(
            m4[:, 2:4], m4[:, 2:4], -1.0), deps=[i_mx0, i_mx1])
        tr4 = sch.op("tensor", lambda t: t.transpose(
            ps[0][:, 0:128], m4[:, :], ident[:, :]),
            deps=[i_mn0, i_mn1, i_ng, lds[8]])
        rmn = sch.op("vector", lambda v: v.tensor_reduce(
            r4[0:4, 0:1], ps[0][0:4, 0:128], AX.X, OP.min), deps=[tr4])
        rowm = sch.op("tensor", lambda t: t.matmul(
            ps[1][0:1, 0:4], r4[0:4, 0:1], pick[0:4, 0:4],
            start=True, stop=True), deps=[rmn, lds[9]])
        row4e = sch.op("vector", lambda v: v.tensor_copy(
            scan_a[0:1, 0:4], ps[1][0:1, 0:4]), deps=[rowm])
        rep = sch.op("tensor", lambda t: t.matmul(
            ps[2][:, 0:4], ones_row[:, :], scan_a[0:1, 0:4],
            start=True, stop=True), deps=[row4e, lds[3]])
        cp_lh = sch.op("vector", lambda v: v.tensor_copy(lohi[:, :], ps[2][:, 0:4]),
                       deps=[rep])
        h1 = sch.op("vector", lambda v: v.tensor_scalar_mul(
            mid[:, 0:2], lohi[:, 2:4], -1.0), deps=[cp_lh])
        h2 = sch.op("vector", lambda v: v.tensor_max(
            mid[:, 0:2], mid[:, 0:2], lohi[:, 2:4]), deps=[h1])
        h3 = sch.op("vector", lambda v: v.tensor_scalar(
            mid[:, 0:2], mid[:, 0:2], 1e-4, 1e-20, OP.mult, OP.add), deps=[h2])
        hfin = sch.op("vector", lambda v: v.tensor_add(
            lohi[:, 2:4], lohi[:, 2:4], mid[:, 0:2]), deps=[h3])
        prev = hfin
        prev_mm = rep
        prev_gno = cp_lh
        for it in range(BISECT_ITERS):
            m1 = sch.op("vector", lambda v: v.tensor_scalar_mul(
                mid[:, :], lohi[:, 0:2], 0.5), deps=[prev])
            m2 = sch.op("vector", lambda v: v.scalar_tensor_tensor(
                cnt[:, :], lohi[:, 2:4], 0.5, mid[:, :], OP.mult, OP.add),
                deps=[m1])
            c0 = sch.op("vector", lambda v: v.tensor_scalar(
                cmp_t[:, 0:64], sct[:, 0:64], cnt[:, 0:1], 0.0, OP.is_ge,
                OP.add, accum_out=cntR[:, 0:1]), deps=[m2])
            c1 = sch.op("vector", lambda v: v.tensor_scalar(
                cmp_t[:, 64:128], sct[:, 64:128], cnt[:, 1:2], 0.0, OP.is_ge,
                OP.add, accum_out=cntR[:, 1:2]), deps=[c0])
            mmc = sch.op("tensor", lambda t: t.matmul(
                ps[3][:, 0:2], onesm[:, :], cntR[:, :], start=True, stop=True),
                deps=[c1, lds[2], prev_mm, prev_gno])
            prev_mm = mmc
            geo = sch.op("vector", lambda v: v.tensor_scalar(
                ge_t[:, :], ps[3][:, 0:2], float(SELK), None, OP.is_ge), deps=[mmc])
            gno = sch.op("vector", lambda v: v.tensor_scalar(
                gn_t[:, :], ps[3][:, 0:2], float(SELK), None, OP.is_lt), deps=[geo])
            prev_gno = gno
            s_lo = sch.op("vector", lambda v: v.copy_predicated(
                lohi[:, 0:2], ge_t[:, :], cnt[:, :]), deps=[geo])
            s_hi = sch.op("vector", lambda v: v.copy_predicated(
                lohi[:, 2:4], gn_t[:, :], cnt[:, :]), deps=[gno, s_lo])
            prev = s_hi
        thr_ready = prev

        mks = []
        srk = []
        bca_prev = None
        cpy_prev = None
        for b in range(2):
            mk = sch.op("vector", lambda v, b=b: v.tensor_scalar(
                maskf[:, 64 * b:64 * b + 64], sct[:, 64 * b:64 * b + 64],
                lohi[:, b:b + 1], None, OP.is_ge), deps=[thr_ready])
            mk16 = sch.op("vector", lambda v, b=b: v.tensor_copy(
                mask16[:, 64 * b:64 * b + 64], maskf[:, 64 * b:64 * b + 64]),
                deps=[mk])
            cumm = sch.op("tensor", lambda t, b=b: t.matmul(
                ps[4 + b][:, 0:64], tril[:, :], mask16[:, 64 * b:64 * b + 64],
                start=True, stop=False), deps=[mk16, lds[4], prev_mm])
            cs = sch.op("tensor", lambda t, b=b: t.matmul(
                ps[6][0:1, 64 * b:64 * b + 64], ones16c[:, :],
                mask16[:, 64 * b:64 * b + 64],
                start=True, stop=True), deps=[mk16, ms_ones16, cpy_prev])
            cpy = sch.op("vector", lambda v, b=b: v.tensor_copy(
                scan_a[0:1, :], ps[6][0:1, 64 * b:64 * b + 64]),
                deps=[cs, bca_prev])
            cpy_prev = cpy
            pr = cpy
            cur, oth = scan_a, scan_b
            for sh in (1, 2, 4, 8, 16, 32):
                cp1 = sch.op("vector", lambda v, cur=cur, oth=oth, sh=sh:
                             v.tensor_copy(oth[0:1, 0:sh], cur[0:1, 0:sh]), deps=[pr])
                ad1 = sch.op("vector", lambda v, cur=cur, oth=oth, sh=sh:
                             v.tensor_add(oth[0:1, sh:64], cur[0:1, sh:64],
                                          cur[0:1, 0:64 - sh]), deps=[cp1])
                pr = ad1
                cur, oth = oth, cur
            z0 = sch.op("vector", lambda v, oth=oth: v.memset(oth[0:1, 0:1], 0.0),
                        deps=[pr])
            sh1 = sch.op("vector", lambda v, cur=cur, oth=oth: v.tensor_copy(
                oth[0:1, 1:64], cur[0:1, 0:63]), deps=[z0])
            bca = sch.op("tensor", lambda t, b=b, oth=oth: t.matmul(
                ps[4 + b][:, 0:64], ones_row[:, :], oth[0:1, 0:64],
                start=False, stop=True), deps=[sh1])
            bca_prev = bca
            sr_a = sch.op("vector", lambda v, b=b: v.tensor_scalar_add(
                cmp_t[:, 64 * b:64 * b + 64], ps[4 + b][:, 0:64],
                float(1e6)), deps=[bca, mk])
            sr = sch.op("vector", lambda v, b=b: v.scalar_tensor_tensor(
                srank[:, 64 * b:64 * b + 64], maskf[:, 64 * b:64 * b + 64],
                -1000001.0, cmp_t[:, 64 * b:64 * b + 64], OP.mult, OP.add),
                deps=[sr_a])
            geb = sch.op("vector", lambda v, b=b: v.tensor_scalar(
                cmp_t[:, 64 * b:64 * b + 64], srank[:, 64 * b:64 * b + 64],
                float(1024), None, OP.is_ge), deps=[sr])
            cl = sch.op("vector", lambda v, b=b: v.scalar_tensor_tensor(
                srank[:, 64 * b:64 * b + 64], cmp_t[:, 64 * b:64 * b + 64], 1e6,
                srank[:, 64 * b:64 * b + 64], OP.mult, OP.add), deps=[geb])
            mks.append(mk)
            srk.append(cl)

        sg1 = sch.op("vector", lambda v: v.tensor_copy(
            srg_s[:, 0:64], srank[:, 0:64]), deps=[srk[0]])
        sg2 = sch.op("vector", lambda v: v.tensor_scalar(
            srg_s[:, 64:128], srank[:, 64:128], 1024.0, None, OP.add),
            deps=[srk[1]])
        st_sg = sch.op("sync", lambda q: q.dma_start(
            out=srg_out, in_=srg_s[:, :]), deps=[sg1, sg2], dma=True)

        MOFF = 10 * S_SEL
        mview = hT[:, MOFF:MOFF + 96 * 128].rearrange(
            "p (col c) -> p c col", c=96)
        mz = sch.op("vector", lambda v: v.memset(hT[:, MOFF:MOFF + 96 * 128], 0.0))
        mc = sch.op("vector", lambda v: v.tensor_copy(ms1[:, :], post[:, :]),
                    deps=[lds[1]])
        for kpow in (4096.0, 2048.0, 1024.0, 512.0, 256.0, 128.0, 64.0):
            mb = sch.op("vector", lambda v, kpow=kpow: v.tensor_scalar(
                ms2[:, :], ms1[:, :], kpow, None, OP.is_ge), deps=[mc])
            mc = sch.op("vector", lambda v, kpow=kpow: v.scalar_tensor_tensor(
                ms1[:, :], ms2[:, :], -kpow, ms1[:, :], OP.mult, OP.add),
                deps=[mb])
        m1 = sch.op("vector", lambda v: v.tensor_sub(
            ms2[:, :], post[:, :], ms1[:, :]), deps=[mc])
        m2 = sch.op("vector", lambda v: v.tensor_scalar_mul(
            ms2[:, :], ms2[:, :], 1.0 / 64.0), deps=[m1])
        m3 = sch.op("vector", lambda v: v.tensor_copy(
            mview[:, 0, 0:128], ms2[:, :]), deps=[m2, mz])
        m4_ = sch.op("vector", lambda v: v.tensor_copy(
            mview[:, 32, 0:128], ms1[:, :]), deps=[mc, mz])
        m5 = sch.op("vector", lambda v: v.tensor_copy(
            mview[:, 64, 0:128], sct[:, :]), deps=[lds[0], mz])
        mp = [m3, m4_, m5]
        prev_oh_mm = {}
        for b in range(2):
            for f in range(64):
                col = 64 * b + f
                g = b * 64 + f
                gb, gf = divmod(g - 2, 64)
                oho = sch.op("vector", lambda v, b=b, f=f, col=col: v.tensor_scalar(
                    oh[f % 2][:, :], siota[:, :], srank[:, col:col + 1], None,
                    OP.is_equal), deps=[srk[b], lds[5],
                                        prev_oh_mm.get((gb, gf))])
                last = None
                for half in range(2):
                    last = sch.op("tensor", lambda t, b=b, f=f, half=half, col=col:
                                  t.matmul(ps[2 * b + half][0:96, 0:512],
                                           hT[:, MOFF + col * 96:
                                              MOFF + (col + 1) * 96],
                                           oh[f % 2][:, 512 * half:512 * half + 512],
                                           start=(f == 0), stop=(f == 63)),
                                  deps=[oho] + mp + ([] if f > 0 else [prev_mm]))
                prev_oh_mm[(b, f)] = last
        ev_pl = []
        for b in range(2):
            for half in range(2):
                sl = slice(1024 * b + 512 * half, 1024 * b + 512 * half + 512)
                pb_ = ps[2 * b + half]
                dep0 = [prev_oh_mm[(b, 63)]]
                e1 = sch.op("vector", lambda v, sl=sl, pb_=pb_: v.tensor_scalar(
                    pos_s[0:1, sl], pb_[0:1, 0:512], 64.0, None, OP.mult),
                    deps=dep0)
                e2 = sch.op("vector", lambda v, sl=sl, pb_=pb_: v.tensor_add(
                    pos_s[0:1, sl], pos_s[0:1, sl], pb_[32:33, 0:512]), deps=[e1])
                e3 = sch.op("vector", lambda v, sl=sl, pb_=pb_: v.tensor_copy(
                    sc_r[0:1, sl], pb_[64:65, 0:512]), deps=[e2])
                ev_pl.append(e3)
        st_pl = sch.op("scalar", lambda s: s.dma_start(
            out=pl_out[0:1, :], in_=pos_s[:, :]), deps=ev_pl, dma=True)
        st_pl2 = sch.op("scalar", lambda s: s.dma_start(
            out=pl_out[1:2, :], in_=sc_r[:, :]), deps=ev_pl, dma=True)

        pb_ev = []
        pbm_prev = None
        for j in range(4):
            pbm = sch.op("tensor", lambda t, j=j: t.matmul(
                ps[4][:, :], ones_row[:, :], pos_s[0:1, j * QT:(j + 1) * QT],
                start=True, stop=True), deps=ev_pl + [pbm_prev])
            pbe = sch.op("vector", lambda v, j=j: v.tensor_copy(
                posB[:, j * QT:(j + 1) * QT], ps[4][:, :]), deps=[pbm])
            pbm_prev = pbe
            pb_ev.append(pbe)
        trp = []
        trp_prev = None
        for c16 in range(16):
            tro = sch.op("tensor", lambda t, c16=c16: t.transpose(
                ps[5][:, 0:128], posB[:, c16 * 128:(c16 + 1) * 128], ident[:, :]),
                deps=[pb_ev[c16 // 4], trp_prev])
            tre = sch.op("vector", lambda v, c16=c16: v.tensor_copy(
                posT[:, c16:c16 + 1], ps[5][:, 0:1]), deps=[tro])
            trp_prev = tre
            trp.append(tre)
        adj = sch.op("vector", lambda v: v.tensor_scalar(
            posT[:, 8:16], posT[:, 8:16], float(T), None, OP.add), deps=trp)
        toi = sch.op("vector", lambda v: v.tensor_copy(idx_u[:, :], posT[:, :]),
                     deps=[adj])

        g_sel_st = []
        rst_st = []
        gpo = []
        prev_sq = None
        tr_ev = {}
        tr_list = []
        for c16 in range(16):
            gth = sch.op("gpsimd", lambda g, c16=c16: g.indirect_dma_start(
                out=gx[c16 % 2][:, :], out_offset=None, in_=x1f,
                in_offset=IndirectOffsetOnAxis(idx_u[:, c16:c16 + 1], 0)),
                deps=[toi, gpo[-2] if len(gpo) >= 2 else None], dma=True)
            st1_ = sch.op("scalar", lambda s, c16=c16: s.dma_start(
                out=selx_out[c16 * 128:(c16 + 1) * 128, :], in_=gx[c16 % 2][:, :]),
                deps=[gth], dma=True)
            g_sel_st.append(st1_)
            sq = sch.op("scalar", lambda s, c16=c16: s.activation(
                out=tmpT[0][:, :], in_=gx[c16 % 2][:, :], func=AF.Square,
                accum_out=rst[:, c16:c16 + 1]), deps=[gth, prev_sq])
            sq_last = sq
            ms_ = sch.op("vector", lambda v, c16=c16: v.tensor_scalar(
                rst[:, 16 + c16:17 + c16], rst[:, c16:c16 + 1], 1.0 / D, None,
                OP.mult), deps=[sq])
            lnv = sch.op("scalar", lambda s, c16=c16: s.activation(
                out=rst[:, 32 + c16:33 + c16], in_=rst[:, 16 + c16:17 + c16],
                func=AF.Ln, bias=epsR[:, 0:1]), deps=[ms_, ms_eps])
            rsd = sch.op("scalar", lambda s, c16=c16: s.activation(
                out=rst[:, 48 + c16:49 + c16], in_=rst[:, 32 + c16:33 + c16],
                func=AF.Exp, scale=-0.5), deps=[lnv])
            prev_sq = rsd
            rstst = sch.op("scalar", lambda s, c16=c16: s.dma_start(
                out=rstdd.ap()[0:1, c16 * 128:(c16 + 1) * 128],
                in_=rst[:, 48 + c16:49 + c16]),
                deps=[rsd], dma=True)
            rst_st.append(rstst)
            for dc in range(DC):
                k = c16 * DC + dc
                d_ = [gth, ld_idh]
                if k < 2:
                    d_ += ev_pl
                if k >= 2:
                    d_.append(tr_list[k - 2])
                trp = sch.op("tensor", lambda t, c16=c16, dc=dc, k=k: t.matmul(
                    ps[k % 2][:, 0:128],
                    gx[c16 % 2][:, dc * 128:(dc + 1) * 128],
                    identh[:, :], start=True, stop=True), deps=d_)
                tev = sch.op("vector", lambda v, c16=c16, dc=dc, k=k: v.tensor_copy(
                    hT[:, dc * S_SEL + c16 * 128:dc * S_SEL + (c16 + 1) * 128],
                    ps[k % 2][:, 0:128]), deps=[trp])
                tr_ev[(c16, dc)] = tev
                tr_list.append(tev)
            gpo.append(sch.op("vector", lambda v, c16=c16: v.engine_nop(),
                              deps=[st1_, sq, tr_ev[(c16, DC - 1)]]))
        ld_rr = sch.op("sync", lambda q: q.dma_start(out=rstrow[:, :], in_=rstdd.ap()),
                       deps=rst_st, dma=True)
        rb_ev = []
        rbm_prev = None
        for j in range(4):
            rbm = sch.op("tensor", lambda t, j=j: t.matmul(
                ps[6][:, :], ones_row[:, :], rstrow[0:1, j * QT:(j + 1) * QT],
                start=True, stop=True), deps=[ld_rr, rbm_prev])
            rbe = sch.op("vector", lambda v, j=j: v.tensor_copy(
                rstdB[:, j * QT:(j + 1) * QT], ps[6][:, :]), deps=[rbm])
            rbm_prev = rbe
            rb_ev.append(rbe)

        hT_done = []
        for dc in range(DC):
            sc_ = sch.op("vector", lambda v, dc=dc: v.scalar_tensor_tensor(
                hT[:, dc * S_SEL:(dc + 1) * S_SEL],
                hT[:, dc * S_SEL:(dc + 1) * S_SEL],
                ln1c[:, dc:dc + 1], rstdB[:, :], OP.mult, OP.mult),
                deps=[ld_ln1] + rb_ev + [tr_ev[(c16, dc)] for c16 in range(16)])
            hT_done.append(sc_)

        bvm = sch.op("tensor", lambda t: t.matmul(
            ps[7][:, 0:128], ones_row[:, :], bv_row[0:1, :], start=True, stop=True),
            deps=[lds[12], lds[3]])
        bve = sch.op("vector", lambda v: v.tensor_copy(bvB[:, :], ps[7][:, 0:128]),
                     deps=[bvm])
        qkv_ev = []
        for head in range(2):
            for stile in range(4):
                bank = head * 4 + stile
                last = None
                for dc in range(DC):
                    last = sch.op("tensor", lambda t, head=head, stile=stile, dc=dc,
                                  bank=bank:
                                  t.matmul(ps[bank][:, :],
                                           wq16[:, (dc * 2 + head) * E:
                                                (dc * 2 + head + 1) * E],
                                           hT[:, dc * S_SEL + stile * QT:
                                              dc * S_SEL + (stile + 1) * QT],
                                           start=(dc == 0), stop=(dc == DC - 1)),
                                  deps=[hT_done[dc]] + ([ld_wq, bve] if dc == 0 else []))
                ev = sch.op("vector", lambda v, head=head, stile=stile, bank=bank:
                            v.tensor_scalar_add(
                                qT[head][:, stile * QT:(stile + 1) * QT],
                                ps[bank][:, :], bq_s[:, head:head + 1]),
                            deps=[last, lds[10]])
                qkv_ev.append(ev)
        for stile in range(4):
            bank = stile
            last = None
            for dc in range(DC):
                last = sch.op("tensor", lambda t, stile=stile, dc=dc, bank=bank:
                              t.matmul(ps[bank][:, :],
                                       wk16[:, dc * E:(dc + 1) * E],
                                       hT[:, dc * S_SEL + stile * QT:
                                          dc * S_SEL + (stile + 1) * QT],
                                       start=(dc == 0), stop=(dc == DC - 1)),
                              deps=[qkv_ev[stile], ld_wk] if dc == 0 else [])
            ev = sch.op("vector", lambda v, stile=stile, bank=bank:
                        v.tensor_scalar_add(kT[:, stile * QT:(stile + 1) * QT],
                                            ps[bank][:, :], bk_s[:, 0:1]),
                        deps=[last, lds[11]])
            qkv_ev.append(ev)
        v_ev = []
        for sc16 in range(16):
            bank = 4 + sc16 % 2
            last = None
            for dc in range(DC):
                last = sch.op("tensor", lambda t, sc16=sc16, dc=dc, bank=bank:
                              t.matmul(ps[bank][:, 0:128],
                                       hT[:, dc * S_SEL + sc16 * 128:
                                          dc * S_SEL + (sc16 + 1) * 128],
                                       wv16[:, dc * E:(dc + 1) * E],
                                       start=(dc == 0), stop=(dc == DC - 1)),
                              deps=([ld_wv, qkv_ev[4 + sc16 % 2],
                                     v_ev[sc16 - 2] if sc16 >= 2 else None]
                                    if dc == 0 else []))
            ev = sch.op("vector", lambda v, sc16=sc16, bank=bank:
                        v.tensor_add(v16[:, sc16 * 128:(sc16 + 1) * 128],
                                     ps[bank][:, 0:128], bvB[:, :]),
                        deps=[last, bve])
            v_ev.append(ev)

        INV2PI = float(1.0 / (2 * np.pi))
        C1 = 6.28125
        C2 = float(2 * np.pi - 6.28125)
        itmp = ee(nc.sbuf_tensor("itmp", [128, QT], mybir.dt.int32))
        ang2 = ee(nc.sbuf_tensor("ang2", [128, QT], F32))

        def build_trig(out_tile, base_op, shift, stile):
            if shift != 0.0:
                a0 = sch.op("vector", lambda v: v.tensor_scalar(
                    ang2[:, :], angB[:, :], shift, None, OP.add), deps=[base_op])
                srct = ang2
            else:
                a0 = base_op
                srct = angB
            n0 = sch.op("vector", lambda v, srct=srct: v.tensor_scalar(
                redu[:, :], srct[:, :], INV2PI, None, OP.mult), deps=[a0])
            n1 = sch.op("vector", lambda v: v.tensor_copy(itmp[:, :], redu[:, :]),
                        deps=[n0])
            n2 = sch.op("vector", lambda v: v.tensor_copy(redu[:, :], itmp[:, :]),
                        deps=[n1])
            r1 = sch.op("vector", lambda v, srct=srct: v.scalar_tensor_tensor(
                out_tile[:, :], redu[:, :], -C1, srct[:, :], OP.mult, OP.add),
                deps=[n2])
            r2 = sch.op("vector", lambda v: v.scalar_tensor_tensor(
                out_tile[:, :], redu[:, :], -C2, out_tile[:, :], OP.mult, OP.add),
                deps=[r1])
            f1 = sch.op("vector", lambda v: v.tensor_scalar(
                redu[:, :], out_tile[:, :], float(np.pi), None, OP.is_gt), deps=[r2])
            f2 = sch.op("vector", lambda v: v.scalar_tensor_tensor(
                out_tile[:, :], redu[:, :], float(-2 * np.pi), out_tile[:, :],
                OP.mult, OP.add), deps=[f1])
            so = sch.op("scalar", lambda s: s.activation(
                out=out_tile[:, :], in_=out_tile[:, :], func=AF.Sin), deps=[f2])
            return so

        rope_done = []
        rp_prev = None
        for stile in range(4):
            ango = sch.op("vector", lambda v, stile=stile: v.tensor_scalar(
                angB[:, :], posB[:, stile * QT:(stile + 1) * QT], invf[:, 0:1],
                None, OP.mult), deps=pb_ev + [lds[7], rp_prev])
            sin_done = build_trig(sinB, ango, 0.0, stile)
            cos_done = build_trig(cosB, ango, float(np.pi / 2), stile)
            for tile in (qT[0], qT[1], kT):
                rmm = sch.op("tensor", lambda t, tile=tile, stile=stile: t.matmul(
                    ps[6][:, :], rott[:, :], tile[:, stile * QT:(stile + 1) * QT],
                    start=True, stop=True),
                    deps=[lds[6], sin_done, cos_done, rp_prev] + qkv_ev)
                t1 = sch.op("vector", lambda v, tile=tile, stile=stile: v.tensor_mul(
                    ang2[:, :], tile[:, stile * QT:(stile + 1) * QT], cosB[:, :]),
                    deps=[sin_done, cos_done, rp_prev])
                t2 = sch.op("vector", lambda v: v.tensor_mul(
                    redu[:, :], ps[6][:, :], sinB[:, :]), deps=[rmm, t1])
                t3 = sch.op("vector", lambda v, tile=tile, stile=stile: v.tensor_add(
                    tile[:, stile * QT:(stile + 1) * QT], ang2[:, :], redu[:, :]),
                    deps=[t2])
                rp_prev = t3
                rope_done.append(t3)

        red_ops = {}
        for jj, tile in enumerate((qT[0], kT, qT[1])):
            for st4 in range(4):
                u = jj * 4 + st4
                sqo = sch.op("scalar", lambda s, tile=tile, st4=st4, u=u:
                             s.activation(out=sq_t[u % 2][:, :],
                                          in_=tile[:, st4 * QT:(st4 + 1) * QT],
                                          func=AF.Square),
                             deps=rope_done + [red_ops.get(u - 2)])
                mmo = sch.op("tensor", lambda t, u=u: t.matmul(
                    ps[u % 4][0:1, 0:QT], ones16c[:, :], sq_t[u % 2][:, :],
                    start=True, stop=True),
                    deps=[sqo, ms_ones16, red_ops.get(u - 4),
                          v_ev[-1] if u == 0 else None])
                rd = sch.op("vector", lambda v, u=u: v.reduce_max(
                    ctile[0:1, u % 4:u % 4 + 1], ps[u % 4][0:1, 0:QT], AX.X),
                    deps=[mmo])
                slot = jj
                if st4 == 0:
                    acc = sch.op("vector", lambda v, slot=slot, u=u: v.tensor_copy(
                        cm[0:1, slot:slot + 1], ctile[0:1, u % 4:u % 4 + 1]),
                        deps=[rd])
                else:
                    acc = sch.op("vector", lambda v, slot=slot, u=u: v.tensor_max(
                        cm[0:1, slot:slot + 1], cm[0:1, slot:slot + 1],
                        ctile[0:1, u % 4:u % 4 + 1]), deps=[rd])
                red_ops[u] = acc
        cb_ready = []
        cb_prev = None
        for qh in range(2):
            pr = sch.op("vector", lambda v, qh=qh: v.tensor_mul(
                ctile[0:1, 4 + qh:5 + qh], cm[0:1, 2 * qh:2 * qh + 1],
                cm[0:1, 1:2]), deps=[red_ops[11], cb_prev])
            ln_ = sch.op("scalar", lambda s, qh=qh: s.activation(
                out=ctile[0:1, 4 + qh:5 + qh], in_=ctile[0:1, 4 + qh:5 + qh],
                func=AF.Ln), deps=[pr])
            ex_ = sch.op("scalar", lambda s, qh=qh: s.activation(
                out=ctile[0:1, 4 + qh:5 + qh], in_=ctile[0:1, 4 + qh:5 + qh],
                func=AF.Exp, bias=lnbias[0:1, 0:1], scale=0.5), deps=[ln_, ms_lnb])
            ng = sch.op("vector", lambda v, qh=qh: v.tensor_scalar(
                ctile[0:1, 4 + qh:5 + qh], ctile[0:1, 4 + qh:5 + qh],
                -1.0, 8.0, OP.mult, OP.add), deps=[ex_])
            bc = sch.op("tensor", lambda t, qh=qh: t.matmul(
                ps[4 + qh][:, 0:1], ones_row[:, :], ctile[0:1, 4 + qh:5 + qh],
                start=True, stop=True), deps=[ng, v_ev[-1]])
            cb = sch.op("vector", lambda v, qh=qh: v.tensor_copy(
                cbias[:, qh:qh + 1], ps[4 + qh][:, 0:1]), deps=[bc])
            cb_ready.append(cb)
            cb_prev = cb


        zstores = []
        exp_g = {}
        pv_g = {}
        zn_g = {}
        rc_g = {}
        gci = 0
        gqi = 0
        lb_prev = None
        lc_prev = None
        for qh in range(2):
            for qsi in range(4):
                nk = 4 * qsi
                for kc in range(nk + 4):
                    mslot = kc - nk if kc >= nk else None
                    first = kc == 0
                    last_c = kc == nk + 3
                    sb = ps[gci % 2]
                    zb = ps[2 + gqi % 2]
                    lbank = ps[4 + gqi % 2]
                    d_ = []
                    if gci == 0:
                        d_ += [v_ev[-1]] + cb_ready
                    if gci >= 2:
                        d_.append(exp_g[gci - 2])
                    else:
                        d_ += [red_ops[11]]
                    if first:
                        if gqi >= 2:
                            d_.append(zn_g[gqi - 2])
                            d_.append(rc_g[gqi - 2])
                        else:
                            d_ += cb_ready
                    sc_ = sch.op("tensor", lambda t, sb=sb, qh=qh, kc=kc, qsi=qsi:
                                 t.matmul(sb[:, :], kT[:, kc * 128:(kc + 1) * 128],
                                          qT[qh][:, qsi * QT:(qsi + 1) * QT],
                                          start=True, stop=True), deps=d_)
                    ed = [sc_, cb_ready[qh]]
                    if gci >= 2:
                        ed.append(pv_g[gci - 2])
                    ex = sch.op("scalar", lambda s, sb=sb, gci=gci, qh=qh:
                                s.activation(out=pT[gci % 2][:, :], in_=sb[:, :],
                                             func=AF.Exp, bias=cbias[:, qh:qh + 1],
                                             scale=INV_SCALE), deps=ed)
                    exp_g[gci] = ex
                    pvd = [ex]
                    if mslot is not None:
                        mko = sch.op("vector", lambda v, gci=gci, mslot=mslot:
                                     v.tensor_mul(pT[gci % 2][:, :], pT[gci % 2][:, :],
                                                  mask_s[:, mslot * QT:(mslot + 1) * QT]),
                                     deps=[ex, ld_mask])
                        pvd = [mko]
                    sch.op("tensor", lambda t, zb=zb, kc=kc, gci=gci, first=first,
                           last_c=last_c:
                           t.matmul(zb[:, :], v16[:, kc * 128:(kc + 1) * 128],
                                    pT[gci % 2][:, :], start=first, stop=last_c),
                           deps=pvd)
                    pv = sch.op("tensor", lambda t, lbank=lbank, gci=gci, first=first,
                                last_c=last_c:
                                t.matmul(lbank[0:1, 0:QT], ones16c[:, :],
                                         pT[gci % 2][:, :], start=first, stop=last_c))
                    pv_g[gci] = pv
                    gci += 1
                    if last_c:
                        rcd = [pv, lb_prev]
                        rc = sch.op("vector", lambda v, lbank=lbank: v.reciprocal(
                            lrec[:, :], lbank[0:1, 0:QT]), deps=rcd)
                        rc_g[gqi] = rc
                        lb = sch.op("tensor", lambda t: t.matmul(
                            ps[6][:, :], ones_row[:, :], lrec[:, :],
                            start=True, stop=True), deps=[rc, lc_prev])
                        lb_prev = lb
                        lc = sch.op("vector", lambda v: v.tensor_copy(
                            lrecB[:, :], ps[6][:, :]), deps=[lb])
                        lc_prev = lc
                        znd = [lc]
                        if len(zstores) >= 2:
                            znd.append(zstores[-2])
                        zn = sch.op("vector", lambda v, gqi=gqi, zb=zb: v.tensor_mul(
                            zn_s[gqi % 2][:, :], zb[:, :], lrecB[:, :]), deps=znd)
                        zn_g[gqi] = zn
                        st = sch.op("scalar", lambda s, qh=qh, qsi=qsi, gqi=gqi:
                                    s.dma_start(
                                        out=zsel_out[qh * E:(qh + 1) * E,
                                                     qsi * QT:(qsi + 1) * QT],
                                        in_=zn_s[gqi % 2][:, :]), deps=[zn], dma=True)
                        zstores.append(st)
                        gqi += 1
        sch.op("sync", lambda q: q.nop(),
               deps=zstores + [st_pl, st_pl2, st_sg] + g_sel_st)
        sch.build(nc, ctx, block)
    return nc


def build_l4a(paranoid=True, serialize_dma=False):
    nc = bass.Bass("TRN2", target_bir_lowering=False, num_devices=NCORES)
    z_in = nc.dram_tensor("z16s", [H, E, 256], F16, kind="ExternalInput").ap()
    wo_in = nc.dram_tensor("wo_f", [D, D], F32, kind="ExternalInput").ap()
    sx_in = nc.dram_tensor("selx16", [256, D], F16, kind="ExternalInput").ap()
    ln2_in = nc.dram_tensor("ln2c", [128, 16], F32, kind="ExternalInput").ap()
    ones_in = nc.dram_tensor("ones128", [1, 128], F32, kind="ExternalInput").ap()
    h2_out = nc.dram_tensor("h2t", [D, 256], F16, kind="ExternalOutput").ap()
    r2_out = nc.dram_tensor("r2t", [D, 256], F32, kind="ExternalOutput").ap()

    sch = Sched(paranoid=paranoid, serialize_dma=serialize_dma)
    with ExitStack() as ctx:
        ee = ctx.enter_context
        z_s = ee(nc.sbuf_tensor("z_s", [128, H * 256], F16))
        wo16 = ee(nc.sbuf_tensor("wo16", [128, H * D], F16))
        sxT = ee(nc.sbuf_tensor("sxT", [128, DC * 256], F16))
        r2T = ee(nc.sbuf_tensor("r2T", [128, DC * 256], F32))
        h2T = ee(nc.sbuf_tensor("h2T", [128, DC * 256], F16))
        ln2c = ee(nc.sbuf_tensor("ln2c_s", [128, 16], F32))
        ones_row = ee(nc.sbuf_tensor("ones_row", [1, 128], F32))
        onesf_col = ee(nc.sbuf_tensor("onesf_col", [128, 1], F32))
        sq32 = ee(nc.sbuf_tensor("sq32", [128, 256], F32))
        msrow = ee(nc.sbuf_tensor("msrow", [1, 256], F32))
        rstdB = ee(nc.sbuf_tensor("rstdB", [128, 256], F32))
        epsR = ee(nc.sbuf_tensor("epsR", [1, 1], F32))
        ps = [ee(nc.psum_tensor(f"ps{i}", [128, QT], F32)) for i in range(6)]
        block = ee(nc.Block())

        ld_z = sch.op("gpsimd", lambda g: g.dma_start(
            out=z_s[:, :].rearrange("p (h s) -> p h s", h=H),
            in_=z_in.rearrange("h p s -> p h s")), dma=True)
        ld_wo = [sch.op("gpsimd", lambda g, hc=hc: g.dma_start(
            out=wo16[:, hc * D:(hc + 1) * D],
            in_=wo_in[hc * 128:(hc + 1) * 128, :]), dma=True) for hc in range(H)]
        ld_ln = sch.op("gpsimd", lambda g: g.dma_start(out=ln2c[:, :], in_=ln2_in),
                       dma=True)
        ld_on = sch.op("gpsimd", lambda g: g.dma_start(out=ones_row[:, :], in_=ones_in),
                       dma=True)
        ms_of = sch.op("vector", lambda v: v.memset(onesf_col[:, :], 1.0))
        ms_ep = sch.op("vector", lambda v: v.memset(epsR[:, :], EPS_RMS))

        sxT_ev = {}
        for dc in range(DC):
            tr = sch.op("sync", lambda q, dc=dc: q.dma_start_transpose(
                sxT[:, dc * 256:(dc + 1) * 256],
                sx_in[:, dc * 128:(dc + 1) * 128]), dma=True)
            sxT_ev[dc] = tr
        sxT_done = sxT_ev[DC - 1]

        r2_ev = {}
        st_r2 = []
        for dc in range(DC):
            bank = dc % 2
            last = None
            for h in range(H):
                d_ = []
                if h == 0:
                    d_ = [ld_z] + ld_wo
                    if dc >= 2:
                        d_.append(r2_ev[dc - 2])
                last = sch.op("tensor", lambda t, h=h, dc=dc, bank=bank: t.matmul(
                    ps[bank][:, 0:256],
                    wo16[:, h * D + dc * 128:h * D + (dc + 1) * 128],
                    z_s[:, h * 256:(h + 1) * 256],
                    start=(h == 0), stop=(h == H - 1)), deps=d_)
            ev = sch.op("vector", lambda v, dc=dc, bank=bank: v.tensor_add(
                r2T[:, dc * 256:(dc + 1) * 256], ps[bank][:, 0:256],
                sxT[:, dc * 256:(dc + 1) * 256]), deps=[last, sxT_ev[dc]])
            r2_ev[dc] = ev
            st = sch.op("sync", lambda q, dc=dc: q.dma_start(
                out=r2_out[dc * 128:(dc + 1) * 128, :],
                in_=r2T[:, dc * 256:(dc + 1) * 256]), deps=[ev], dma=True)
            st_r2.append(st)
        mm_prev = None
        for dc in range(DC):
            sq = sch.op("scalar", lambda s, dc=dc: s.activation(
                out=sq32[:, :], in_=r2T[:, dc * 256:(dc + 1) * 256],
                func=AF.Square), deps=[r2_ev[dc], mm_prev])
            mm = sch.op("tensor", lambda t, dc=dc: t.matmul(
                ps[4][0:1, 0:256], onesf_col[:, :], sq32[:, :],
                start=(dc == 0), stop=(dc == DC - 1)), deps=[sq, ms_of])
            mm_prev = mm
        cst1 = sch.op("vector", lambda v: v.tensor_scalar_mul(
            msrow[0:1, :], ps[4][0:1, 0:256], 1.0 / D), deps=[mm_prev])
        lnv = sch.op("scalar", lambda s: s.activation(
            out=msrow[0:1, :], in_=msrow[0:1, :], func=AF.Ln,
            bias=epsR[0:1, 0:1]), deps=[cst1, ms_ep])
        rsd = sch.op("scalar", lambda s: s.activation(
            out=msrow[0:1, :], in_=msrow[0:1, :], func=AF.Exp, scale=-0.5),
            deps=[lnv])
        bcm = sch.op("tensor", lambda t: t.matmul(
            ps[5][:, 0:256], ones_row[:, :], msrow[0:1, :], start=True, stop=True),
            deps=[rsd, ld_on])
        bce = sch.op("vector", lambda v: v.tensor_copy(rstdB[:, :], ps[5][:, 0:256]),
                     deps=[bcm])
        st_h2 = []
        for dc in range(DC):
            h2 = sch.op("vector", lambda v, dc=dc: v.scalar_tensor_tensor(
                h2T[:, dc * 256:(dc + 1) * 256], r2T[:, dc * 256:(dc + 1) * 256],
                ln2c[:, dc:dc + 1], rstdB[:, :], OP.mult, OP.mult),
                deps=[bce, ld_ln, r2_ev[dc]])
            st = sch.op("sync", lambda q, dc=dc: q.dma_start(
                out=h2_out[dc * 128:(dc + 1) * 128, :],
                in_=h2T[:, dc * 256:(dc + 1) * 256]), deps=[h2], dma=True)
            st_h2.append(st)
        sch.op("sync", lambda q: q.nop(), deps=st_r2 + st_h2)
        sch.build(nc, ctx, block)
    return nc


def build_l4b(paranoid=True, serialize_dma=False):
    nc = bass.Bass("TRN2", target_bir_lowering=False, num_devices=NCORES)
    h2_in = nc.dram_tensor("h2tf", [D, S_SEL], F16, kind="ExternalInput").ap()
    wg_in = nc.dram_tensor("wg_sl", [D, 1024], F32, kind="ExternalInput").ap()
    wu_in = nc.dram_tensor("wu_sl", [D, 1024], F32, kind="ExternalInput").ap()
    at_out = nc.dram_tensor("at", [1024, S_SEL], F16, kind="ExternalOutput").ap()

    NIC = 8
    sch = Sched(paranoid=paranoid, serialize_dma=serialize_dma)
    with ExitStack() as ctx:
        ee = ctx.enter_context
        h2_s = ee(nc.sbuf_tensor("h2_s", [128, DC * S_SEL], F16))
        wg16 = ee(nc.sbuf_tensor("wg16", [128, DC * 1024], F16))
        wu16 = ee(nc.sbuf_tensor("wu16", [128, DC * 1024], F16))
        sil = ee(nc.sbuf_tensor("sil", [128, QT], F16))
        a_s = [ee(nc.sbuf_tensor(f"a_s{i}", [128, QT], F16)) for i in range(2)]
        ps = [ee(nc.psum_tensor(f"ps{i}", [128, QT], F32)) for i in range(4)]
        block = ee(nc.Block())

        ld_h2 = [sch.op("gpsimd", lambda g, dq=dq: g.dma_start(
            out=h2_s[:, :].rearrange("p (dc s) -> p dc s", dc=DC)[
                :, dq * 4:(dq + 1) * 4],
            in_=h2_in.rearrange("(dc p) s -> p dc s", p=128)[
                :, dq * 4:(dq + 1) * 4]), dma=True) for dq in range(4)]
        ld_wg = [sch.op("gpsimd", lambda g, dq=dq: g.dma_start(
            out=wg16[:, :].rearrange("p (dc i) -> p dc i", dc=DC)[
                :, dq * 4:(dq + 1) * 4],
            in_=wg_in.rearrange("(dc p) i -> p dc i", p=128)[
                :, dq * 4:(dq + 1) * 4]), dma=True) for dq in range(4)]
        ld_wu = [sch.op("gpsimd", lambda g, dq=dq: g.dma_start(
            out=wu16[:, :].rearrange("p (dc i) -> p dc i", dc=DC)[
                :, dq * 4:(dq + 1) * 4],
            in_=wu_in.rearrange("(dc p) i -> p dc i", p=128)[
                :, dq * 4:(dq + 1) * 4]), dma=True) for dq in range(4)]

        a_st = []
        sil_prev = None
        aprev = {}
        for ic in range(NIC):
            for st4 in range(4):
                u = ic * 4 + st4
                gb = (u % 2) * 2
                lastg = None
                lastu = None
                for dc in range(DC):
                    d_ = ([ld_h2[dc // 4], ld_wg[dc // 4], ld_wu[dc // 4]]
                          if u == 0 else [])
                    if dc == 0:
                        if u >= 2:
                            d_.append(aprev[u - 2])
                    lastg = sch.op("tensor", lambda t, ic=ic, st4=st4, dc=dc, gb=gb:
                                   t.matmul(ps[gb][:, :],
                                            wg16[:, dc * 1024 + ic * 128:
                                                 dc * 1024 + (ic + 1) * 128],
                                            h2_s[:, dc * S_SEL + st4 * QT:
                                                 dc * S_SEL + (st4 + 1) * QT],
                                            start=(dc == 0), stop=(dc == DC - 1)),
                                   deps=d_)
                    lastu = sch.op("tensor", lambda t, ic=ic, st4=st4, dc=dc, gb=gb:
                                   t.matmul(ps[gb + 1][:, :],
                                            wu16[:, dc * 1024 + ic * 128:
                                                 dc * 1024 + (ic + 1) * 128],
                                            h2_s[:, dc * S_SEL + st4 * QT:
                                                 dc * S_SEL + (st4 + 1) * QT],
                                            start=(dc == 0), stop=(dc == DC - 1)))
                so = sch.op("scalar", lambda s, gb=gb: s.activation(
                    out=sil[:, :], in_=ps[gb][:, :], func=AF.Silu),
                    deps=[lastg, sil_prev])
                ao = sch.op("vector", lambda v, u=u, gb=gb: v.tensor_mul(
                    a_s[u % 2][:, :], sil[:, :], ps[gb + 1][:, :]),
                    deps=[so, lastu, a_st[-2] if len(a_st) >= 2 else None])
                sil_prev = ao
                aprev[u] = ao
                st = sch.op("sync", lambda q, ic=ic, st4=st4, u=u: q.dma_start(
                    out=at_out[ic * 128:(ic + 1) * 128, st4 * QT:(st4 + 1) * QT],
                    in_=a_s[u % 2][:, :]), deps=[ao], dma=True)
                a_st.append(st)
        sch.op("sync", lambda q: q.nop(), deps=a_st)
        sch.build(nc, ctx, block)
    return nc


def build_l4c(paranoid=True, serialize_dma=False):
    nc = bass.Bass("TRN2", target_bir_lowering=False, num_devices=NCORES)
    at_in = nc.dram_tensor("atf", [I_FF, S_SEL], F16, kind="ExternalInput").ap()
    wd_in = nc.dram_tensor("wd_sl", [I_FF, 256], F32, kind="ExternalInput").ap()
    r2_in = nc.dram_tensor("r2sl", [S_SEL, 256], F32, kind="ExternalInput").ap()
    sx_in = nc.dram_tensor("sx_sl", [S_SEL, 256], F16, kind="ExternalInput").ap()
    sc_in = nc.dram_tensor("sct16", [128, 16], F32, kind="ExternalInput").ap()
    g_out = nc.dram_tensor("gated", [S_SEL, 256], F32, kind="ExternalOutput").ap()

    NI = I_FF // 128
    sch = Sched(paranoid=paranoid, serialize_dma=serialize_dma)
    with ExitStack() as ctx:
        ee = ctx.enter_context
        wd16 = ee(nc.sbuf_tensor("wd16", [128, NI * 256], F16))
        a_t = [ee(nc.sbuf_tensor(f"a_t{i}", [128, NI * 512], F16)) for i in range(2)]
        r2_t = [ee(nc.sbuf_tensor(f"r2_t{i}", [128, 256], F32)) for i in range(2)]
        sx_t = [ee(nc.sbuf_tensor(f"sx_t{i}", [128, 256], F16)) for i in range(2)]
        go_t = [ee(nc.sbuf_tensor(f"go_t{i}", [128, 256], F32)) for i in range(2)]
        tsc = ee(nc.sbuf_tensor("tsc", [128, 256], F32))
        g_s = ee(nc.sbuf_tensor("g_s", [128, 16], F32))
        ps = [ee(nc.psum_tensor(f"ps{i}", [128, 256], F32)) for i in range(2)]
        block = ee(nc.Block())

        ld_wd = sch.op("gpsimd", lambda g: g.dma_start(
            out=wd16[:, :].rearrange("p (ic d) -> p ic d", ic=NI),
            in_=wd_in.rearrange("(ic p) d -> p ic d", p=128)), dma=True)
        ld_sc = sch.op("gpsimd", lambda g: g.dma_start(out=g_s[:, :], in_=sc_in),
                       dma=True)
        sig = sch.op("scalar", lambda s: s.activation(
            out=g_s[:, :], in_=g_s[:, :], func=AF.Sigmoid), deps=[ld_sc])

        g_st = []
        ev_prev = {}
        lda = {}
        for sc4 in range(4):
            lda[sc4] = sch.op("sync", lambda q, sc4=sc4: q.dma_start(
                out=a_t[sc4 % 2][:, :].rearrange("p (ic s) -> p ic s", ic=NI),
                in_=at_in.rearrange("(ic p) s -> p ic s", p=128)[
                    :, :, sc4 * 512:(sc4 + 1) * 512]),
                deps=[ev_prev.get(4 * (sc4 - 2) + 3)], dma=True)
            for ss in range(4):
                sc16 = sc4 * 4 + ss
                ldr = sch.op("sync", lambda q, sc16=sc16: q.dma_start(
                    out=r2_t[sc16 % 2][:, :],
                    in_=r2_in[sc16 * 128:(sc16 + 1) * 128, :]),
                    deps=[ev_prev.get(sc16 - 2)], dma=True)
                lds_ = sch.op("sync", lambda q, sc16=sc16: q.dma_start(
                    out=sx_t[sc16 % 2][:, :],
                    in_=sx_in[sc16 * 128:(sc16 + 1) * 128, :]),
                    deps=[ev_prev.get(sc16 - 2)], dma=True)
                last = None
                for ic in range(NI):
                    d_ = []
                    if ic == 0:
                        d_ = [lda[sc4], ld_wd]
                        if sc16 >= 2:
                            d_.append(ev_prev[sc16 - 2])
                    last = sch.op("tensor", lambda t, sc16=sc16, ss=ss, ic=ic:
                                  t.matmul(
                                      ps[sc16 % 2][:, :],
                                      a_t[(sc16 // 4) % 2][:, ic * 512 + ss * 128:
                                           ic * 512 + (ss + 1) * 128],
                                      wd16[:, ic * 256:(ic + 1) * 256],
                                      start=(ic == 0), stop=(ic == NI - 1)),
                                  deps=d_)
                e1 = sch.op("vector", lambda v, sc16=sc16: v.tensor_add(
                    go_t[sc16 % 2][:, :], ps[sc16 % 2][:, :], r2_t[sc16 % 2][:, :]),
                    deps=[last, ldr, g_st[-2] if len(g_st) >= 2 else None])
                e2 = sch.op("vector", lambda v, sc16=sc16: v.tensor_sub(
                    tsc[:, :], go_t[sc16 % 2][:, :], sx_t[sc16 % 2][:, :]),
                    deps=[e1, lds_])
                e3 = sch.op("vector", lambda v, sc16=sc16: v.tensor_scalar(
                    go_t[sc16 % 2][:, :], tsc[:, :], g_s[:, sc16:sc16 + 1], None,
                    OP.mult), deps=[e2, sig])
                ev_prev[sc16] = e3
                st = sch.op("sync", lambda q, sc16=sc16: q.dma_start(
                    out=g_out[sc16 * 128:(sc16 + 1) * 128, :],
                    in_=go_t[sc16 % 2][:, :]), deps=[e3], dma=True)
                g_st.append(st)
        sch.op("sync", lambda q: q.nop(), deps=g_st)
        sch.build(nc, ctx, block)
    return nc


def build_l5(paranoid=True, serialize_dma=False):
    nc = bass.Bass("TRN2", target_bir_lowering=False, num_devices=NCORES)
    x1_in = nc.dram_tensor("x1_own", [2048, D], F32, kind="ExternalInput").ap()
    gf_in = nc.dram_tensor("gatedf2", [S_SEL + 128, D], F32,
                           kind="ExternalInput").ap()
    sr_in = nc.dram_tensor("sri", [128, 16], F32, kind="ExternalInput").ap()
    io_in = nc.dram_tensor("ioc", [128, 1], F32, kind="ExternalInput").ap()
    out_o = nc.dram_tensor("out", [2048, D], F32, kind="ExternalOutput").ap()

    sch = Sched(paranoid=paranoid, serialize_dma=serialize_dma)
    with ExitStack() as ctx:
        ee = ctx.enter_context
        sr_s = ee(nc.sbuf_tensor("sr_s", [128, 16], F32))
        io_s = ee(nc.sbuf_tensor("io_s", [128, 1], F32))
        rel = ee(nc.sbuf_tensor("rel", [128, 16], F32))
        v1 = ee(nc.sbuf_tensor("v1", [128, 16], F32))
        v2 = ee(nc.sbuf_tensor("v2", [128, 16], F32))
        idx_u = ee(nc.sbuf_tensor("idx_u", [128, 16], U32))
        NB5 = 4
        xo = [ee(nc.sbuf_tensor(f"xo{i}", [128, D], F32)) for i in range(NB5)]
        mo = [ee(nc.sbuf_tensor(f"mo{i}", [128, D], F32)) for i in range(NB5)]
        r_t = ee(nc.sbuf_tensor("r_t", [128, D], F32))
        sqd = ee(nc.sbuf_tensor("sqd", [128, D], F32))
        st1 = ee(nc.sbuf_tensor("st1", [128, 8], F32))
        o_t = [ee(nc.sbuf_tensor(f"o_t{i}", [128, D], F32)) for i in range(NB5)]
        eps_c = ee(nc.sbuf_tensor("eps_c", [128, 1], F32))
        block = ee(nc.Block())

        ld1 = sch.op("gpsimd", lambda g: g.dma_start(out=sr_s[:, :], in_=sr_in),
                     dma=True)
        ld2 = sch.op("gpsimd", lambda g: g.dma_start(out=io_s[:, :], in_=io_in),
                     dma=True)
        ms_e = sch.op("vector", lambda v: v.memset(eps_c[:, :], EPS_LN))
        i1 = sch.op("vector", lambda v: v.tensor_scalar(
            v1[:, :], sr_s[:, :], 2048.0, None, OP.is_ge), deps=[ld1])
        i2 = sch.op("vector", lambda v: v.tensor_scalar(
            v2[:, :], v1[:, :], io_s[:, 0:1], None, OP.mult), deps=[i1, ld2])
        i3 = sch.op("vector", lambda v: v.tensor_scalar(
            v1[:, :], v1[:, :], -1.0, 1.0, OP.mult, OP.add), deps=[i2])
        i4 = sch.op("vector", lambda v: v.tensor_mul(
            rel[:, :], sr_s[:, :], v1[:, :]), deps=[i3])
        i5 = sch.op("vector", lambda v: v.tensor_add(
            rel[:, :], rel[:, :], v2[:, :]), deps=[i4])
        i6 = sch.op("vector", lambda v: v.tensor_copy(idx_u[:, :], rel[:, :]),
                    deps=[i5])
        o_st = []
        ra_hist = []
        x1o_prev = None
        lnv_prev = None
        for tc in range(16):
            ldx = sch.op("sync", lambda q, tc=tc: q.dma_start(
                out=xo[tc % NB5][:, :], in_=x1_in[tc * 128:(tc + 1) * 128, :]),
                deps=[ra_hist[tc - NB5] if tc >= NB5 else None], dma=True)
            ldm = sch.op("gpsimd", lambda g, tc=tc: g.indirect_dma_start(
                out=mo[tc % NB5][:, :], out_offset=None, in_=gf_in,
                in_offset=IndirectOffsetOnAxis(idx_u[:, tc:tc + 1], 0)),
                deps=[i6, ra_hist[tc - NB5] if tc >= NB5 else None], dma=True)
            ra = sch.op("vector", lambda v, tc=tc: v.scalar_tensor_tensor(
                r_t[:, :], xo[tc % NB5][:, :], 2.0, mo[tc % NB5][:, :],
                OP.mult, OP.add),
                deps=[ldx, ldm, x1o_prev])
            ra_hist.append(ra)
            sm = sch.op("vector", lambda v: v.reduce_sum(
                st1[:, 0:1], r_t[:, :], axis=AX.X), deps=[ra])
            s2 = sch.op("scalar", lambda s: s.activation(
                out=sqd[:, :], in_=r_t[:, :], func=AF.Square,
                accum_out=st1[:, 1:2]), deps=[ra, x1o_prev])
            mean = sch.op("vector", lambda v: v.tensor_scalar_mul(
                st1[:, 2:3], st1[:, 0:1], 1.0 / D), deps=[sm])
            msq = sch.op("vector", lambda v: v.tensor_scalar_mul(
                st1[:, 3:4], st1[:, 1:2], 1.0 / D), deps=[s2])
            m2_ = sch.op("vector", lambda v: v.tensor_mul(
                st1[:, 4:5], st1[:, 2:3], st1[:, 2:3]), deps=[mean])
            var = sch.op("vector", lambda v: v.tensor_sub(
                st1[:, 5:6], st1[:, 3:4], st1[:, 4:5]), deps=[msq, m2_, lnv_prev])
            lnv = sch.op("scalar", lambda s: s.activation(
                out=st1[:, 6:7], in_=st1[:, 5:6], func=AF.Ln,
                bias=eps_c[:, 0:1]), deps=[var, ms_e])
            lnv_prev = lnv
            rstd = sch.op("scalar", lambda s: s.activation(
                out=st1[:, 7:8], in_=st1[:, 6:7], func=AF.Exp, scale=-0.5),
                deps=[lnv])
            oo = sch.op("vector", lambda v, tc=tc: v.tensor_scalar(
                o_t[tc % NB5][:, :], r_t[:, :], st1[:, 2:3], st1[:, 7:8],
                OP.subtract, OP.mult),
                deps=[rstd, o_st[-NB5] if len(o_st) >= NB5 else None])
            x1o_prev = oo
            st = sch.op("sync", lambda q, tc=tc: q.dma_start(
                out=out_o[tc * 128:(tc + 1) * 128, :], in_=o_t[tc % NB5][:, :]),
                deps=[oo], dma=True)
            o_st.append(st)
        sch.op("sync", lambda q: q.nop(), deps=o_st)
        sch.build(nc, ctx, block)
    return nc


_CACHE = {}


def _causal_masks_f16():
    m = np.zeros((4, 128, QT), dtype=np.float16)
    q = np.arange(QT)[None, :]
    k = np.arange(128)[:, None]
    for ci in range(4):
        m[ci] = ((128 * ci + k) <= q).astype(np.float16)
    return m


def _run_l1(inputs):
    xt = np.ascontiguousarray(np.transpose(inputs["L0_attn_x"], (0, 2, 1)))
    WQ, WK, WV = inputs["L0_attn_W_Q"], inputs["L0_attn_W_K"], inputs["L0_attn_W_V"]
    bQ, bK, bV = inputs["L0_attn_b_Q"], inputs["L0_attn_b_K"], inputs["L0_attn_b_V"]
    ones128 = np.ones((1, 128), np.float32)
    masks = _causal_masks_f16()
    in_maps = []
    for c in range(NCORES):
        b = c // 4
        hs = slice(4 * (c % 4), 4 * (c % 4) + HPC)
        in_maps.append({
            "x_t": np.ascontiguousarray(xt[b]),
            "wq": np.ascontiguousarray(WQ[hs]),
            "wk": np.ascontiguousarray(WK[hs]),
            "wv_mov": np.ascontiguousarray(
                np.transpose(WV[hs], (1, 0, 2)).reshape(D, HPC * E)),
            "bq_t": np.ascontiguousarray(bQ[hs].T),
            "bk_t": np.ascontiguousarray(bK[hs].T),
            "bv_mov": np.ascontiguousarray(bV[hs].reshape(1, HPC * E)),
            "ones128": ones128,
            "masks": masks,
        })
    if "l1" not in _CACHE:
        _CACHE["l1"] = build_l1()
    res = run_bass_kernel_spmd(_CACHE["l1"], in_maps, core_ids=list(range(NCORES)))
    return [res.results[c]["z"] for c in range(NCORES)]


def _run_l2(inputs, z_shards):
    x = inputs["L0_attn_x"]
    WO = inputs["L0_attn_W_O"]
    bO = inputs["L0_attn_b_O"]
    wrt = inputs["L0_ffn_w_router"]
    ones128 = np.ones((1, 128), np.float32)
    in_maps = []
    for c in range(NCORES):
        sl = slice(1024 * c, 1024 * c + 1024)
        z16 = np.empty((H, E, 2048), dtype=np.float16)
        for h in range(H):
            src, pair = h // 4, h % 4
            z16[h, :, 0:1024] = z_shards[src][pair * E:(pair + 1) * E, sl]
            z16[h, :, 1024:2048] = z_shards[4 + src][pair * E:(pair + 1) * E, sl]
        x_own = np.concatenate([x[0, sl], x[1, sl]], axis=0)
        in_maps.append({
            "z16": z16,
            "wo": np.ascontiguousarray(WO),
            "bo": bO.reshape(1, D),
            "x_own": np.ascontiguousarray(x_own),
            "wrt": wrt.reshape(1, D),
            "ones128": ones128,
        })
    if "l2" not in _CACHE:
        _CACHE["l2"] = build_l2()
    res = run_bass_kernel_spmd(_CACHE["l2"], in_maps, core_ids=list(range(NCORES)))
    return ([res.results[c]["x1"] for c in range(NCORES)],
            [res.results[c]["x1h"] for c in range(NCORES)],
            [res.results[c]["scores"] for c in range(NCORES)])


def _l3_consts():
    c = {}
    c["postile"] = np.tile(
        (np.arange(64)[None, :] * 128 + np.arange(128)[:, None]).astype(np.float32),
        (1, 2))
    c["onesmat"] = np.ones((128, 128), np.float32)
    c["trilT"] = np.triu(np.ones((128, 128), np.float16))
    c["siota"] = np.tile(np.arange(1024, dtype=np.float16)[None, :], (128, 1))
    rot = np.zeros((128, 128), np.float32)
    for r in range(64):
        rot[r, r + 64] = -1.0
    for r in range(64, 128):
        rot[r, r - 64] = 1.0
    c["rotT"] = rot.T.astype(np.float16)
    inv_freq = (1.0 / (10000.0 ** (np.arange(0, 128, 2, dtype=np.float32) / 128.0)
                       )).astype(np.float32)
    c["invf"] = np.concatenate([inv_freq, inv_freq]).reshape(128, 1)
    c["ident"] = np.eye(128, dtype=np.float32)
    c["pickmat"] = np.diag([1.0, 1.0, -1.0, -1.0]).astype(np.float32)
    return c


def _run_l3(inputs, x1h_shards, sc_shards):
    x1f = np.concatenate(
        [np.concatenate([x1h_shards[c][0:1024] for c in range(NCORES)], axis=0),
         np.concatenate([x1h_shards[c][1024:2048] for c in range(NCORES)], axis=0)],
        axis=0)
    sct = np.empty((128, 128), np.float32)
    for c in range(NCORES):
        sct[:, 8 * c:8 * c + 8] = sc_shards[c][:, 0:8]
        sct[:, 64 + 8 * c:64 + 8 * c + 8] = sc_shards[c][:, 8:16]
    cst = _l3_consts()
    wq = inputs["L0_ffn_wq"]
    wk = inputs["L0_ffn_wk"]
    wv = inputs["L0_ffn_wv"]
    bq = inputs["L0_ffn_bq"]
    bk = inputs["L0_ffn_bk"]
    bv = inputs["L0_ffn_bv"]
    ln1 = inputs["L0_ffn_ln1_w"]
    in_maps = []
    for c in range(NCORES):
        in_maps.append({
            "x1f": x1f,
            "sctile": sct,
            "postile": cst["postile"],
            "onesmat": cst["onesmat"],
            "ones128": np.ones((1, 128), np.float32),
            "trilT": cst["trilT"],
            "siota": cst["siota"],
            "rotT": cst["rotT"],
            "invf": cst["invf"],
            "ident": cst["ident"],
            "identh": cst["ident"].astype(np.float16),
            "pickmat": cst["pickmat"],
            "masks": _causal_masks_f16(),
            "wq_sel": np.ascontiguousarray(wq[:, 256 * c:256 * c + 256]),
            "wk_sel": np.ascontiguousarray(wk[:, 128 * c:128 * c + 128]),
            "wv_sel": np.ascontiguousarray(wv[:, 128 * c:128 * c + 128]),
            "bq_sel": np.ascontiguousarray(
                bq[256 * c:256 * c + 256].reshape(2, 128).T),
            "bk_sel": bk[128 * c:128 * c + 128].reshape(128, 1),
            "bv_sel": bv[128 * c:128 * c + 128].reshape(1, 128),
            "ln1c": np.ascontiguousarray(ln1.reshape(16, 128).T),
        })
    if "l3" not in _CACHE:
        _CACHE["l3"] = build_l3()
    res = run_bass_kernel_spmd(_CACHE["l3"], in_maps, core_ids=list(range(NCORES)))
    poslist = res.results[0]["poslist"]
    selx = res.results[0]["selx"]
    srg = res.results[0]["srg"]
    z_sel = [res.results[c]["z_sel"] for c in range(NCORES)]
    return poslist, selx, srg, z_sel


def _run_l4a(inputs, zsel_shards, selx):
    wo = inputs["L0_ffn_wo"]
    ln2 = inputs["L0_ffn_ln2_w"]
    ones128 = np.ones((1, 128), np.float32)
    ln2c = np.ascontiguousarray(ln2.reshape(16, 128).T)
    in_maps = []
    for c in range(NCORES):
        z16s = np.empty((H, E, 256), np.float16)
        for h in range(H):
            z16s[h] = zsel_shards[h // 2][(h % 2) * E:(h % 2 + 1) * E,
                                          256 * c:256 * c + 256]
        in_maps.append({
            "z16s": z16s,
            "wo_f": wo,
            "selx16": np.ascontiguousarray(selx[256 * c:256 * c + 256]),
            "ln2c": ln2c,
            "ones128": ones128,
        })
    if "l4a" not in _CACHE:
        _CACHE["l4a"] = build_l4a()
    res = run_bass_kernel_spmd(_CACHE["l4a"], in_maps, core_ids=list(range(NCORES)))
    return ([res.results[c]["h2t"] for c in range(NCORES)],
            [res.results[c]["r2t"] for c in range(NCORES)])


def _run_l4b(inputs, h2t_shards):
    h2tf = np.concatenate(h2t_shards, axis=1)
    wg = inputs["L0_ffn_w_gate"]
    wu = inputs["L0_ffn_w_up"]
    in_maps = []
    for c in range(NCORES):
        in_maps.append({
            "h2tf": h2tf,
            "wg_sl": np.ascontiguousarray(wg[:, 1024 * c:1024 * c + 1024]),
            "wu_sl": np.ascontiguousarray(wu[:, 1024 * c:1024 * c + 1024]),
        })
    if "l4b" not in _CACHE:
        _CACHE["l4b"] = build_l4b()
    res = run_bass_kernel_spmd(_CACHE["l4b"], in_maps, core_ids=list(range(NCORES)))
    return [res.results[c]["at"] for c in range(NCORES)]


def _run_l4c(inputs, at_shards, r2t_shards, selx, poslist):
    atf = np.concatenate(at_shards, axis=0)
    r2sm = np.ascontiguousarray(
        np.concatenate(r2t_shards, axis=1).T)
    wd = inputs["L0_ffn_w_down"]
    sct16 = np.ascontiguousarray(poslist[1].reshape(16, 128).T)
    in_maps = []
    for c in range(NCORES):
        in_maps.append({
            "atf": atf,
            "wd_sl": np.ascontiguousarray(wd[:, 256 * c:256 * c + 256]),
            "r2sl": np.ascontiguousarray(r2sm[:, 256 * c:256 * c + 256]),
            "sx_sl": np.ascontiguousarray(selx[:, 256 * c:256 * c + 256]),
            "sct16": sct16,
        })
    if "l4c" not in _CACHE:
        _CACHE["l4c"] = build_l4c()
    res = run_bass_kernel_spmd(_CACHE["l4c"], in_maps, core_ids=list(range(NCORES)))
    return [res.results[c]["gated"] for c in range(NCORES)]


def _run_l5(inputs, x1_shards, gated_shards, srg):
    gatedf2 = np.concatenate(
        [np.concatenate(gated_shards, axis=1),
         np.zeros((128, D), np.float32)], axis=0)
    ioc = (2048.0 + np.arange(128, dtype=np.float32)[:, None])
    in_maps = []
    for c in range(NCORES):
        sri = np.empty((128, 16), np.float32)
        sri[:, 0:8] = srg[:, 8 * c:8 * c + 8]
        sri[:, 8:16] = srg[:, 64 + 8 * c:64 + 8 * c + 8]
        in_maps.append({
            "x1_own": x1_shards[c],
            "gatedf2": gatedf2,
            "sri": np.ascontiguousarray(sri),
            "ioc": ioc,
        })
    if "l5" not in _CACHE:
        _CACHE["l5"] = build_l5()
    res = run_bass_kernel_spmd(_CACHE["l5"], in_maps, core_ids=list(range(NCORES)))
    out = np.empty((B, T, D), np.float32)
    for c in range(NCORES):
        o = res.results[c]["out"]
        out[0, 1024 * c:1024 * c + 1024] = o[0:1024]
        out[1, 1024 * c:1024 * c + 1024] = o[1024:2048]
    return out


def kernel(**inputs):
    z_shards = _run_l1(inputs)
    x1_shards, x1h_shards, sc_shards = _run_l2(inputs, z_shards)
    poslist, selx, srg, zsel_shards = _run_l3(inputs, x1h_shards, sc_shards)
    h2t_shards, r2t_shards = _run_l4a(inputs, zsel_shards, selx)
    at_shards = _run_l4b(inputs, h2t_shards)
    gated_shards = _run_l4c(inputs, at_shards, r2t_shards, selx, poslist)
    return _run_l5(inputs, x1_shards, gated_shards, srg)
